# version 23
# baseline (speedup 1.0000x reference)
import contextlib
import numpy as np
import ml_dtypes
import concourse.bass as bass
import concourse.mybir as mybir
from concourse.bass_utils import run_bass_kernel_spmd

F32 = mybir.dt.float32
BF16 = mybir.dt.bfloat16
AF = mybir.ActivationFunctionType
ALU = mybir.AluOpType
AX = mybir.AxisListType.X
DEBUG = False

B, L, D, H = 4, 8192, 1024, 4
DV = 256
P5 = 5
TILE = 512
NT = L // TILE
WIN = 1024
NW = L // WIN
PAIRS = [[0, 1], [2, 3], [4, 5], [6, 7]]
QUADS = [[0, 2, 4, 6], [1, 3, 5, 7]]
FIRS = (("fs", 3), ("fm", 15), ("fl", 64))
ENGINES = ("T", "V", "S", "G", "Y")



def stats_row(j, p, h_loc, kind):
    return j * 40 + (p * 2 + h_loc) * 4 + kind


def build_selR(b, j):
    sel = np.zeros((40, 128, 80), np.float32)
    for t in range(40):
        gg = b * 20 + t // 2
        p = gg // 16
        r = gg % 16
        h_glob, kind = r // 4, r % 4
        sel[t, :, stats_row(h_glob // 2, p, h_glob % 2, kind)] = 1.0
    return sel


def build_fir_ST(w_slice, K):
    NG_ = (K + 2) // 4 + 1
    ST = np.zeros((NG_, 128, 128), np.float32)
    for G in range(NG_):
        for r in range(4):
            for d in range(4):
                jt = 4 * G + r - d
                if 0 <= jt <= K - 1:
                    ST[G, 32 * r:32 * r + 32, 32 * d:32 * d + 32][
                        np.arange(32), np.arange(32)] = w_slice[:, jt]
    return ST


def build_selstat2():
    s = np.zeros((30, 128, 96), np.float32)
    for r in range(30):
        s[r, :, 32 * (r // 10) + r % 10] = 1.0
    return s


def build_selrow8e(j):
    s = np.zeros((4, 8, 128), np.float32)
    for e in range(4):
        s[e, 4 * j + e, :] = 1.0
    return s


def build_selmix(j):
    sm = np.zeros((10, 20, 128), np.float32)
    for hh in range(2):
        for pi in range(5):
            sm[hh * 5 + pi, (2 * j + hh) * 5 + pi, :] = 1.0
    return sm


def build_selstat():
    combos = [(p, kd) for p in range(P5) for kd in range(3)]
    sel = np.zeros((2, 15, 128, 40), np.float32)
    for h in range(2):
        for ci, (p, kd) in enumerate(combos):
            base = (p * 2 + h) * 4
            if kd == 0:
                sel[h, ci, :, base + 0] = 1.0
            elif kd == 1:
                sel[h, ci, :, base + 1] = 1.0
                sel[h, ci, :, base + 3] = 1.0
            else:
                sel[h, ci, :, base + 2] = 1.0
    return sel


def host_prep(inp):
    x = np.asarray(inp["L0_attn_hidden_states"], np.float32)
    gW1 = np.asarray(inp["L0_attn_gW1"], np.float32)
    masks = {}
    mlow = np.zeros((128, 128), np.float32)
    mup = np.zeros((128, 128), np.float32)
    minc = np.zeros((128, 128), np.float32)
    for r in range(4):
        s = slice(32 * r, 32 * r + 32)
        mlow[s, s] = -np.tril(np.ones((32, 32), np.float32), -1)
        mup[s, s] = -np.triu(np.ones((32, 32), np.float32), 1)
        minc[s, s] = np.triu(np.ones((32, 32), np.float32), 0)
    ident = np.eye(128, dtype=np.float32)
    onesc = np.ones((128, 1), np.float32)
    onesr = np.ones((1, 128), np.float32)
    ones5 = np.ones((1, 5), np.float32)
    zrow = np.zeros((1, 512), np.float32)

    maps = []
    for c in range(8):
        b, j = c // 2, c % 2
        hsl = slice(512 * j, 512 * (j + 1))
        csl = slice(1024 * j, 1024 * (j + 1))
        d = {
            "x": np.ascontiguousarray(x[b].T),
            "xres": np.ascontiguousarray(x[b].T[hsl]),
            "Wq": np.ascontiguousarray(inp["L0_attn_Wq"][:, hsl]),
            "Wk": np.ascontiguousarray(inp["L0_attn_Wk"][:, hsl]),
            "Wv": np.ascontiguousarray(inp["L0_attn_Wv"][:, hsl]),
            "Wb2": np.ascontiguousarray(inp["L0_attn_Wb"][:, 2 * j:2 * j + 2]),
            "gW1a": np.ascontiguousarray(gW1[0:1024, csl]),
            "gW1r": np.ascontiguousarray(gW1[1024 + 5120 * b:1024 + 5120 * (b + 1), csl]),
            "gb1c": np.ascontiguousarray(np.asarray(inp["L0_attn_gb1"], np.float32)[csl].reshape(8, 128).T),
            "gW2": np.ascontiguousarray(np.asarray(inp["L0_attn_gW2"], np.float32)[csl]),
            "gb2r": np.asarray(inp["L0_attn_gb2"], np.float32).reshape(1, 20),
            "ltau": np.asarray(inp["L0_attn_log_tau"], np.float32).reshape(1, 4),
            "epsr": np.asarray(inp["L0_attn_eps_logit"], np.float32).reshape(1, 20),
            "nwc": np.ascontiguousarray(np.asarray(inp["L0_attn_norm_w"], np.float32).reshape(2, 128).T),
            "Wo": np.ascontiguousarray(inp["L0_attn_Wo"][:, hsl]),
            "rwT": np.ascontiguousarray(np.asarray(inp["L0_ffn_router_w"], np.float32).T),
            "up": np.ascontiguousarray(inp["L0_ffn_up_proj"][4 * j:4 * j + 4]),
            "down": np.ascontiguousarray(inp["L0_ffn_down_proj"][4 * j:4 * j + 4]),
            "selR": build_selR(b, j),
            "selmix": build_selmix(j),
            "selstat": build_selstat(),
            "selrow8e": build_selrow8e(j),
            "ident": ident, "onesc": onesc, "onesr": onesr,
            "biasrow": np.array([[1e-6, 1e-5]], np.float32),
            "ones5": ones5, "zrow": zrow,
            "mlow": mlow, "mup": mup, "minc": minc,
        }
        cst = np.zeros((3, 4, 4, 128, 128), np.float32)
        for ai, nm in enumerate(("conv_q", "conv_k", "conv_v")):
            cw = np.asarray(inp["L0_attn_" + nm], np.float32)[hsl]
            for jt in range(4):
                for ct in range(4):
                    cst[ai, jt, ct][np.arange(128), np.arange(128)] = cw[128 * ct:128 * ct + 128, jt]
        d["convST"] = cst
        stf = np.zeros((16, 24, 128, 128), np.float32)
        for g in range(16):
            off = 0
            for nm, K in (("fir_s", 3), ("fir_m", 15), ("fir_l", 64)):
                w = np.asarray(inp["L0_attn_" + nm], np.float32)[hsl][32 * g:32 * g + 32]
                st = build_fir_ST(w, K)
                stf[g, off:off + st.shape[0]] = st
                off += st.shape[0]
        d["STf"] = stf
        maps.append(d)
    return maps



class Prog:
    def __init__(self, nc, es):
        self.nc = nc
        self.sems = {e: es.enter_context(nc.semaphore("sem" + e)) for e in ENGINES}
        self.sems["D"] = es.enter_context(nc.semaphore("semD"))
        self.csem = es.enter_context(nc.semaphore("semC"))
        self.count = {e: 0 for e in ENGINES}
        self.count["D"] = 0
        self.ccount = 0
        self.last_wait = {}
        self.prog = {e: [] for e in ENGINES}

    def op(self, eng, fn, deps=(), inc=1, cinc=0, dinc=0):
        if eng in ("V", "S") and not dinc:
            inc = max(inc, 1)
            if self.count[eng] > 0:
                deps = list(deps) + [(eng, self.count[eng])]
        waits = []
        for d in deps:
            if isinstance(d, tuple):
                de, dv = d
            else:
                de, dv = d, (self.ccount if d == "C" else self.count[d])
            if dv <= 0:
                continue
            key = (eng, de)
            if self.last_wait.get(key, -1) >= dv:
                continue
            self.last_wait[key] = dv
            waits.append((self.csem if de == "C" else self.sems[de], dv))
        if dinc:
            self.count["D"] += dinc
            sem = self.sems["D"]
            niv = dinc
        else:
            self.count[eng] += inc
            if cinc:
                self.ccount += cinc
            sem = self.csem if cinc else self.sems[eng]
            niv = cinc if cinc else inc

        def emit(h, waits=waits, fn=fn, sem=sem, niv=niv):
            for s, v in waits:
                h.wait_ge(s, v)
            ins = fn(h)
            if niv:
                ins.then_inc(sem, niv)
        self.prog[eng].append(emit)
        return self.count["D"] if dinc else (self.ccount if cinc else self.count[eng])

    def barrier(self):
        snap = dict(self.count)
        for e in ENGINES:
            deps = [(d, snap[d]) for d in ENGINES if d != e and snap[d] > 0]
            self.op(e, lambda h: h.nop(), deps=deps)

    def finalize(self, block):
        for ename, attr in (("G", "gpsimd"), ("S", "scalar"), ("T", "tensor"),
                            ("V", "vector"), ("Y", "sync")):
            plist = self.prog[ename]

            def runner(h, plist=plist):
                for f in plist:
                    f(h)
            getattr(block, attr)(runner)



def build_nc():
    nc = bass.Bass("TRN2", target_bir_lowering=False)
    hb = {}
    for name, shape, dt in (
        ("x", [1024, L], F32), ("xres", [512, L], F32),
        ("Wq", [1024, 512], F32), ("Wk", [1024, 512], F32), ("Wv", [1024, 512], F32),
        ("Wb2", [1024, 2], F32), ("gW1a", [1024, 1024], F32),
        ("gW1r", [5120, 1024], F32), ("gb1c", [128, 8], F32),
        ("gW2", [1024, 20], F32), ("gb2r", [1, 20], F32),
        ("ltau", [1, 4], F32), ("epsr", [1, 20], F32), ("nwc", [128, 2], F32),
        ("Wo", [1024, 512], F32), ("rwT", [1024, 8], F32),
        ("up", [4, 1024, 2048], F32), ("down", [4, 1024, 1024], F32),
        ("selR", [40, 128, 80], F32),
        ("selstat", [2, 15, 128, 40], F32),
        ("selrow8e", [4, 8, 128], F32),
        ("selmix", [10, 20, 128], F32),
        ("ident", [128, 128], F32), ("onesc", [128, 1], F32), ("biasrow", [1, 2], F32),
        ("onesr", [1, 128], F32), ("ones5", [1, 5], F32), ("zrow", [1, 512], F32),
        ("mlow", [128, 128], F32), ("mup", [128, 128], F32), ("minc", [128, 128], F32),
        ("convST", [3, 4, 4, 128, 128], F32), ("STf", [16, 24, 128, 128], F32),
    ):
        hb[name] = nc.dram_tensor(name, shape, dt, kind="ExternalInput").ap()
    out_h = nc.dram_tensor("out", [512, L], F32, kind="ExternalOutput").ap()
    ds = {}
    for name, shape, dt in (
        ("xn_sp", [1024, L], BF16), ("v_sp", [512, L], BF16),
        ("od_sp", [512, L], BF16),
        ("fs_sp", [512, L], BF16), ("fm_sp", [512, L], BF16), ("fl_sp", [512, L], BF16),
        ("stats_in", [40, L], BF16), ("stats_out", [80, L], BF16),
        ("R_in", [80, 1024], F32), ("R_out", [80, 1024], F32),
        ("lg_in", [L, 20], F32), ("lg_out", [L, 20], F32),
        ("o_in0", [512, L // 2], BF16), ("o_in1", [512, L // 2], BF16),
        ("o_out0", [1024, L // 2], BF16), ("o_out1", [1024, L // 2], BF16),
        ("moe_sp", [4, 1024, 2048], F32), ("moe_rs", [4, 512, 2048], F32),
        ("probs_dbg", [L, 20], F32), ("rl_sp", [L, 8], F32),
    ):
        ds[name] = nc.dram_tensor(name, shape, dt, kind="Internal").ap()

    es = contextlib.ExitStack()
    with es:
        def sb(n, s, d=BF16):
            return es.enter_context(nc.sbuf_tensor(n, s, d))

        def psum(n, s, d=F32):
            return es.enter_context(nc.psum_tensor(n, s, d))

        p = Prog(nc, es)
        E = p.op

        idb = sb("idb", [128, 128])
        onescb = sb("onescb", [128, 1])
        onesrb = sb("onesrb", [1, 128])
        zrowb = sb("zrowb", [1, 512])
        biasr = sb("biasr", [1, 2], F32)
        mlowb = sb("mlowb", [128, 128])
        mupb = sb("mupb", [128, 128])
        mincb = sb("mincb", [128, 128])
        Rsb = sb("Rsb", [80, 1024])
        gb2bc = sb("gb2bc", [128, 20], F32)
        itaubc = sb("itaubc", [128, 20], F32)
        facbc = sb("facbc", [128, 20], F32)
        epsbc = sb("epsbc", [128, 20], F32)
        row20 = sb("row20", [1, 4, 20], F32)
        rowsml = sb("rowsml", [1, 32], F32)
        rowb16 = sb("rowb16", [1, 4, 20])
        ones5v = sb("ones5v", [1, 5], F32)

        pbig = psum("pbig", [128, 512])
        ptr = psum("ptr", [128, 512], BF16)
        pw1 = psum("pw1", [128, 512])
        pw2 = psum("pw2", [128, 512])
        pw3 = psum("pw3", [128, 512])
        psml = psum("psml", [128, 512])

        block = es.enter_context(nc.Block())

        E("Y", lambda h: h.dma_start(out=biasr[:], in_=hb["biasrow"][:]), inc=16)
        for buf, src in ((idb, hb["ident"]), (onescb, hb["onesc"]), (onesrb, hb["onesr"]),
                         (zrowb, hb["zrow"]), (mlowb, hb["mlow"]), (mupb, hb["mup"]),
                         (mincb, hb["minc"])):
            E("G", lambda h, b=buf, s=src: h.dma_start(out=b[:], in_=s[:]), inc=16)
        E("Y", lambda h: h.dma_start(out=row20[0:1, 0, :], in_=hb["gb2r"][:]), inc=16)
        E("Y", lambda h: h.dma_start(out=rowsml[0:1, 0:4], in_=hb["ltau"][:]), inc=16)
        E("Y", lambda h: h.dma_start(out=rowsml[0:1, 4:24], in_=hb["epsr"][:]), inc=16)
        E("Y", lambda h: h.dma_start(out=ones5v[:], in_=hb["ones5"][:]), inc=16)
        consts_g = p.count["G"]
        consts_y = p.count["Y"]

        E("S", lambda h: h.activation(rowsml[0:1, 24:28], rowsml[0:1, 0:4], AF.Abs),
          deps=[("Y", consts_y)])
        E("S", lambda h: h.activation(rowsml[0:1, 24:28], rowsml[0:1, 24:28], AF.Exp, scale=-1.0))
        E("S", lambda h: h.activation(rowsml[0:1, 24:28], rowsml[0:1, 24:28], AF.Ln, bias=1.0))
        E("S", lambda h: h.activation(rowsml[0:1, 28:32], rowsml[0:1, 0:4], AF.Relu))
        E("V", lambda h: h.tensor_add(rowsml[0:1, 24:28], rowsml[0:1, 24:28], rowsml[0:1, 28:32]), deps=["S"])
        E("V", lambda h: h.tensor_scalar(rowsml[0:1, 24:28], rowsml[0:1, 24:28], 0.5, None, ALU.add))
        E("V", lambda h: h.reciprocal(rowsml[0:1, 24:28], rowsml[0:1, 24:28]))
        E("S", lambda h: h.activation(row20[0:1, 3, :], rowsml[0:1, 4:24], AF.Sigmoid,), deps=["V"])
        E("S", lambda h: h.activation(row20[0:1, 3, :], row20[0:1, 3, :], AF.Identity, scale=0.2))
        E("V", lambda h: h.reduce_sum(rowsml[0:1, 28:32], row20[0:1, 3, :].rearrange("o (a b) -> o a b", b=5),
                                      axis=AX), deps=["S"])
        E("V", lambda h: h.tensor_scalar(rowsml[0:1, 28:32], rowsml[0:1, 28:32], -1.0, 1.0, ALU.mult, ALU.add))
        for hh in range(4):
            E("V", lambda h, hh=hh: h.tensor_scalar(
                row20[0:1, 1, 5 * hh:5 * hh + 5], ones5v[0:1, :], rowsml[0:1, 24 + hh:25 + hh],
                None, ALU.mult))
            E("V", lambda h, hh=hh: h.tensor_scalar(
                row20[0:1, 2, 5 * hh:5 * hh + 5], ones5v[0:1, :], rowsml[0:1, 28 + hh:29 + hh],
                None, ALU.mult))
        E("V", lambda h: h.tensor_copy(rowb16[:], row20[:]))
        rows_ready = p.count["V"]
        for i, dst in enumerate((gb2bc, itaubc, facbc, epsbc)):
            E("T", lambda h, i=i: h.matmul(psml[:, 0:20], onesrb[:], rowb16[0:1, i, :],
                                           start=True, stop=True),
              deps=[("V", rows_ready), ("G", consts_g), "V"])
            E("V", lambda h, dst=dst: h.tensor_copy(dst[:], psml[:, 0:20]), deps=["T"])

        selRb = sb("selRb", [128, 2, 80])
        g1rb = sb("g1rb", [128, 2, 1024])
        for t in range(40):
            d = t % 2
            E("G", lambda h, t=t, d=d: h.dma_start(out=selRb[:, d, :], in_=hb["selR"][t]),
              deps=[("T", p.count["T"])], inc=16)
            E("G", lambda h, t=t, d=d: h.dma_start(out=g1rb[:, d, :], in_=hb["gW1r"][128 * t:128 * t + 128, :]),
              inc=16)
            for half in range(2):
                E("T", lambda h, t=t, d=d, half=half: h.matmul(
                    (pw1 if half == 0 else pw2)[0:80, 0:512], selRb[:, d, :],
                    g1rb[:, d, 512 * half:512 * half + 512],
                    start=(t == 0), stop=(t == 39), skip_group_check=True), deps=["G"])
        Rq = sb("Rq", [80, 1024], F32)
        E("V", lambda h: h.tensor_copy(Rq[:, 0:512], pw1[0:80, :]), deps=["T"])
        E("V", lambda h: h.tensor_copy(Rq[:, 512:1024], pw2[0:80, :]))
        E("Y", lambda h: h.dma_start(out=ds["R_in"][:], in_=Rq[:]), deps=["V"], inc=16)
        E("G", lambda h: h.collective_compute(
            "AllReduce", ALU.add, replica_groups=QUADS,
            ins=[ds["R_in"]], outs=[ds["R_out"]]),
          deps=[("Y", p.count["Y"])], inc=0, cinc=1)
        rc_R = p.ccount
        E("G", lambda h: h.dma_start(out=Rsb[:], in_=ds["R_out"][:]), deps=[("C", rc_R)], inc=16)
        Rsb_g = p.count["G"]

        with contextlib.ExitStack() as es1:
            def sb1(n, s, d=BF16):
                return es1.enter_context(nc.sbuf_tensor(n, s, d))
            Sp = [es1.enter_context(nc.psum_tensor("Sp%d" % i, [128, 512], F32)) for i in range(2)]
            Wqb = sb1("Wqb", [128, 8, 512])
            Wkb = sb1("Wkb", [128, 8, 512])
            Wvb = sb1("Wvb", [128, 8, 512])
            Wbb = sb1("Wbb", [128, 8, 2])
            cstb = sb1("cstb", [128, 3, 4, 4, 128])
            rwTf = sb1("rwTf", [128, 8, 8], F32)
            rlstg = sb1("rlstg", [128, 4, 8], F32)
            E("G", lambda h: h.dma_start(out=Wqb[:], in_=hb["Wq"].rearrange("(a p) c -> p a c", p=128)), inc=16)
            E("G", lambda h: h.dma_start(out=Wkb[:], in_=hb["Wk"].rearrange("(a p) c -> p a c", p=128)), inc=16)
            E("G", lambda h: h.dma_start(out=Wvb[:], in_=hb["Wv"].rearrange("(a p) c -> p a c", p=128)), inc=16)
            E("G", lambda h: h.dma_start(out=Wbb[:], in_=hb["Wb2"].rearrange("(a p) c -> p a c", p=128)), inc=16)
            E("G", lambda h: h.dma_start(out=cstb[:], in_=hb["convST"].rearrange("a t c p m -> p a t c m")), inc=16)
            selstatb = sb1("selstatb", [128, 2, 15, 40])
            E("G", lambda h: h.dma_start(out=selstatb[:], in_=hb["selstat"].rearrange("a s p c -> p a s c")), inc=16)
            consts_g = p.count["G"]
            E("Y", lambda h: h.dma_start(out=rwTf[:], in_=hb["rwT"].rearrange("(a p) c -> p a c", p=128)), inc=16)
            xbf = sb1("xbf", [128, 1, 8, 512])
            sqx = sb1("sqx", [128, 2, 512])
            lnrow = sb1("lnrow", [1, 3, 512], F32)
            lnb16 = sb1("lnb16", [1, 2, 512])
            xn_sb = sb1("xn_sb", [128, 8, 512])
            tmpf = sb1("tmpf", [128, 2, 512], F32)
            rq = sb1("rq", [128, 4, 520])
            rk = sb1("rk", [128, 4, 520])
            rv = sb1("rv", [128, 4, 520])
            qkraw = sb1("qkraw", [128, 4, 512])
            qkraw2 = sb1("qkraw2", [128, 4, 512])
            qh_sb = sb1("qh_sb", [128, 4, 512])
            kh_sb = sb1("kh_sb", [128, 4, 512])
            bT = sb1("bT", [1, 1, 512], F32)
            bb16 = sb1("bb16", [1, 2, 512])
            bcol = sb1("bcol", [128, 2, 4], F32)
            rsqb = sb1("rsqb", [1, 1, 512])
            kbg = [sb1("kbg%d" % i, [128, 2, 128]) for i in range(2)]
            At = [sb1("At%d" % i, [128, 128]) for i in range(2)]
            AtT = [sb1("AtT%d" % i, [128, 128]) for i in range(2)]
            locT = [sb1("locT%d" % i, [128, 32]) for i in range(2)]
            sqA = [sb1("sqA%d" % i, [128, 3, 128]) for i in range(2)]
            sqAT = [sb1("sqAT%d" % i, [128, 4, 128]) for i in range(2)]
            X0 = [sb1("X0%d" % i, [128, 512]) for i in range(2)]
            X1 = [sb1("X1%d" % i, [128, 512]) for i in range(2)]
            wT = [sb1("wT%d" % i, [128, 2, 128]) for i in range(2)]
            kts = [sb1("kts%d" % i, [128, 256]) for i in range(2)]
            Sbf = sb1("Sbf", [128, 2, 512])
            u_sb = [sb1("u_sb%d" % i, [128, 256]) for i in range(2)]
            osg = [sb1("osg%d" % i, [128, 256]) for i in range(2)]
            vfir = sb1("vfir", [128, 4, 1096])
            vdup = sb1("vdup", [128, 2, 1092])
            STfb = sb1("STfb", [128, 3, 24, 128])
            fsb = {"fs": sb1("fssb", [128, 4, 1024]), "fm": sb1("fmsb", [128, 4, 1024]),
                   "fl": sb1("flsb", [128, 4, 1024])}
            od_win = sb1("od_win", [128, 4, 1024])
            kscr = sb1("kscr", [128, 6, 512])
            stk = sb1("stk", [128, 5, 512])
            stag40 = sb1("stag40", [128, 40])
            statstg = sb1("statstg", [40, 512])

            E("T", lambda h: h.matmul(Sp[0][:, :], onesrb[:], zrowb[:], start=True, stop=False),
              deps=[("G", consts_g)])
            E("T", lambda h: h.matmul(Sp[1][:, :], onesrb[:], zrowb[:], start=True, stop=False))
            for ring in (rq, rk, rv):
                E("V", lambda h, r=ring: h.memset(r[:, :, 0:8], 0.0))
            E("V", lambda h: h.memset(vfir[:, :, 0:64], 0.0))
            E("V", lambda h: h.memset(vfir[:, :, 1088:1096], 0.0))

            xr = hb["x"].rearrange("(a p) l -> p a l", p=128)
            fir_pending = []
            odspill_y = [0]
            stats_spill_y = [0]
            fsb_spill_y = [0]
            fir_guard = [0, 0]
            fir_guard = [0, 0]
            fir_staged = [0]
            fir_mm_done = {}
            fir_stage_cnt = {}
            fir_bank_free = {0: (0, 0), 1: (0, 0)}
            fir_nm_ctr = [0]

            def stage_fir(grp):
                gl = grp % 16
                gp2 = grp % 2
                sl3 = grp % 3
                ct = gl // 4
                rw = 32 * (gl % 4)
                E("G", lambda h, gl=gl, sl3=sl3: h.dma_start(
                    out=STfb[:, sl3, :, :], in_=hb["STf"][gl].rearrange("G p m -> p G m")),
                  deps=[("T", fir_mm_done.get(grp - 3, 0))], inc=16)
                tdep = ("T", fir_mm_done.get(grp - 2, 0))
                for r, eng in enumerate(("V", "S", "V", "S")):
                    if eng == "V":
                        E("V", lambda h, r=r, rw=rw, ct=ct, gp2=gp2: h.tensor_copy(
                            vdup[32 * r:32 * r + 32, gp2, :], vfir[rw:rw + 32, ct, 1 + r:1093 + r]),
                          deps=[tdep, ("S", p.count["S"])])
                    else:
                        E("S", lambda h, r=r, rw=rw, ct=ct, gp2=gp2: h.activation(
                            vdup[32 * r:32 * r + 32, gp2, :], vfir[rw:rw + 32, ct, 1 + r:1093 + r],
                            AF.Identity), deps=[tdep])
                fir_stage_cnt[grp] = (p.count["G"], p.count["V"], p.count["S"])

            def mm_fir(grp):
                lim = (grp // 16) * 16 + 16
                while fir_staged[0] < lim and fir_staged[0] <= grp + 1:
                    stage_fir(fir_staged[0])
                    fir_staged[0] += 1
                gp2 = grp % 2
                sl3 = grp % 3
                ct = (grp % 16) // 4
                rw = 32 * ((grp % 16) % 4)
                sg, sv, ss = fir_stage_cnt[grp]
                stoff = 0
                for nm, K in FIRS:
                    bi = fir_nm_ctr[0] % 2
                    fir_nm_ctr[0] += 1
                    bank = (pw1, pbig)[bi]
                    bv, bs = fir_bank_free[bi]
                    NG_ = (K + 2) // 4 + 1
                    for G in range(NG_):
                        off = 4 * G + (64 - K)
                        E("T", lambda h, gp2=gp2, bank=bank, so=stoff + G, off=off,
                          s0=(G == 0), s1=(G == NG_ - 1): h.matmul(
                            bank[:, 0:256], STfb[:, sl3, so, :],
                            vdup[:, gp2, off:off + 1024:4],
                            start=s0, stop=s1, skip_group_check=True),
                          deps=([("V", max(sv, bv, fir_guard[0])),
                                 ("S", max(ss, bs, fir_guard[1])), ("G", sg)] if G == 0 else []))
                    stoff += NG_
                    for dd in range(4):
                        if dd % 2 == 0:
                            E("V", lambda h, nm=nm, bank=bank, dd=dd, rw=rw, ct=ct: h.tensor_copy(
                                fsb[nm][rw:rw + 32, ct, dd:1024:4], bank[32 * dd:32 * dd + 32, 0:256]),
                              deps=["T", ("Y", fsb_spill_y[0])])
                        else:
                            E("S", lambda h, nm=nm, bank=bank, dd=dd, rw=rw, ct=ct: h.activation(
                                fsb[nm][rw:rw + 32, ct, dd:1024:4], bank[32 * dd:32 * dd + 32, 0:256],
                                AF.Identity), deps=["T", ("Y", fsb_spill_y[0])])
                    fir_bank_free[bi] = (p.count["V"], p.count["S"])
                fir_mm_done[grp] = p.count["T"]

            for t in range(NT):
                d = t % 2
                sl = slice(TILE * t, TILE * (t + 1))
                g_x = E("G", lambda h, sl=sl: h.dma_start(out=xbf[:, 0, :, :], in_=xr[:, :, sl]),
                        deps=[("S", p.count["S"]), ("T", p.count["T"]), ("V", p.count["V"])], inc=16)
                E("T", lambda h: h.matmul(psml[:, :], onesrb[:], zrowb[:], start=True, stop=False),
                  deps=["V", "S"])
                for dt in range(8):
                    if dt % 2 == 0:
                        E("S", lambda h, dt=dt: h.activation(sqx[:, 0, :], xbf[:, 0, dt, :], AF.Square),
                          deps=[("G", g_x), ("T", p.count["T"])])
                    else:
                        E("V", lambda h, dt=dt: h.tensor_mul(sqx[:, 1, :], xbf[:, 0, dt, :],
                                                             xbf[:, 0, dt, :]),
                          deps=[("G", g_x), ("T", p.count["T"])])
                    E("T", lambda h, dt=dt: h.matmul(psml[0:1, 0:512], onescb[:], sqx[:, dt % 2, :],
                                                     start=False, stop=False, skip_group_check=True),
                      deps=["S" if dt % 2 == 0 else "V"])
                    E("T", lambda h, dt=dt: h.matmul(psml[32:33, 0:512], onescb[:], xbf[:, 0, dt, :],
                                                          start=False, stop=False, skip_group_check=True))
                E("V", lambda h: h.tensor_scalar(lnrow[0:1, 0, :], psml[32:33, 0:512], 1.0 / 1024, None, ALU.mult),
                  deps=["T"])
                E("V", lambda h: h.tensor_scalar(lnrow[0:1, 1, :], psml[0:1, 0:512], 1.0 / 1024, None, ALU.mult))
                E("S", lambda h: h.activation(lnrow[0:1, 2, :], lnrow[0:1, 0, :], AF.Square), deps=["V"])
                E("V", lambda h: h.tensor_sub(lnrow[0:1, 1, :], lnrow[0:1, 1, :], lnrow[0:1, 2, :]), deps=["S"])
                E("V", lambda h: h.tensor_scalar(lnrow[0:1, 1, :], lnrow[0:1, 1, :], 1e-5, None, ALU.add))
                E("S", lambda h: h.activation(lnrow[0:1, 1, :], lnrow[0:1, 1, :], AF.Sqrt), deps=["V"])
                E("V", lambda h: h.reciprocal(lnrow[0:1, 1, :], lnrow[0:1, 1, :]), deps=["S"])
                E("V", lambda h: h.tensor_mul(lnrow[0:1, 2, :], lnrow[0:1, 0, :], lnrow[0:1, 1, :]))
                E("V", lambda h: h.tensor_scalar(lnrow[0:1, 2, :], lnrow[0:1, 2, :], -1.0, None, ALU.mult))
                E("V", lambda h: h.tensor_copy(lnb16[0:1, 0, :], lnrow[0:1, 1, :]), deps=["T"])
                E("V", lambda h: h.tensor_copy(lnb16[0:1, 1, :], lnrow[0:1, 2, :]))
                E("T", lambda h: h.matmul(pw2[:, :], onesrb[:], lnb16[0:1, 0, :], start=True, stop=True),
                  deps=["V"])
                E("T", lambda h: h.matmul(pw3[:, :], onesrb[:], lnb16[0:1, 1, :], start=True, stop=True))
                E("T", lambda h: h.matmul(pbig[:, 0:32], onesrb[:], zrowb[0:1, 0:32],
                                          start=True, stop=False), deps=["V", "S"])
                bcast_t = p.count["T"]
                tmpf_war = {0: bcast_t, 1: bcast_t}
                for dt in range(8):
                    sl2 = dt % 2
                    E("V", lambda h, dt=dt, sl2=sl2: h.tensor_mul(tmpf[:, sl2, :], xbf[:, 0, dt, :], pw2[:, :]),
                      deps=([("T", bcast_t), ("Y", p.count["Y"])] if dt == 0
                            else [("T", tmpf_war[sl2])]))
                    E("V", lambda h, sl2=sl2: h.tensor_add(tmpf[:, sl2, :], tmpf[:, sl2, :], pw3[:, :]))
                    vdone = p.count["V"]
                    E("S", lambda h, dt=dt, sl2=sl2: h.activation(xn_sb[:, dt, :], tmpf[:, sl2, :],
                                                                  AF.Identity), deps=[("V", vdone)])
                    for blk in range(4):
                        E("T", lambda h, sl2=sl2, dt=dt, blk=blk: h.matmul(
                            pbig[:, 8 * blk:8 * blk + 8], tmpf[:, sl2, 128 * blk:128 * blk + 128],
                            rwTf[:, dt, :], start=False, stop=False,
                            skip_group_check=True), deps=([("V", vdone)] if blk == 0 else []))
                    tmpf_war[sl2] = p.count["T"]
                E("V", lambda h: h.tensor_copy(rlstg[:, :, :].rearrange("p a c -> p (a c)"), pbig[:, 0:32]),
                  deps=["T"])
                E("Y", lambda h, t=t: h.dma_start(
                    out=ds["rl_sp"].rearrange("(t b p) c -> p t b c", p=128, b=4)[:, t, :, :],
                    in_=rlstg[:, :, :]), deps=["V"], inc=16)
                xn_s = p.count["S"]
                E("Y", lambda h, sl=sl: h.dma_start(
                    out=ds["xn_sp"].rearrange("(a p) l -> p a l", p=128)[:, :, sl], in_=xn_sb[:, :, :]),
                  deps=[("S", xn_s)], inc=16)
                for hh in range(2):
                    for dt in range(8):
                        E("T", lambda h, hh=hh, dt=dt: h.matmul(
                            psml[0:1, 0:512], Wbb[:, dt, hh:hh + 1], xn_sb[:, dt, :],
                            start=(dt == 0), stop=(dt == 7), skip_group_check=True),
                          deps=(["V", ("S", xn_s)] if dt == 0 else []))
                    E("S", lambda h: h.activation(bT[0:1, 0, :], psml[0:1, 0:512], AF.Tanh, scale=0.5),
                      deps=["T"])
                    E("V", lambda h: h.tensor_scalar(bT[0:1, 0, :], bT[0:1, 0, :], 0.5, 0.5,
                                                     ALU.mult, ALU.add), deps=["S"])
                    E("V", lambda h, hh=hh: h.tensor_copy(bb16[0:1, hh, :], bT[0:1, 0, :]), deps=["T"])
                    for blk in range(4):
                        E("T", lambda h, hh=hh, blk=blk: h.transpose(
                            ptr[:, 0:1], bb16[0:1, hh, 128 * blk:128 * blk + 128], idb[0:1, 0:1]),
                          deps=["V"])
                        E("V", lambda h, hh=hh, blk=blk: h.tensor_copy(bcol[:, hh, blk:blk + 1], ptr[:, 0:1]),
                          deps=["T", "S"])
                l2n_pending = []

                def l2n_flush():
                    while l2n_pending:
                        dsth_, rawb_, hh, rowb, mmt = l2n_pending.pop(0)
                        E("S", lambda h, rowb=rowb: h.activation(
                            lnrow[0:1, 0, :], psml[rowb:rowb + 1, 0:512], AF.Sqrt, bias=biasr[0:1, 0:1]),
                          deps=[("T", mmt), ("V", p.count["V"])])
                        E("V", lambda h: h.reciprocal(lnrow[0:1, 0, :], lnrow[0:1, 0, :]), deps=["S"])
                        E("V", lambda h: h.tensor_copy(rsqb[0:1, 0, :], lnrow[0:1, 0, :]), deps=["T"])
                        E("T", lambda h: h.matmul(
                            pw2[:, :], onesrb[:], rsqb[0:1, 0, :], start=True, stop=True),
                          deps=["V", "S"])
                        tt = p.count["T"]
                        for ci in range(2):
                            ct = 2 * hh + ci
                            E("V", lambda h, dsth_=dsth_, rawb_=rawb_, ct=ct: h.tensor_mul(
                                dsth_[:, ct, :], rawb_[:, ct, :], pw2[:, :]), deps=[("T", tt)])
                for ai, (Wsb, ring, dsth) in enumerate(((Wqb, rq, qh_sb), (Wkb, rk, kh_sb), (Wvb, rv, None))):
                    rawb = qkraw if ai == 0 else qkraw2
                    for ct in range(4):
                        ring_g = 0
                        if t > 0:
                            ring_g = E("G", lambda h, r=ring, ct=ct: h.tensor_copy(
                                r[:, ct, 0:8], r[:, ct, 512:520]), deps=["T", "S"])
                        for dt in range(8):
                            E("T", lambda h, Wsb=Wsb, ct=ct, dt=dt: h.matmul(
                                pw1[:, :], Wsb[:, dt, 128 * ct:128 * ct + 128], xn_sb[:, dt, :],
                                start=(dt == 0), stop=(dt == 7)),
                              deps=(["V", "S"] if dt == 0 else []))
                        E("S", lambda h, r=ring, ct=ct: h.activation(r[:, ct, 8:520], pw1[:, :], AF.Identity),
                          deps=["T", ("G", ring_g)])
                        for jt in range(4):
                            E("T", lambda h, ai=ai, jt=jt, ct=ct, r=ring: h.matmul(
                                pw2[:, :], cstb[:, ai, jt, ct, :], r[:, ct, 5 + jt:5 + jt + 512],
                                start=(jt == 0), stop=(jt == 3)),
                              deps=(["V", "S"] if jt == 0 else []))
                        if dsth is None:
                            E("S", lambda h, ct=ct, t=t: h.activation(
                                vfir[:, ct, 64 + 512 * (t % 2):64 + 512 * (t % 2) + 512],
                                pw2[:, :], AF.Silu), deps=["T", "V", "G", ("Y", p.count["Y"])])
                        else:
                            E("S", lambda h, rawb=rawb, ct=ct: h.activation(rawb[:, ct, :], pw2[:, :],
                                                                            AF.Silu), deps=["T", "V"])
                    if dsth is None:
                        l2n_flush()
                        continue
                    l2n_flush()
                    for hh in range(2):
                        rowb = (0, 32, 64, 32)[2 * ai + hh]
                        for ci in range(2):
                            ct = 2 * hh + ci
                            E("S", lambda h, rawb=rawb, ct=ct, sl2=(2 * hh + ci) % 2: h.activation(
                                sqx[:, sl2, :], rawb[:, ct, :], AF.Square),
                              deps=[("T", p.count["T"]), ("V", p.count["V"])])
                            E("T", lambda h, rowb=rowb, sl2=(2 * hh + ci) % 2, s0=(ci == 0): h.matmul(
                                psml[rowb:rowb + 1, 0:512], onescb[:], sqx[:, sl2, :],
                                start=s0, stop=False, skip_group_check=True), deps=["S"])
                        l2n_pending.append((dsth, rawb, hh, rowb, p.count["T"]))
                if t % 2 == 1:
                    base16 = (t // 2) * 16
                    fir_pending.extend(range(base16, base16 + 16))
                for g in range(4):
                    ga = t * 4 + g
                    gsl = slice(128 * g, 128 * (g + 1))
                    pmm = [pbig, pw1]
                    pap = [pw2, psml]
                    ptrs = [0]
                    ptr_free = [("V", 0)]

                    def ptr_slot():
                        s = ptrs[0] % 4
                        ptrs[0] += 1
                        return s, ptr[:, 128 * s:128 * s + 128]
                    tc = [p.count["T"], p.count["T"]]
                    vc = [p.count["V"], p.count["V"]]
                    voff = 64 + 512 * (t % 2) + 128 * g
                    for hh in range(2):
                        tc[hh] = E("T", lambda h, hh=hh, gsl=gsl: h.matmul(
                            pmm[hh][:, 0:128], onesrb[:], bb16[0:1, hh, gsl], start=True, stop=True),
                            deps=["V", "S"])
                    for hh in range(2):
                        for ci, ct in enumerate((2 * hh, 2 * hh + 1)):
                            E("V", lambda h, hh=hh, ci=ci, ct=ct, gsl=gsl: h.tensor_mul(
                                kbg[hh][:, ci, :], kh_sb[:, ct, gsl], pmm[hh][:, 0:128]),
                              deps=[("T", tc[hh])])
                        vc[hh] = p.count["V"]
                    for hh in range(2):
                        for ci, ct in enumerate((2 * hh, 2 * hh + 1)):
                            E("T", lambda h, hh=hh, ci=ci, ct=ct, gsl=gsl: h.matmul(
                                pmm[hh][:, 0:128], kbg[hh][:, ci, :], kh_sb[:, ct, gsl],
                                start=(ci == 0), stop=(ci == 1), skip_group_check=True),
                              deps=[("V", vc[hh])])
                        for ci, ct in enumerate((2 * hh, 2 * hh + 1)):
                            E("T", lambda h, hh=hh, ci=ci, ct=ct, gsl=gsl: h.matmul(
                                pmm[hh][:, 128:256], kh_sb[:, ct, gsl], kbg[hh][:, ci, :],
                                start=(ci == 0), stop=(ci == 1), skip_group_check=True))
                        for ci, ct in enumerate((2 * hh, 2 * hh + 1)):
                            E("T", lambda h, hh=hh, ci=ci, ct=ct, gsl=gsl: h.matmul(
                                pmm[hh][:, 256:384], kh_sb[:, ct, gsl], qh_sb[:, ct, gsl],
                                start=(ci == 0), stop=(ci == 1), skip_group_check=True))
                        tc[hh] = p.count["T"]
                    for hh in range(2):
                        E("V", lambda h, hh=hh: h.tensor_mul(At[hh][:, :], pmm[hh][:, 0:128], mlowb[:, :]),
                          deps=[("T", tc[hh])])
                        E("V", lambda h, hh=hh: h.tensor_mul(AtT[hh][:, :], pmm[hh][:, 128:256], mupb[:, :]))
                        for r in range(4):
                            rs = slice(32 * r, 32 * r + 32)
                            E("V", lambda h, hh=hh, rs=rs: h.tensor_mul(
                                locT[hh][rs, :], pmm[hh][rs, 256 + rs.start:256 + rs.start + 32],
                                mincb[rs, rs]))
                        vc[hh] = p.count["V"]
                    def apg(hh, x):
                        if isinstance(x, tuple):
                            return x[0][hh][:, x[1], :]
                        return x[hh][:, :]
                    xb_units = []
                    for hh in range(2):
                        for ci in range(2):
                            xb_units.append(("v", hh, ci))
                            xb_units.append(("k", hh, ci))

                    def emit_xb(unit, voff=voff, g=g, gsl=gsl):
                        kind, hh, ci = unit
                        ct = 2 * hh + ci
                        s, pv = ptr_slot()
                        if kind == "v":
                            E("T", lambda h, ct=ct, pv=pv, voff=voff: h.transpose(
                                pv, vfir[:, ct, voff:voff + 128], idb[:, :]),
                              deps=[ptr_free[0], ("S", p.count["S"])])
                            tt = p.count["T"]
                            E("S", lambda h, hh=hh, ci=ci, pv=pv, g=g: h.activation(
                                X0[hh][:, 128 * ci:128 * ci + 128], pv, AF.Identity,
                                scale=bcol[:, hh, g:g + 1]), deps=[("T", tt)])
                            ptr_free[0] = ("S", p.count["S"])
                        else:
                            E("T", lambda h, ct=ct, pv=pv, gsl=gsl: h.transpose(
                                pv, kh_sb[:, ct, gsl], idb[:, :]),
                              deps=[ptr_free[0]])
                            tt = p.count["T"]
                            E("V", lambda h, hh=hh, ci=ci, pv=pv: h.tensor_copy(
                                kts[hh][:, 128 * ci:128 * ci + 128], pv), deps=[("T", tt)])
                            E("V", lambda h, hh=hh, ci=ci, pv=pv, g=g: h.tensor_scalar(
                                X0[hh][:, 256 + 128 * ci:256 + 128 * ci + 128], pv, bcol[:, hh, g:g + 1],
                                None, ALU.mult))
                            ptr_free[0] = ("V", p.count["V"])
                    xb_i = 0
                    sc_hh = [p.count["S"], p.count["S"]]
                    sqlv = (((AtT, At, (sqA, 0)), (At, AtT, (sqAT, 0))),
                            (((sqAT, 0), (sqA, 0), (sqA, 1)), ((sqA, 0), (sqAT, 0), (sqAT, 1))),
                            (((sqAT, 1), (sqA, 1), (sqA, 2)), ((sqA, 1), (sqAT, 1), (sqAT, 2))),
                            (((sqA, 2), (sqAT, 2), (sqAT, 3)),))
                    for level in sqlv:
                        for _ in range(2):
                            if xb_i < len(xb_units):
                                emit_xb(xb_units[xb_i])
                                xb_i += 1
                        for hh in range(2):
                            for li, (lhsT, rhs, dstx) in enumerate(level):
                                E("T", lambda h, hh=hh, a=lhsT, b=rhs, li=li: h.matmul(
                                    pmm[hh][:, 128 * li:128 * li + 128], apg(hh, a), apg(hh, b),
                                    start=True, stop=True, skip_group_check=True),
                                  deps=([("V", vc[hh]), ("S", sc_hh[hh])] if li == 0 else []))
                            tc[hh] = p.count["T"]
                        for hh in range(2):
                            for li, (lhsT, rhs, dstx) in enumerate(level):
                                if li % 2 == 0:
                                    E("S", lambda h, hh=hh, dstx=dstx, li=li: h.activation(
                                        apg(hh, dstx), pmm[hh][:, 128 * li:128 * li + 128], AF.Identity),
                                      deps=[("T", tc[hh])])
                                else:
                                    E("V", lambda h, hh=hh, dstx=dstx, li=li: h.tensor_copy(
                                        apg(hh, dstx), pmm[hh][:, 128 * li:128 * li + 128]),
                                      deps=[("T", tc[hh])])
                            vc[hh] = p.count["V"]
                            sc_hh[hh] = p.count["S"]
                    while xb_i < len(xb_units):
                        emit_xb(xb_units[xb_i])
                        xb_i += 1
                    xb_done = p.count["V"]
                    fir_guard[0] = p.count["V"]
                    fir_guard[1] = p.count["S"]
                    xb_done_s = p.count["S"]
                    fir_guard[0] = p.count["V"]
                    fir_guard[1] = p.count["S"]
                    xcur = [X0[0], X0[1]]
                    xnxt = [X1[0], X1[1]]
                    for lvl in ((sqAT, 3), (sqAT, 2), (sqAT, 1), (sqAT, 0), AtT):
                        for hh in range(2):
                            tc[hh] = E("T", lambda h, hh=hh, lvl=lvl, xc=xcur[hh]: h.matmul(
                                pap[hh][:, :], apg(hh, lvl), xc[:, :], start=True, stop=True,
                                skip_group_check=True), deps=[("V", vc[hh]), ("V", xb_done),
                                                              ("S", sc_hh[hh]),
                                                              ("S", xb_done_s)])
                        for hh in range(2):
                            E("V", lambda h, hh=hh, xc=xcur[hh], xn_=xnxt[hh]: h.tensor_add(
                                xn_[:, :], xc[:, :], pap[hh][:, :]), deps=[("T", tc[hh])])
                            vc[hh] = p.count["V"]
                        xcur, xnxt = xnxt, xcur
                    Xf = xcur
                    for hh in range(2):
                        for ci in range(2):
                            s, pv = ptr_slot()
                            E("T", lambda h, hh=hh, ci=ci, pv=pv, Xf=Xf: h.transpose(
                                pv, Xf[hh][:, 256 + 128 * ci:256 + 128 * ci + 128], idb[:, :]),
                              deps=[ptr_free[0], ("V", vc[hh])])
                            tt = p.count["T"]
                            E("V", lambda h, hh=hh, ci=ci, pv=pv: h.tensor_copy(wT[hh][:, ci, :], pv),
                              deps=[("T", tt)])
                            ptr_free[0] = ("V", p.count["V"])
                        vc[hh] = p.count["V"]
                    ureg = [pw3[0:32, 0:256], pw2[0:32, 0:256]]
                    oreg = [pw3[32:64, 0:256], pw2[32:64, 0:256]]
                    vall = p.count["V"]
                    for r in range(4):
                        gc = ga * 4 + r
                        rsl = slice(32 * r, 32 * r + 32)
                        tp96 = (32 * r, 0)
                        sbf_c = [None, None]
                        if gc > 0:
                            E("V", lambda h: h.tensor_copy(Sbf[:, 0, :], Sp[0][:, :]),
                              deps=[("T", tc[0]), ("V", vall)])
                            sbf_c[0] = ("V", p.count["V"])
                            E("S", lambda h: h.activation(Sbf[:, 1, :], Sp[1][:, :], AF.Identity),
                              deps=[("T", tc[1])])
                            sbf_c[1] = ("S", p.count["S"])
                            ut = [0, 0]
                            for hh in range(2):
                                for ci in range(2):
                                    E("T", lambda h, hh=hh, ci=ci, rsl=rsl: h.matmul(
                                        ureg[hh], wT[hh][:, ci, rsl], Sbf[:, hh, 256 * ci:256 * ci + 256],
                                        start=(ci == 0), stop=(ci == 1), skip_group_check=True),
                                      deps=[sbf_c[hh]])
                                ut[hh] = p.count["T"]
                            for hh in range(2):
                                E("V", lambda h, hh=hh, rsl=rsl, Xf=Xf: h.tensor_sub(
                                    u_sb[hh][rsl, :], Xf[hh][rsl, 0:256], ureg[hh]),
                                  deps=[("T", ut[hh])])
                                vc[hh] = p.count["V"]
                            ot = [0, 0]
                            for hh in range(2):
                                for ci, ct in enumerate((2 * hh, 2 * hh + 1)):
                                    E("T", lambda h, hh=hh, ct=ct, ci=ci, g=g, r=r: h.matmul(
                                        oreg[hh], qh_sb[:, ct, 128 * g + 32 * r:128 * g + 32 * r + 32],
                                        Sbf[:, hh, 256 * ci:256 * ci + 256], start=(ci == 0), stop=False,
                                        skip_group_check=True), deps=[("V", vc[hh])])
                                E("T", lambda h, hh=hh, rsl=rsl, r=r: h.matmul(
                                    oreg[hh], locT[hh][rsl, :], u_sb[hh][rsl, :],
                                    start=False, stop=True, skip_group_check=True,
                                    tile_position=(32 * r, 32)))
                                ot[hh] = p.count["T"]
                        else:
                            for hh in range(2):
                                E("V", lambda h, hh=hh, rsl=rsl, Xf=Xf: h.tensor_copy(
                                    u_sb[hh][rsl, :], Xf[hh][rsl, 0:256]), deps=[("T", tc[hh])])
                                vc[hh] = p.count["V"]
                            ot = [0, 0]
                            for hh in range(2):
                                E("T", lambda h, hh=hh, rsl=rsl, r=r: h.matmul(
                                    oreg[hh], locT[hh][rsl, :], u_sb[hh][rsl, :],
                                    start=True, stop=True, skip_group_check=True,
                                    tile_position=(32 * r, 32)), deps=[("V", vc[hh]), ("V", vall)])
                                ot[hh] = p.count["T"]
                        for hh in range(2):
                            E("S", lambda h, hh=hh, rsl=rsl: h.activation(osg[hh][rsl, :], oreg[hh],
                                                                          AF.Identity),
                              deps=[("T", ot[hh])])
                        os_s = p.count["S"]
                        for hh in range(2):
                            for ci in range(2):
                                E("T", lambda h, hh=hh, ci=ci, rsl=rsl, tp96=tp96: h.matmul(
                                    Sp[hh][:, 256 * ci:256 * ci + 256],
                                    kts[hh][rsl, 128 * ci:128 * ci + 128],
                                    u_sb[hh][rsl, :], start=False, stop=True,
                                    skip_group_check=True, tile_position=tp96),
                                  deps=[("V", vc[hh]), ("S", os_s)])
                            tc[hh] = p.count["T"]
                        if fir_pending:
                            mm_fir(fir_pending.pop(0))
                    for hh in range(2):
                        for ci in range(2):
                            s, pv = ptr_slot()
                            E("T", lambda h, hh=hh, ci=ci, pv=pv: h.transpose(
                                pv, osg[hh][:, 128 * ci:128 * ci + 128], idb[:, :]),
                              deps=[ptr_free[0], ("V", vc[hh]), ("S", p.count["S"])])
                            tt = p.count["T"]
                            E("V", lambda h, hh=hh, ci=ci, t=t, g=g, pv=pv: h.tensor_copy(
                                od_win[:, 2 * hh + ci, 512 * (t % 2) + 128 * g:512 * (t % 2) + 128 * g + 128],
                                pv), deps=[("T", tt), ("Y", odspill_y[0])])
                            ptr_free[0] = ("V", p.count["V"])
                if t % 2 == 1:
                    w = t // 2
                    wsl = slice(WIN * w, WIN * (w + 1))
                    for grp in fir_pending:
                        mm_fir(grp)
                    fir_pending.clear()
                    for sub in range(2):
                        ssl = slice(512 * sub, 512 * sub + 512)
                        vof = 64 + 512 * sub
                        E("T", lambda h: h.matmul(psml[:, 0:160], onesrb[:], zrowb[0:1, 0:160],
                                                  start=True, stop=False), deps=["V", "S"])

                        def pathf(pi, hh, ci):
                            if pi < 3:
                                return fsb[("fs", "fm", "fl")[pi]][:, 2 * hh + ci, ssl]
                            if pi == 3:
                                return od_win[:, 2 * hh + ci, ssl]
                            return vfir[:, 2 * hh + ci, vof:vof + 512]
                        combos = [(hh, pi, kd) for hh in range(2) for pi in range(P5)
                                  for kd in range(3)]
                        LEAD = 3
                        stk_free = {i: 0 for i in range(5)}
                        kscr_free = {i: 0 for i in range(3)}
                        add_v = {}
                        kp_of = {}
                        kp_ctr = [0]

                        def preproc(idx, pathf=pathf):
                            hh, pi, kd = combos[idx]
                            sslot = idx % 5
                            if kd == 0:
                                a, b = pathf(pi, hh, 0), pathf(pi, hh, 1)
                                sdep = ("S", p.count["S"])
                            else:
                                kp = kp_ctr[0] % 3
                                kp_ctr[0] += 1
                                kp_of[idx] = kp
                                if kd == 1:
                                    for ci in range(2):
                                        E("S", lambda h, pi=pi, hh=hh, ci=ci, kp=kp, pathf=pathf:
                                          h.activation(kscr[:, 2 * kp + ci, :], pathf(pi, hh, ci),
                                                       AF.Square),
                                          deps=[("T", kscr_free[kp]), ("V", p.count["V"])])
                                    sdep = ("S", p.count["S"])
                                else:
                                    for ci in range(2):
                                        E("V", lambda h, pi=pi, hh=hh, ci=ci, kp=kp, pathf=pathf:
                                          h.scalar_tensor_tensor(kscr[:, 2 * kp + ci, :],
                                                                 pathf(pi, hh, ci), -1.0,
                                                                 pathf(pi, hh, ci),
                                                                 ALU.mult, ALU.max),
                                          deps=[("T", kscr_free[kp]), ("S", p.count["S"])])
                                    sdep = ("V", p.count["V"])
                                a, b = kscr[:, 2 * kp, :], kscr[:, 2 * kp + 1, :]
                            E("V", lambda h, a=a, b=b, sslot=sslot: h.tensor_add(
                                stk[:, sslot, :], a, b), deps=[sdep, ("T", stk_free[sslot])])
                            add_v[idx] = p.count["V"]

                        def emit_mm(idx):
                            hh, pi, kd = combos[idx]
                            sslot = idx % 5
                            combo = pi * 3 + kd
                            for blk in range(4):
                                E("T", lambda h, hh=hh, combo=combo, sslot=sslot, blk=blk: h.matmul(
                                    psml[:, 40 * blk:40 * blk + 40],
                                    stk[:, sslot, 128 * blk:128 * blk + 128],
                                    selstatb[:, hh, combo, :], start=False, stop=False,
                                    skip_group_check=True),
                                  deps=([("V", add_v[idx])] if blk == 0 else []))
                            stk_free[sslot] = p.count["T"]
                            if idx in kp_of:
                                kscr_free[kp_of[idx]] = p.count["T"]
                        for idx in range(30):
                            preproc(idx)
                            if idx >= LEAD:
                                emit_mm(idx - LEAD)
                        for idx in range(30 - LEAD, 30):
                            emit_mm(idx)
                        for blk in range(4):
                            E("V", lambda h, blk=blk: h.tensor_scalar(
                                psml[:, 40 * blk:40 * blk + 40].rearrange("p (a b) -> p a b", b=4)[:, :, 0:3],
                                psml[:, 40 * blk:40 * blk + 40].rearrange("p (a b) -> p a b", b=4)[:, :, 0:3],
                                1.0 / 256, None, ALU.mult), deps=["T"])
                            E("S", lambda h, blk=blk: h.activation(
                                psml[:, 40 * blk:40 * blk + 40].rearrange("p (a b) -> p a b", b=4)[:, :, 3:4],
                                psml[:, 40 * blk:40 * blk + 40].rearrange("p (a b) -> p a b", b=4)[:, :, 3:4],
                                AF.Sqrt), deps=["V"])
                            E("V", lambda h, blk=blk: h.tensor_copy(stag40[:, :], psml[:, 40 * blk:40 * blk + 40]),
                              deps=["S", "T"])
                            E("T", lambda h: h.transpose(ptr[0:40, 0:128], stag40[:, :], idb[:, :]), deps=["V"])
                            E("V", lambda h, blk=blk: h.tensor_copy(statstg[:, 128 * blk:128 * blk + 128],
                                                                    ptr[0:40, 0:128]), deps=["T", ("Y", p.count["Y"])])
                        E("Y", lambda h, w=w, sub=sub: h.dma_start(
                            out=ds["stats_in"][:, WIN * w + 512 * sub:WIN * w + 512 * sub + 512],
                            in_=statstg[:, :]), deps=["V"], inc=16)
                    E("Y", lambda h, wsl=wsl: h.dma_start(
                        out=ds["od_sp"].rearrange("(a p) l -> p a l", p=128)[:, :, wsl],
                        in_=od_win[:, :, :]), deps=["V"], inc=16)
                    odspill_y[0] = p.count["Y"]
                    E("Y", lambda h, wsl=wsl: h.dma_start(
                        out=ds["v_sp"].rearrange("(a p) l -> p a l", p=128)[:, :, wsl],
                        in_=vfir[:, :, 64:1088]), deps=["S", "V"], inc=16)
                    for nm in ("fs", "fm", "fl"):
                        E("Y", lambda h, nm=nm, wsl=wsl: h.dma_start(
                            out=ds[nm + "_sp"].rearrange("(a p) l -> p a l", p=128)[:, :, wsl],
                            in_=fsb[nm][:, :, :]), deps=["V", "S"], inc=16)
                    fsb_spill_y[0] = p.count["Y"]
                    spill_y = p.count["Y"]
                    for ct in range(4):
                        E("V", lambda h, ct=ct: h.tensor_copy(vfir[:, ct, 0:64], vfir[:, ct, 1024:1088]),
                          deps=[("Y", spill_y), ("T", p.count["T"])])
            E("G", lambda h: h.collective_compute(
                "AllGather", ALU.bypass, replica_groups=PAIRS,
                ins=[ds["stats_in"]], outs=[ds["stats_out"]]),
              deps=[("Y", p.count["Y"])], inc=0, cinc=1)
            cc_stats = p.ccount
            p.barrier()

        with contextlib.ExitStack() as es2:
            def sb2(n, s, d=BF16):
                return es2.enter_context(nc.sbuf_tensor(n, s, d))
            up_b = sb2("up_b", [128, 2, 8, 2048])
            dn_b = sb2("dn_b", [128, 2, 8, 1024])
            xn_ld = sb2("xn_ld", [128, 2, 8, 512])
            rlld = sb2("rlld", [128, 2, 4, 8], F32)
            selrow8b = sb2("selrow8b", [8, 4, 128])
            hid = sb2("hid", [128, 8, 512])
            htmp = sb2("htmp", [128, 512])
            gateT = sb2("gateT", [8, L])
            gtb16 = sb2("gtb16", [128, 8])
            rb8 = sb2("rb8", [128, 6, 8], F32)
            cols8 = sb2("cols8", [128, 8], F32)
            mstage = sb2("mstage", [128, 2, 8, 512], F32)
            tmpf_moe = sb2("tmpf_moe", [128, 512])
            gbc = sb2("gbc", [128, 512])
            E("G", lambda h: h.dma_start(out=selrow8b[:], in_=hb["selrow8e"].rearrange("e k m -> k e m")), inc=16)
            E("G", lambda h: h.dma_start(out=up_b[:, 0, :, :], in_=hb["up"][0].rearrange("(a p) c -> p a c", p=128)), inc=16)
            E("G", lambda h: h.dma_start(out=dn_b[:, 0, :, :], in_=hb["down"][0].rearrange("(a p) c -> p a c", p=128)), inc=16)
            moe_accum_g = {}
            xnr = ds["xn_sp"].rearrange("(a p) l -> p a l", p=128)
            for e in range(4):
                if e < 3:
                    E("G", lambda h, e=e: h.dma_start(
                        out=up_b[:, (e + 1) % 2, :, :],
                        in_=hb["up"][e + 1].rearrange("(a p) c -> p a c", p=128)),
                      deps=[("T", p.count["T"])], inc=16)
                    E("G", lambda h, e=e: h.dma_start(
                        out=dn_b[:, (e + 1) % 2, :, :],
                        in_=hb["down"][e + 1].rearrange("(a p) c -> p a c", p=128)), inc=16)
                eb = e % 2
                w_g = p.count["G"]
                for t in range(NT):
                    d = t % 2
                    sl = slice(TILE * t, TILE * (t + 1))
                    it = e * NT + t
                    if it == 0:
                        xn_y = {}
                        xn_war = {0: 0, 1: 0}

                        def load_xn(it2):
                            tt_ = it2 % NT
                            d_ = tt_ % 2
                            sl_ = slice(TILE * tt_, TILE * (tt_ + 1))
                            E("Y", lambda h, d_=d_, sl_=sl_: h.dma_start(
                                out=xn_ld[:, d_, :, :], in_=xnr[:, :, sl_]),
                              deps=[("T", xn_war[d_])], inc=16)
                            xn_y[d_] = p.count["Y"]
                        load_xn(0)
                    if it + 1 < 4 * NT:
                        load_xn(it + 1)
                    y_xn = xn_y[d]
                    if e == 0:
                        E("Y", lambda h, d=d, t=t: h.dma_start(
                            out=rlld[:, d, :, :],
                            in_=ds["rl_sp"].rearrange("(t b p) c -> p t b c", p=128, b=4)[:, t, :, :]),
                          deps=[("V", p.count["V"])], inc=16)
                        y_rl = p.count["Y"]
                        for blk in range(4):
                            ps8 = rlld[:, d, blk, :]
                            E("V", lambda h, ps8=ps8: h.reduce_max(cols8[:, 0:1], ps8, axis=AX),
                              deps=["T", ("Y", y_rl)])
                            E("V", lambda h, ps8=ps8: h.tensor_scalar(rb8[:, 0, :], ps8, cols8[:, 0:1],
                                                                      None, ALU.subtract))
                            E("S", lambda h: h.activation(rb8[:, 0, :], rb8[:, 0, :], AF.Exp), deps=["V"])
                            E("V", lambda h: h.reduce_sum(cols8[:, 1:2], rb8[:, 0, :], axis=AX), deps=["S"])
                            E("V", lambda h: h.reciprocal(cols8[:, 1:2], cols8[:, 1:2]))
                            E("V", lambda h: h.tensor_scalar(rb8[:, 1, :], rb8[:, 0, :], cols8[:, 1:2],
                                                             None, ALU.mult))
                            E("V", lambda h: h.reduce_max(cols8[:, 2:3], rb8[:, 1, :], axis=AX))
                            E("V", lambda h: h.tensor_scalar(rb8[:, 2, :], rb8[:, 1, :], cols8[:, 2:3],
                                                             None, ALU.is_ge))
                            E("V", lambda h: h.tensor_mul(rb8[:, 3, :], rb8[:, 1, :], rb8[:, 2, :]))
                            E("V", lambda h: h.tensor_sub(rb8[:, 3, :], rb8[:, 1, :], rb8[:, 3, :]))
                            E("V", lambda h: h.reduce_max(cols8[:, 3:4], rb8[:, 3, :], axis=AX))
                            E("V", lambda h: h.tensor_scalar(rb8[:, 4, :], rb8[:, 3, :], cols8[:, 3:4],
                                                             None, ALU.is_ge))
                            E("V", lambda h: h.tensor_add(cols8[:, 4:5], cols8[:, 2:3], cols8[:, 3:4]))
                            E("V", lambda h: h.reciprocal(cols8[:, 4:5], cols8[:, 4:5]))
                            E("V", lambda h: h.tensor_mul(cols8[:, 5:6], cols8[:, 2:3], cols8[:, 4:5]))
                            E("V", lambda h: h.tensor_mul(cols8[:, 6:7], cols8[:, 3:4], cols8[:, 4:5]))
                            E("V", lambda h: h.tensor_scalar(rb8[:, 5, :], rb8[:, 4, :], cols8[:, 6:7],
                                                             None, ALU.mult))
                            E("V", lambda h: h.scalar_tensor_tensor(gtb16[:, :], rb8[:, 2, :], cols8[:, 5:6],
                                                                    rb8[:, 5, :], ALU.mult, ALU.add), deps=["T"])
                            E("T", lambda h: h.transpose(ptr[0:8, 0:128], gtb16[:, :], idb[:, :]), deps=["V"])
                            E("V", lambda h, t=t, blk=blk: h.tensor_copy(
                                gateT[0:8, TILE * t + 128 * blk:TILE * t + 128 * blk + 128], ptr[0:8, 0:128]),
                              deps=["T"])
                    E("T", lambda h, e=e, t=t: h.matmul(
                        pw3[:, :], selrow8b[:, e, :], gateT[0:8, TILE * t:TILE * t + 512],
                        start=True, stop=True), deps=["V"])
                    E("V", lambda h: h.tensor_copy(gbc[:, :], pw3[:, :]), deps=["T"])
                    gbc_v = p.count["V"]
                    pair_vdone = {}
                    pair_sdone = {}
                    for ci in range(8):
                        bA, bB = (pw1, pw2) if ci % 2 == 0 else (pw3, psml)
                        dep0 = [("V", pair_vdone.get(ci - 2, gbc_v)),
                                ("S", pair_sdone.get(ci - 2, 0)), ("G", w_g), ("Y", y_xn)]
                        for dt in range(8):
                            E("T", lambda h, eb=eb, d=d, ci=ci, dt=dt, bA=bA: h.matmul(
                                bA[:, :], up_b[:, eb, dt, 128 * ci:128 * ci + 128], xn_ld[:, d, dt, :],
                                start=(dt == 0), stop=(dt == 7)),
                              deps=(dep0 if dt == 0 else []))
                        for dt in range(8):
                            E("T", lambda h, eb=eb, d=d, ci=ci, dt=dt, bB=bB: h.matmul(
                                bB[:, :], up_b[:, eb, dt, 1024 + 128 * ci:1024 + 128 * ci + 128],
                                xn_ld[:, d, dt, :], start=(dt == 0), stop=(dt == 7)))
                        tdone = p.count["T"]
                        E("S", lambda h, bA=bA: h.activation(htmp[:, :], bA[:, :], AF.Silu),
                          deps=[("T", tdone)])
                        pair_sdone[ci] = p.count["S"]
                        E("V", lambda h, bB=bB: h.tensor_mul(tmpf_moe[:, :], htmp[:, :], bB[:, :]),
                          deps=["S"])
                        E("V", lambda h, ci=ci: h.tensor_mul(hid[:, ci, :], tmpf_moe[:, :], gbc[:, :]))
                        pair_vdone[ci] = p.count["V"]
                    oc_vdone = {}
                    for oc in range(8):
                        bC = pw1 if oc % 2 == 0 else pw2
                        dep0 = [("V", oc_vdone.get(oc - 2, p.count["V"]))]
                        for ht in range(8):
                            E("T", lambda h, eb=eb, oc=oc, ht=ht, bC=bC: h.matmul(
                                bC[:, :], dn_b[:, eb, ht, 128 * oc:128 * oc + 128], hid[:, ht, :],
                                start=(ht == 0), stop=(ht == 7)),
                              deps=(dep0 if ht == 0 else []))
                        tdone = p.count["T"]
                        E("V", lambda h, d=d, oc=oc, bC=bC: h.tensor_copy(mstage[:, d, oc, :], bC[:, :]),
                          deps=[("T", tdone), ("G", moe_accum_g.get((e, t - 2), 0))])
                        oc_vdone[oc] = p.count["V"]
                    mv = p.count["V"]
                    E("G", lambda h, e=e, t=t, d=d: h.dma_start(
                        out=ds["moe_sp"][t // 4].rearrange("(a p) l -> p a l", p=128)[
                            :, :, 512 * (t % 4):512 * (t % 4) + 512],
                        in_=mstage[:, d, :, :],
                        accum_op=(ALU.bypass if e == 0 else ALU.add)),
                      deps=[("V", mv), ("G", moe_accum_g.get((e - 1, t), 0))], inc=16)
                    moe_accum_g[(e, t)] = p.count["G"]
                    xn_war[d] = p.count["T"]
                    if e == 3 and t % 4 == 3:
                        lq = t // 4
                        E("G", lambda h, lq=lq: h.collective_compute(
                            "ReduceScatter", ALU.add, replica_groups=PAIRS,
                            ins=[ds["moe_sp"][lq]], outs=[ds["moe_rs"][lq]]),
                          inc=0, cinc=1)
            if DEBUG:
                dbgg = nc.dram_tensor("dbg_gate", [8, L], F32, kind="ExternalOutput").ap()
                E("G", lambda h, dbgg=dbgg: h.dma_start(out=dbgg[:], in_=gateT[:, :]),
                  deps=["V", "T", ("G", p.count["G"])], inc=16)
                for lq in range(4):
                    dbgm = nc.dram_tensor("dbg_msp%d" % lq, [1024, 2048], F32, kind="ExternalOutput").ap()
                    E("Y", lambda h, dbgm=dbgm, lq=lq: h.dma_start(out=dbgm[:], in_=ds["moe_sp"][lq]),
                      deps=[("G", p.count["G"])], inc=16)
            cc_moe = p.ccount
            p.barrier()

        with contextlib.ExitStack() as es3:
            def sb3(n, s, d=BF16):
                return es3.enter_context(nc.sbuf_tensor(n, s, d))
            gW1ab = sb3("gW1ab", [128, 8, 1024])
            gb1b = sb3("gb1b", [128, 8], F32)
            gW2b = sb3("gW2b", [128, 8, 20])
            statsAG = sb3("statsAG", [80, L])
            xn_ld2 = sb3("xn_ld2", [128, 2, 8, 512])
            G1 = sb3("G1", [128, 8, 512])
            lgstg = sb3("lgstg", [128, 4, 20], F32)
            E("G", lambda h: h.dma_start(out=gW1ab[:], in_=hb["gW1a"].rearrange("(a p) c -> p a c", p=128)), inc=16)
            E("Y", lambda h: h.dma_start(out=gb1b[:], in_=hb["gb1c"][:]), inc=16)
            E("G", lambda h: h.dma_start(out=gW2b[:], in_=hb["gW2"].rearrange("(a p) c -> p a c", p=128)), inc=16)
            E("G", lambda h: h.dma_start(out=statsAG[:], in_=ds["stats_out"][:]),
              deps=[("C", cc_stats)], inc=16)
            w2a_g = p.count["G"]
            w2a_y = p.count["Y"]
            for t in range(NT):
                d = t % 2
                sl = slice(TILE * t, TILE * (t + 1))
                E("Y", lambda h, d=d, sl=sl: h.dma_start(out=xn_ld2[:, d, :, :], in_=xnr[:, :, sl]),
                  deps=[("T", p.count["T"])], inc=16)
                y_xn = p.count["Y"]
                for ct in range(8):
                    for dt in range(8):
                        E("T", lambda h, d=d, ct=ct, dt=dt: h.matmul(
                            pw1[:, :], gW1ab[:, dt, 128 * ct:128 * ct + 128], xn_ld2[:, d, dt, :],
                            start=(dt == 0), stop=False, skip_group_check=True),
                          deps=(["S", ("G", w2a_g), ("Y", y_xn)] if dt == 0 else []))
                    E("T", lambda h, ct=ct, sl=sl: h.matmul(
                        pw1[:, :], Rsb[:, 128 * ct:128 * ct + 128], statsAG[:, sl],
                        start=False, stop=True, skip_group_check=True),
                      deps=[("G", Rsb_g)])
                    E("S", lambda h, ct=ct: h.activation(G1[:, ct, :], pw1[:, :], AF.Gelu,
                                                         bias=gb1b[:, ct:ct + 1]), deps=["T"])
                for blk in range(4):
                    for ct in range(8):
                        E("T", lambda h, ct=ct, blk=blk: h.matmul(
                            psml[:, 20 * blk:20 * blk + 20], G1[:, ct, 128 * blk:128 * blk + 128],
                            gW2b[:, ct, :], start=(ct == 0), stop=(ct == 7), skip_group_check=True),
                          deps=(["S", "V"] if ct == 0 else []))
                    E("V", lambda h, blk=blk: h.tensor_copy(lgstg[:, blk, :], psml[:, 20 * blk:20 * blk + 20]),
                      deps=["T", ("Y", p.count["Y"])])
                E("Y", lambda h, t=t: h.dma_start(
                    out=ds["lg_in"].rearrange("(t b p) c -> p t b c", p=128, b=4)[:, t, :, :],
                    in_=lgstg[:, :, :]), deps=["V"], inc=16)
            E("G", lambda h: h.collective_compute(
                "AllReduce", ALU.add, replica_groups=PAIRS,
                ins=[ds["lg_in"]], outs=[ds["lg_out"]]),
              deps=[("Y", p.count["Y"])], inc=0, cinc=1)
            cc_lg = p.ccount
            p.barrier()

        with contextlib.ExitStack() as es4:
            def sb4(n, s, d=BF16):
                return es4.enter_context(nc.sbuf_tensor(n, s, d))
            selmixb = sb4("selmixb", [20, 10, 128])
            lgld = sb4("lgld", [128, 2, 4, 20], F32)
            prw = sb4("prw", [128, 4, 20], F32)
            prmax = sb4("prmax", [128, 4, 4], F32)
            prb16 = sb4("prb16", [128, 20])
            probsT = sb4("probsT", [20, 512])
            pth = {}
            for nm in ("v", "od", "fs", "fm", "fl"):
                pth[nm] = sb4("p_" + nm, [128, 2, 4, 512])
            oacc = sb4("oacc", [128, 2, 512], F32)
            otmp = sb4("otmp", [128, 512], F32)
            sqo = sb4("sqo", [128, 2, 512])
            orow = sb4("orow", [1, 2, 512], F32)
            orb16 = sb4("orb16", [1, 512])
            obuf = sb4("obuf", [128, 4, 512])
            nwcb2 = sb4("nwcb2", [128, 2], F32)
            E("G", lambda h: h.dma_start(out=selmixb[:], in_=hb["selmix"].rearrange("s k m -> k s m")), inc=16)
            E("Y", lambda h: h.dma_start(out=nwcb2[:], in_=hb["nwc"][:]), inc=16)
            sel_g = p.count["G"]
            nw_y = p.count["Y"]
            for t in range(NT):
                d = t % 2
                sl = slice(TILE * t, TILE * (t + 1))
                E("Y", lambda h, d=d, t=t: h.dma_start(
                    out=lgld[:, d, :, :],
                    in_=ds["lg_out"].rearrange("(t b p) c -> p t b c", p=128, b=4)[:, t, :, :]),
                  deps=[("C", cc_lg), ("V", p.count["V"]), ("S", p.count["S"])], inc=16)
                for nm in ("v", "od", "fs", "fm", "fl"):
                    E("Y", lambda h, nm=nm, d=d, sl=sl: h.dma_start(
                        out=pth[nm][:, d, :, :],
                        in_=ds[(nm if nm in ("fs", "fm", "fl") else nm) + "_sp"].rearrange(
                            "(a p) l -> p a l", p=128)[:, :, sl]),
                      deps=[("T", p.count["T"]), ("V", p.count["V"])], inc=16)
                y_ld = p.count["Y"]
                for blk in range(4):
                    E("V", lambda h, d=d, blk=blk: h.tensor_add(prw[:, blk, :], lgld[:, d, blk, :], gb2bc[:, :]),
                      deps=[("Y", y_ld)])
                    E("V", lambda h, blk=blk: h.tensor_mul(prw[:, blk, :], prw[:, blk, :], itaubc[:, :]))
                    E("V", lambda h, blk=blk: h.reduce_max(
                        prmax[:, blk, :].rearrange("p (a b) -> p a b", b=1),
                        prw[:, blk, :].rearrange("p (a b) -> p a b", b=5), axis=AX))
                for blk in range(4):
                    for hh4 in range(4):
                        E("V", lambda h, blk=blk, hh4=hh4: h.tensor_scalar(
                            prw[:, blk, 5 * hh4:5 * hh4 + 5], prw[:, blk, 5 * hh4:5 * hh4 + 5],
                            prmax[:, blk, hh4:hh4 + 1], None, ALU.subtract))
                E("S", lambda h: h.activation(prw[:, :, :], prw[:, :, :], AF.Exp), deps=["V"])
                for blk in range(4):
                    E("V", lambda h, blk=blk: h.reduce_sum(
                        prmax[:, blk, :].rearrange("p (a b) -> p a b", b=1),
                        prw[:, blk, :].rearrange("p (a b) -> p a b", b=5), axis=AX), deps=["S"])
                    E("V", lambda h, blk=blk: h.reciprocal(prmax[:, blk, :], prmax[:, blk, :]))
                    for hh4 in range(4):
                        E("V", lambda h, blk=blk, hh4=hh4: h.tensor_scalar(
                            prw[:, blk, 5 * hh4:5 * hh4 + 5], prw[:, blk, 5 * hh4:5 * hh4 + 5],
                            prmax[:, blk, hh4:hh4 + 1], None, ALU.mult))
                    E("V", lambda h, blk=blk: h.tensor_mul(prw[:, blk, :], prw[:, blk, :], facbc[:, :]))
                    E("V", lambda h, blk=blk: h.tensor_add(prw[:, blk, :], prw[:, blk, :], epsbc[:, :]))
                    E("V", lambda h, blk=blk: h.tensor_copy(prb16[:, :], prw[:, blk, :]), deps=["T"])
                    E("T", lambda h: h.transpose(ptr[0:20, 0:128], prb16[:, :], idb[:, :]), deps=["V"])
                    E("V", lambda h, blk=blk: h.tensor_copy(probsT[:, 128 * blk:128 * blk + 128], ptr[0:20, 0:128]),
                      deps=["T"])
                for hh in range(2):
                    for pi, nm in enumerate(("fs", "fm", "fl", "od", "v")):
                        E("T", lambda h, hh=hh, pi=pi: h.matmul(
                            pw2[:, :], selmixb[:, 5 * hh + pi, :],
                            probsT[:, :], start=True, stop=True), deps=["V"])
                        for ci in range(2):
                            ct = 2 * hh + ci
                            if pi == 0:
                                E("V", lambda h, nm=nm, d=d, ct=ct, ci=ci: h.tensor_mul(
                                    oacc[:, ci, :], pth[nm][:, d, ct, :], pw2[:, :]), deps=["T"])
                            else:
                                E("V", lambda h, nm=nm, d=d, ct=ct: h.tensor_mul(
                                    otmp[:, :], pth[nm][:, d, ct, :], pw2[:, :]), deps=["T"])
                                E("V", lambda h, ci=ci: h.tensor_add(oacc[:, ci, :], oacc[:, ci, :], otmp[:, :]))
                    for ci in range(2):
                        E("S", lambda h, ci=ci: h.activation(sqo[:, ci, :], oacc[:, ci, :], AF.Square),
                          deps=["V"])
                        E("T", lambda h, ci=ci: h.matmul(psml[0:1, 0:512], onescb[:], sqo[:, ci, :],
                                                         start=(ci == 0), stop=(ci == 1),
                                                         skip_group_check=True), deps=["S"])
                    E("V", lambda h: h.tensor_scalar(orow[0:1, 0, :], psml[0:1, 0:512], 1.0 / 256, 1e-5,
                                                     ALU.mult, ALU.add), deps=["T"])
                    E("S", lambda h: h.activation(orow[0:1, 0, :], orow[0:1, 0, :], AF.Sqrt), deps=["V"])
                    E("V", lambda h: h.reciprocal(orow[0:1, 0, :], orow[0:1, 0, :]), deps=["S"])
                    E("V", lambda h: h.tensor_copy(orb16[0:1, :], orow[0:1, 0, :]), deps=["T"])
                    E("T", lambda h: h.matmul(pw2[:, :], onesrb[:], orb16[0:1, :], start=True, stop=True),
                      deps=["V"])
                    for ci in range(2):
                        ct = 2 * hh + ci
                        E("V", lambda h, ci=ci: h.tensor_mul(otmp[:, :], oacc[:, ci, :], pw2[:, :]),
                          deps=["T"])
                        E("V", lambda h, ct=ct, ci=ci: h.tensor_scalar(
                            obuf[:, ct, :], otmp[:, :], nwcb2[:, ci:ci + 1], None, ALU.mult),
                          deps=[("Y", p.count["Y"])])
                half = t // 8
                if DEBUG:
                    E("Y", lambda h, t=t: h.dma_start(
                        out=ds["probs_dbg"].rearrange("(t b p) c -> p t b c", p=128, b=4)[:, t, :, :],
                        in_=prw[:, :, :]), deps=["V"], inc=16)
                E("Y", lambda h, t=t, half=half: h.dma_start(
                    out=ds["o_in%d" % half].rearrange("(a p) l -> p a l", p=128)[
                        :, :, TILE * (t % 8):TILE * (t % 8) + 512],
                    in_=obuf[:, :, :]), deps=["V"], inc=16)
                if t == 7:
                    E("G", lambda h: h.collective_compute(
                        "AllGather", ALU.bypass, replica_groups=PAIRS,
                        ins=[ds["o_in0"]], outs=[ds["o_out0"]]),
                      deps=[("Y", p.count["Y"])], inc=0, cinc=1)
            E("G", lambda h: h.collective_compute(
                "AllGather", ALU.bypass, replica_groups=PAIRS,
                ins=[ds["o_in1"]], outs=[ds["o_out1"]]),
              deps=[("Y", p.count["Y"])], inc=0, cinc=1)
            cc_o = p.ccount
            p.barrier()

        with contextlib.ExitStack() as es5:
            def sb5(n, s, d=BF16):
                return es5.enter_context(nc.sbuf_tensor(n, s, d))
            Wob = sb5("Wob", [128, 8, 512])
            o_ld = sb5("o_ld", [128, 2, 8, 512])
            grs = sb5("grs", [128, 2, 4, 512], F32)
            gt = sb5("gt", [128, 4, 512])
            xrt = sb5("xrt", [128, 2, 4, 512], F32)
            sgt = sb5("sgt", [128, 512], F32)
            outsb = sb5("outsb", [128, 4, 512], F32)
            E("G", lambda h: h.dma_start(out=Wob[:], in_=hb["Wo"].rearrange("(a p) c -> p a c", p=128)), inc=16)
            wo_g = p.count["G"]
            xrr = hb["xres"].rearrange("(a p) l -> p a l", p=128)
            outr = out_h.rearrange("(a p) l -> p a l", p=128)
            for t in range(NT):
                d = t % 2
                half = t // 8
                oo = ds["o_out%d" % half].rearrange("(a p) l -> p a l", p=128)
                E("Y", lambda h, d=d, t=t, oo=oo, half=half: h.dma_start(
                    out=o_ld[:, d, :, :], in_=oo[:, :, TILE * (t % 8):TILE * (t % 8) + 512]),
                  deps=[("C", cc_o - (1 if half == 0 else 0)), ("T", p.count["T"])], inc=16)
                E("Y", lambda h, d=d, t=t: h.dma_start(
                    out=grs[:, d, :, :],
                    in_=ds["moe_rs"][t // 4].rearrange("(a p) l -> p a l", p=128)[
                        :, :, 512 * (t % 4):512 * (t % 4) + 512]),
                  deps=[("C", cc_moe), ("S", p.count["S"])], inc=16)
                E("Y", lambda h, d=d, sl=slice(TILE * t, TILE * t + 512): h.dma_start(
                    out=xrt[:, d, :, :], in_=xrr[:, :, sl]),
                  deps=[("V", p.count["V"])], inc=16)
                y_ld3 = p.count["Y"]
                for a4 in range(4):
                    E("S", lambda h, d=d, a4=a4: h.activation(gt[:, a4, :], grs[:, d, a4, :], AF.Silu),
                      deps=[("Y", y_ld3), ("V", p.count["V"])])
                for oc in range(4):
                    for kt in range(8):
                        E("T", lambda h, d=d, oc=oc, kt=kt: h.matmul(
                            pw1[:, :], Wob[:, kt, 128 * oc:128 * oc + 128], o_ld[:, d, kt, :],
                            start=(kt == 0), stop=(kt == 7)),
                          deps=(["V", ("G", wo_g), ("Y", y_ld3)] if kt == 0 else []))
                    E("V", lambda h, oc=oc: h.tensor_mul(sgt[:, :], gt[:, oc, :], pw1[:, :]),
                      deps=["T", "S"])
                    E("V", lambda h, d=d, oc=oc: h.tensor_add(outsb[:, oc, :], xrt[:, d, oc, :], sgt[:, :]))
                E("Y", lambda h, t=t: h.dma_start(
                    out=outr[:, :, TILE * t:TILE * t + 512], in_=outsb[:, :, :]),
                  deps=["V"], inc=16)
            if DEBUG:
                for buf, nm2 in ((gb2bc, "gb2bc"), (itaubc, "itaubc"), (facbc, "facbc"), (epsbc, "epsbc")):
                    dbg2 = nc.dram_tensor("dbg_" + nm2, [128, 20], F32, kind="ExternalOutput").ap()
                    E("Y", lambda h, dbg2=dbg2, buf=buf: h.dma_start(out=dbg2[:], in_=buf[:]),
                      deps=[("Y", p.count["Y"])], inc=16)
                dbgr = nc.dram_tensor("dbg_row20", [1, 4, 20], F32, kind="ExternalOutput").ap()
                E("Y", lambda h, dbgr=dbgr: h.dma_start(out=dbgr[:], in_=row20[:, :, :]), inc=16)
                dbgs = nc.dram_tensor("dbg_rowsml", [1, 32], F32, kind="ExternalOutput").ap()
                E("Y", lambda h, dbgs=dbgs: h.dma_start(out=dbgs[:], in_=rowsml[:, :]), inc=16)
                for nm, shp in (("xn_sp", [1024, L]), ("v_sp", [512, L]), ("od_sp", [512, L]),
                                ("fs_sp", [512, L]), ("fm_sp", [512, L]), ("fl_sp", [512, L]),
                                ("stats_out", [80, L]), ("o_out0", [1024, L // 2]),
                                ("o_out1", [1024, L // 2])):
                    dbg = nc.dram_tensor("dbg_" + nm, shp, F32, kind="ExternalOutput").ap()
                    E("G", lambda h, dbg=dbg, nm=nm: h.dma_start(out=dbg[:], in_=ds[nm][:]),
                      deps=[("Y", p.count["Y"]), ("C", p.ccount)], inc=16)
                dbg = nc.dram_tensor("dbg_lg", [L, 20], F32, kind="ExternalOutput").ap()
                E("G", lambda h, dbg=dbg: h.dma_start(out=dbg[:], in_=ds["lg_out"][:]), inc=16)
                dbg = nc.dram_tensor("dbg_probs", [L, 20], F32, kind="ExternalOutput").ap()
                E("G", lambda h, dbg=dbg: h.dma_start(out=dbg[:], in_=ds["probs_dbg"][:]), inc=16)
                dbg = nc.dram_tensor("dbg_rl", [L, 8], F32, kind="ExternalOutput").ap()
                E("G", lambda h, dbg=dbg: h.dma_start(out=dbg[:], in_=ds["rl_sp"][:]), inc=16)
                for lq in range(4):
                    dbg = nc.dram_tensor("dbg_moe%d" % lq, [512, 2048], F32, kind="ExternalOutput").ap()
                    E("G", lambda h, dbg=dbg, lq=lq: h.dma_start(out=dbg[:], in_=ds["moe_rs"][lq]), inc=16)
            E("G", lambda h: h.nop(), deps=[("Y", p.count["Y"]), ("G", p.count["G"])])

        p.finalize(block)
    return nc



def kernel(**inputs):
    maps = host_prep(inputs)
    nc = build_nc()
    res = run_bass_kernel_spmd(nc, maps, core_ids=list(range(8)))
    out = np.zeros((B, L, D), np.float32)
    for c in range(8):
        b, j = c // 2, c % 2
        out[b, :, 512 * j:512 * (j + 1)] = res.results[c]["out"].T
    return out


# version 24
# speedup vs baseline: 1.0021x; 1.0021x over previous
import contextlib
import numpy as np
import ml_dtypes
import concourse.bass as bass
import concourse.mybir as mybir
from concourse.bass_utils import run_bass_kernel_spmd

F32 = mybir.dt.float32
BF16 = mybir.dt.bfloat16
AF = mybir.ActivationFunctionType
ALU = mybir.AluOpType
AX = mybir.AxisListType.X
DEBUG = False

B, L, D, H = 4, 8192, 1024, 4
DV = 256
P5 = 5
TILE = 512
NT = L // TILE
WIN = 1024
NW = L // WIN
PAIRS = [[0, 1], [2, 3], [4, 5], [6, 7]]
QUADS = [[0, 2, 4, 6], [1, 3, 5, 7]]
FIRS = (("fs", 3), ("fm", 15), ("fl", 64))
ENGINES = ("T", "V", "S", "G", "Y")



def stats_row(j, p, h_loc, kind):
    return j * 40 + (p * 2 + h_loc) * 4 + kind


def build_selR(b, j):
    sel = np.zeros((40, 128, 80), np.float32)
    for t in range(40):
        gg = b * 20 + t // 2
        p = gg // 16
        r = gg % 16
        h_glob, kind = r // 4, r % 4
        sel[t, :, stats_row(h_glob // 2, p, h_glob % 2, kind)] = 1.0
    return sel


def build_fir_ST(w_slice, K):
    NG_ = (K + 2) // 4 + 1
    ST = np.zeros((NG_, 128, 128), np.float32)
    for G in range(NG_):
        for r in range(4):
            for d in range(4):
                jt = 4 * G + r - d
                if 0 <= jt <= K - 1:
                    ST[G, 32 * r:32 * r + 32, 32 * d:32 * d + 32][
                        np.arange(32), np.arange(32)] = w_slice[:, jt]
    return ST


def build_selstat2():
    s = np.zeros((30, 128, 96), np.float32)
    for r in range(30):
        s[r, :, 32 * (r // 10) + r % 10] = 1.0
    return s


def build_selrow8e(j):
    s = np.zeros((4, 8, 128), np.float32)
    for e in range(4):
        s[e, 4 * j + e, :] = 1.0
    return s


def build_selmix(j):
    sm = np.zeros((10, 20, 128), np.float32)
    for hh in range(2):
        for pi in range(5):
            sm[hh * 5 + pi, (2 * j + hh) * 5 + pi, :] = 1.0
    return sm


def build_selstat():
    combos = [(p, kd) for p in range(P5) for kd in range(3)]
    sel = np.zeros((2, 15, 128, 40), np.float32)
    for h in range(2):
        for ci, (p, kd) in enumerate(combos):
            base = (p * 2 + h) * 4
            if kd == 0:
                sel[h, ci, :, base + 0] = 1.0
            elif kd == 1:
                sel[h, ci, :, base + 1] = 1.0
                sel[h, ci, :, base + 3] = 1.0
            else:
                sel[h, ci, :, base + 2] = 1.0
    return sel


def host_prep(inp):
    x = np.asarray(inp["L0_attn_hidden_states"], np.float32)
    gW1 = np.asarray(inp["L0_attn_gW1"], np.float32)
    masks = {}
    mlow = np.zeros((128, 128), np.float32)
    mup = np.zeros((128, 128), np.float32)
    minc = np.zeros((128, 128), np.float32)
    for r in range(4):
        s = slice(32 * r, 32 * r + 32)
        mlow[s, s] = -np.tril(np.ones((32, 32), np.float32), -1)
        mup[s, s] = -np.triu(np.ones((32, 32), np.float32), 1)
        minc[s, s] = np.triu(np.ones((32, 32), np.float32), 0)
    ident = np.eye(128, dtype=np.float32)
    onesc = np.ones((128, 1), np.float32)
    onesr = np.ones((1, 128), np.float32)
    ones5 = np.ones((1, 5), np.float32)
    zrow = np.zeros((1, 512), np.float32)

    maps = []
    for c in range(8):
        b, j = c // 2, c % 2
        hsl = slice(512 * j, 512 * (j + 1))
        csl = slice(1024 * j, 1024 * (j + 1))
        d = {
            "x": np.ascontiguousarray(x[b].T),
            "xres": np.ascontiguousarray(x[b].T[hsl]),
            "Wq": np.ascontiguousarray(inp["L0_attn_Wq"][:, hsl]),
            "Wk": np.ascontiguousarray(inp["L0_attn_Wk"][:, hsl]),
            "Wv": np.ascontiguousarray(inp["L0_attn_Wv"][:, hsl]),
            "Wb2": np.ascontiguousarray(inp["L0_attn_Wb"][:, 2 * j:2 * j + 2]),
            "gW1a": np.ascontiguousarray(gW1[0:1024, csl]),
            "gW1r": np.ascontiguousarray(gW1[1024 + 5120 * b:1024 + 5120 * (b + 1), csl]),
            "gb1c": np.ascontiguousarray(np.asarray(inp["L0_attn_gb1"], np.float32)[csl].reshape(8, 128).T),
            "gW2": np.ascontiguousarray(np.asarray(inp["L0_attn_gW2"], np.float32)[csl]),
            "gb2r": np.asarray(inp["L0_attn_gb2"], np.float32).reshape(1, 20),
            "ltau": np.asarray(inp["L0_attn_log_tau"], np.float32).reshape(1, 4),
            "epsr": np.asarray(inp["L0_attn_eps_logit"], np.float32).reshape(1, 20),
            "nwc": np.ascontiguousarray(np.asarray(inp["L0_attn_norm_w"], np.float32).reshape(2, 128).T),
            "Wo": np.ascontiguousarray(inp["L0_attn_Wo"][:, hsl]),
            "rwT": np.ascontiguousarray(np.asarray(inp["L0_ffn_router_w"], np.float32).T),
            "up": np.ascontiguousarray(inp["L0_ffn_up_proj"][4 * j:4 * j + 4]),
            "down": np.ascontiguousarray(inp["L0_ffn_down_proj"][4 * j:4 * j + 4]),
            "selR": build_selR(b, j),
            "selmix": build_selmix(j),
            "selstat": build_selstat(),
            "selrow8e": build_selrow8e(j),
            "ident": ident, "onesc": onesc, "onesr": onesr,
            "biasrow": np.array([[1e-6, 1e-5]], np.float32),
            "ones5": ones5, "zrow": zrow,
            "mlow": mlow, "mup": mup, "minc": minc,
        }
        cst = np.zeros((3, 4, 4, 128, 128), np.float32)
        for ai, nm in enumerate(("conv_q", "conv_k", "conv_v")):
            cw = np.asarray(inp["L0_attn_" + nm], np.float32)[hsl]
            for jt in range(4):
                for ct in range(4):
                    cst[ai, jt, ct][np.arange(128), np.arange(128)] = cw[128 * ct:128 * ct + 128, jt]
        d["convST"] = cst
        stf = np.zeros((16, 24, 128, 128), np.float32)
        for g in range(16):
            off = 0
            for nm, K in (("fir_s", 3), ("fir_m", 15), ("fir_l", 64)):
                w = np.asarray(inp["L0_attn_" + nm], np.float32)[hsl][32 * g:32 * g + 32]
                st = build_fir_ST(w, K)
                stf[g, off:off + st.shape[0]] = st
                off += st.shape[0]
        d["STf"] = stf
        maps.append(d)
    return maps



class Prog:
    def __init__(self, nc, es):
        self.nc = nc
        self.sems = {e: es.enter_context(nc.semaphore("sem" + e)) for e in ENGINES}
        self.sems["D"] = es.enter_context(nc.semaphore("semD"))
        self.csem = es.enter_context(nc.semaphore("semC"))
        self.count = {e: 0 for e in ENGINES}
        self.count["D"] = 0
        self.ccount = 0
        self.last_wait = {}
        self.prog = {e: [] for e in ENGINES}

    def op(self, eng, fn, deps=(), inc=1, cinc=0, dinc=0):
        if eng in ("V", "S") and not dinc:
            inc = max(inc, 1)
            if self.count[eng] > 0:
                deps = list(deps) + [(eng, self.count[eng])]
        waits = []
        for d in deps:
            if isinstance(d, tuple):
                de, dv = d
            else:
                de, dv = d, (self.ccount if d == "C" else self.count[d])
            if dv <= 0:
                continue
            key = (eng, de)
            if self.last_wait.get(key, -1) >= dv:
                continue
            self.last_wait[key] = dv
            waits.append((self.csem if de == "C" else self.sems[de], dv))
        if dinc:
            self.count["D"] += dinc
            sem = self.sems["D"]
            niv = dinc
        else:
            self.count[eng] += inc
            if cinc:
                self.ccount += cinc
            sem = self.csem if cinc else self.sems[eng]
            niv = cinc if cinc else inc

        def emit(h, waits=waits, fn=fn, sem=sem, niv=niv):
            for s, v in waits:
                h.wait_ge(s, v)
            ins = fn(h)
            if niv:
                ins.then_inc(sem, niv)
        self.prog[eng].append(emit)
        return self.count["D"] if dinc else (self.ccount if cinc else self.count[eng])

    def barrier(self):
        snap = dict(self.count)
        for e in ENGINES:
            deps = [(d, snap[d]) for d in ENGINES if d != e and snap[d] > 0]
            self.op(e, lambda h: h.nop(), deps=deps)

    def finalize(self, block):
        for ename, attr in (("G", "gpsimd"), ("S", "scalar"), ("T", "tensor"),
                            ("V", "vector"), ("Y", "sync")):
            plist = self.prog[ename]

            def runner(h, plist=plist):
                for f in plist:
                    f(h)
            getattr(block, attr)(runner)



def build_nc():
    nc = bass.Bass("TRN2", target_bir_lowering=False)
    hb = {}
    for name, shape, dt in (
        ("x", [1024, L], F32), ("xres", [512, L], F32),
        ("Wq", [1024, 512], F32), ("Wk", [1024, 512], F32), ("Wv", [1024, 512], F32),
        ("Wb2", [1024, 2], F32), ("gW1a", [1024, 1024], F32),
        ("gW1r", [5120, 1024], F32), ("gb1c", [128, 8], F32),
        ("gW2", [1024, 20], F32), ("gb2r", [1, 20], F32),
        ("ltau", [1, 4], F32), ("epsr", [1, 20], F32), ("nwc", [128, 2], F32),
        ("Wo", [1024, 512], F32), ("rwT", [1024, 8], F32),
        ("up", [4, 1024, 2048], F32), ("down", [4, 1024, 1024], F32),
        ("selR", [40, 128, 80], F32),
        ("selstat", [2, 15, 128, 40], F32),
        ("selrow8e", [4, 8, 128], F32),
        ("selmix", [10, 20, 128], F32),
        ("ident", [128, 128], F32), ("onesc", [128, 1], F32), ("biasrow", [1, 2], F32),
        ("onesr", [1, 128], F32), ("ones5", [1, 5], F32), ("zrow", [1, 512], F32),
        ("mlow", [128, 128], F32), ("mup", [128, 128], F32), ("minc", [128, 128], F32),
        ("convST", [3, 4, 4, 128, 128], F32), ("STf", [16, 24, 128, 128], F32),
    ):
        hb[name] = nc.dram_tensor(name, shape, dt, kind="ExternalInput").ap()
    out_h = nc.dram_tensor("out", [512, L], F32, kind="ExternalOutput").ap()
    ds = {}
    for name, shape, dt in (
        ("xn_sp", [1024, L], BF16), ("v_sp", [512, L], BF16),
        ("od_sp", [512, L], BF16),
        ("fs_sp", [512, L], BF16), ("fm_sp", [512, L], BF16), ("fl_sp", [512, L], BF16),
        ("stats_in", [40, L], BF16), ("stats_out", [80, L], BF16),
        ("R_in", [80, 1024], F32), ("R_out", [80, 1024], F32),
        ("lg_in", [L, 20], F32), ("lg_out", [L, 20], F32),
        ("o_in0", [512, L // 2], BF16), ("o_in1", [512, L // 2], BF16),
        ("o_out0", [1024, L // 2], BF16), ("o_out1", [1024, L // 2], BF16),
        ("moe_sp", [4, 1024, 2048], F32), ("moe_rs", [4, 512, 2048], F32),
        ("probs_dbg", [L, 20], F32), ("rl_sp", [L, 8], F32),
    ):
        ds[name] = nc.dram_tensor(name, shape, dt, kind="Internal").ap()

    es = contextlib.ExitStack()
    with es:
        def sb(n, s, d=BF16):
            return es.enter_context(nc.sbuf_tensor(n, s, d))

        def psum(n, s, d=F32):
            return es.enter_context(nc.psum_tensor(n, s, d))

        p = Prog(nc, es)
        E = p.op

        idb = sb("idb", [128, 128])
        onescb = sb("onescb", [128, 1])
        onesrb = sb("onesrb", [1, 128])
        zrowb = sb("zrowb", [1, 512])
        biasr = sb("biasr", [1, 2], F32)
        mlowb = sb("mlowb", [128, 128])
        mupb = sb("mupb", [128, 128])
        mincb = sb("mincb", [128, 128])
        Rsb = sb("Rsb", [80, 1024])
        gb2bc = sb("gb2bc", [128, 20], F32)
        itaubc = sb("itaubc", [128, 20], F32)
        facbc = sb("facbc", [128, 20], F32)
        epsbc = sb("epsbc", [128, 20], F32)
        row20 = sb("row20", [1, 4, 20], F32)
        rowsml = sb("rowsml", [1, 32], F32)
        rowb16 = sb("rowb16", [1, 4, 20])
        ones5v = sb("ones5v", [1, 5], F32)

        pbig = psum("pbig", [128, 512])
        ptr = psum("ptr", [128, 512], BF16)
        pw1 = psum("pw1", [128, 512])
        pw2 = psum("pw2", [128, 512])
        pw3 = psum("pw3", [128, 512])
        psml = psum("psml", [128, 512])

        block = es.enter_context(nc.Block())

        E("Y", lambda h: h.dma_start(out=biasr[:], in_=hb["biasrow"][:]), inc=16)
        for buf, src in ((idb, hb["ident"]), (onescb, hb["onesc"]), (onesrb, hb["onesr"]),
                         (zrowb, hb["zrow"]), (mlowb, hb["mlow"]), (mupb, hb["mup"]),
                         (mincb, hb["minc"])):
            E("G", lambda h, b=buf, s=src: h.dma_start(out=b[:], in_=s[:]), inc=16)
        E("Y", lambda h: h.dma_start(out=row20[0:1, 0, :], in_=hb["gb2r"][:]), inc=16)
        E("Y", lambda h: h.dma_start(out=rowsml[0:1, 0:4], in_=hb["ltau"][:]), inc=16)
        E("Y", lambda h: h.dma_start(out=rowsml[0:1, 4:24], in_=hb["epsr"][:]), inc=16)
        E("Y", lambda h: h.dma_start(out=ones5v[:], in_=hb["ones5"][:]), inc=16)
        consts_g = p.count["G"]
        consts_y = p.count["Y"]

        E("S", lambda h: h.activation(rowsml[0:1, 24:28], rowsml[0:1, 0:4], AF.Abs),
          deps=[("Y", consts_y)])
        E("S", lambda h: h.activation(rowsml[0:1, 24:28], rowsml[0:1, 24:28], AF.Exp, scale=-1.0))
        E("S", lambda h: h.activation(rowsml[0:1, 24:28], rowsml[0:1, 24:28], AF.Ln, bias=1.0))
        E("S", lambda h: h.activation(rowsml[0:1, 28:32], rowsml[0:1, 0:4], AF.Relu))
        E("V", lambda h: h.tensor_add(rowsml[0:1, 24:28], rowsml[0:1, 24:28], rowsml[0:1, 28:32]), deps=["S"])
        E("V", lambda h: h.tensor_scalar(rowsml[0:1, 24:28], rowsml[0:1, 24:28], 0.5, None, ALU.add))
        E("V", lambda h: h.reciprocal(rowsml[0:1, 24:28], rowsml[0:1, 24:28]))
        E("S", lambda h: h.activation(row20[0:1, 3, :], rowsml[0:1, 4:24], AF.Sigmoid,), deps=["V"])
        E("S", lambda h: h.activation(row20[0:1, 3, :], row20[0:1, 3, :], AF.Identity, scale=0.2))
        E("V", lambda h: h.reduce_sum(rowsml[0:1, 28:32], row20[0:1, 3, :].rearrange("o (a b) -> o a b", b=5),
                                      axis=AX), deps=["S"])
        E("V", lambda h: h.tensor_scalar(rowsml[0:1, 28:32], rowsml[0:1, 28:32], -1.0, 1.0, ALU.mult, ALU.add))
        for hh in range(4):
            E("V", lambda h, hh=hh: h.tensor_scalar(
                row20[0:1, 1, 5 * hh:5 * hh + 5], ones5v[0:1, :], rowsml[0:1, 24 + hh:25 + hh],
                None, ALU.mult))
            E("V", lambda h, hh=hh: h.tensor_scalar(
                row20[0:1, 2, 5 * hh:5 * hh + 5], ones5v[0:1, :], rowsml[0:1, 28 + hh:29 + hh],
                None, ALU.mult))
        E("V", lambda h: h.tensor_copy(rowb16[:], row20[:]))
        rows_ready = p.count["V"]
        for i, dst in enumerate((gb2bc, itaubc, facbc, epsbc)):
            E("T", lambda h, i=i: h.matmul(psml[:, 0:20], onesrb[:], rowb16[0:1, i, :],
                                           start=True, stop=True),
              deps=[("V", rows_ready), ("G", consts_g), "V"])
            E("V", lambda h, dst=dst: h.tensor_copy(dst[:], psml[:, 0:20]), deps=["T"])

        selRb = sb("selRb", [128, 2, 80])
        g1rb = sb("g1rb", [128, 2, 1024])
        for t in range(40):
            d = t % 2
            E("G", lambda h, t=t, d=d: h.dma_start(out=selRb[:, d, :], in_=hb["selR"][t]),
              deps=[("T", p.count["T"])], inc=16)
            E("G", lambda h, t=t, d=d: h.dma_start(out=g1rb[:, d, :], in_=hb["gW1r"][128 * t:128 * t + 128, :]),
              inc=16)
            for half in range(2):
                E("T", lambda h, t=t, d=d, half=half: h.matmul(
                    (pw1 if half == 0 else pw2)[0:80, 0:512], selRb[:, d, :],
                    g1rb[:, d, 512 * half:512 * half + 512],
                    start=(t == 0), stop=(t == 39), skip_group_check=True), deps=["G"])
        Rq = sb("Rq", [80, 1024], F32)
        E("V", lambda h: h.tensor_copy(Rq[:, 0:512], pw1[0:80, :]), deps=["T"])
        E("V", lambda h: h.tensor_copy(Rq[:, 512:1024], pw2[0:80, :]))
        E("Y", lambda h: h.dma_start(out=ds["R_in"][:], in_=Rq[:]), deps=["V"], inc=16)
        E("G", lambda h: h.collective_compute(
            "AllReduce", ALU.add, replica_groups=QUADS,
            ins=[ds["R_in"]], outs=[ds["R_out"]]),
          deps=[("Y", p.count["Y"])], inc=0, cinc=1)
        rc_R = p.ccount
        E("G", lambda h: h.dma_start(out=Rsb[:], in_=ds["R_out"][:]), deps=[("C", rc_R)], inc=16)
        Rsb_g = p.count["G"]

        with contextlib.ExitStack() as es1:
            def sb1(n, s, d=BF16):
                return es1.enter_context(nc.sbuf_tensor(n, s, d))
            Sp = [es1.enter_context(nc.psum_tensor("Sp%d" % i, [128, 512], F32)) for i in range(2)]
            Wqb = sb1("Wqb", [128, 8, 512])
            Wkb = sb1("Wkb", [128, 8, 512])
            Wvb = sb1("Wvb", [128, 8, 512])
            Wbb = sb1("Wbb", [128, 8, 2])
            cstb = sb1("cstb", [128, 3, 4, 4, 128])
            rwTf = sb1("rwTf", [128, 8, 8], F32)
            rlstg = sb1("rlstg", [128, 4, 8], F32)
            E("G", lambda h: h.dma_start(out=Wqb[:], in_=hb["Wq"].rearrange("(a p) c -> p a c", p=128)), inc=16)
            E("G", lambda h: h.dma_start(out=Wkb[:], in_=hb["Wk"].rearrange("(a p) c -> p a c", p=128)), inc=16)
            E("G", lambda h: h.dma_start(out=Wvb[:], in_=hb["Wv"].rearrange("(a p) c -> p a c", p=128)), inc=16)
            E("G", lambda h: h.dma_start(out=Wbb[:], in_=hb["Wb2"].rearrange("(a p) c -> p a c", p=128)), inc=16)
            E("G", lambda h: h.dma_start(out=cstb[:], in_=hb["convST"].rearrange("a t c p m -> p a t c m")), inc=16)
            selstatb = sb1("selstatb", [128, 2, 15, 40])
            E("G", lambda h: h.dma_start(out=selstatb[:], in_=hb["selstat"].rearrange("a s p c -> p a s c")), inc=16)
            consts_g = p.count["G"]
            E("Y", lambda h: h.dma_start(out=rwTf[:], in_=hb["rwT"].rearrange("(a p) c -> p a c", p=128)), inc=16)
            xbf = sb1("xbf", [128, 1, 8, 512])
            sqx = sb1("sqx", [128, 2, 512])
            lnrow = sb1("lnrow", [1, 3, 512], F32)
            lnb16 = sb1("lnb16", [1, 2, 512])
            xn_sb = sb1("xn_sb", [128, 8, 512])
            tmpf = sb1("tmpf", [128, 2, 512], F32)
            rq = sb1("rq", [128, 4, 520])
            rk = sb1("rk", [128, 4, 520])
            rv = sb1("rv", [128, 4, 520])
            qkraw = sb1("qkraw", [128, 4, 512])
            qkraw2 = sb1("qkraw2", [128, 4, 512])
            qh_sb = sb1("qh_sb", [128, 4, 512])
            kh_sb = sb1("kh_sb", [128, 4, 512])
            bT = sb1("bT", [1, 1, 512], F32)
            bb16 = sb1("bb16", [1, 2, 512])
            bcol = sb1("bcol", [128, 2, 4], F32)
            rsqb = sb1("rsqb", [1, 1, 512])
            kbg = [sb1("kbg%d" % i, [128, 2, 128]) for i in range(2)]
            At = [sb1("At%d" % i, [128, 128]) for i in range(2)]
            AtT = [sb1("AtT%d" % i, [128, 128]) for i in range(2)]
            locT = [sb1("locT%d" % i, [128, 32]) for i in range(2)]
            sqA = [sb1("sqA%d" % i, [128, 3, 128]) for i in range(2)]
            sqAT = [sb1("sqAT%d" % i, [128, 4, 128]) for i in range(2)]
            X0 = [sb1("X0%d" % i, [128, 512]) for i in range(2)]
            X1 = [sb1("X1%d" % i, [128, 512]) for i in range(2)]
            wT = [sb1("wT%d" % i, [128, 2, 128]) for i in range(2)]
            kts = [sb1("kts%d" % i, [128, 256]) for i in range(2)]
            Sbf = sb1("Sbf", [128, 2, 512])
            u_sb = [sb1("u_sb%d" % i, [128, 256]) for i in range(2)]
            osg = [sb1("osg%d" % i, [128, 256]) for i in range(2)]
            vfir = sb1("vfir", [128, 4, 1096])
            vdup = sb1("vdup", [128, 2, 1092])
            STfb = sb1("STfb", [128, 3, 24, 128])
            fsb = {"fs": sb1("fssb", [128, 4, 1024]), "fm": sb1("fmsb", [128, 4, 1024]),
                   "fl": sb1("flsb", [128, 4, 1024])}
            od_win = sb1("od_win", [128, 4, 1024])
            kscr = sb1("kscr", [128, 6, 512])
            stk = sb1("stk", [128, 5, 512])
            stag40 = sb1("stag40", [128, 40])
            statstg = sb1("statstg", [40, 512])

            E("T", lambda h: h.matmul(Sp[0][:, :], onesrb[:], zrowb[:], start=True, stop=False),
              deps=[("G", consts_g)])
            E("T", lambda h: h.matmul(Sp[1][:, :], onesrb[:], zrowb[:], start=True, stop=False))
            for ring in (rq, rk, rv):
                E("V", lambda h, r=ring: h.memset(r[:, :, 0:8], 0.0))
            E("V", lambda h: h.memset(vfir[:, :, 0:64], 0.0))
            E("V", lambda h: h.memset(vfir[:, :, 1088:1096], 0.0))

            xr = hb["x"].rearrange("(a p) l -> p a l", p=128)
            fir_pending = []
            odspill_y = [0]
            stats_spill_y = [0]
            fsb_spill_y = [0]
            fir_guard = [0, 0]
            fir_guard = [0, 0]
            fir_staged = [0]
            fir_mm_done = {}
            fir_stage_cnt = {}
            fir_bank_free = {0: (0, 0), 1: (0, 0)}
            fir_nm_ctr = [0]

            def stage_fir(grp):
                gl = grp % 16
                gp2 = grp % 2
                sl3 = grp % 3
                ct = gl // 4
                rw = 32 * (gl % 4)
                E("G", lambda h, gl=gl, sl3=sl3: h.dma_start(
                    out=STfb[:, sl3, :, :], in_=hb["STf"][gl].rearrange("G p m -> p G m")),
                  deps=[("T", fir_mm_done.get(grp - 3, 0))], inc=16)
                tdep = ("T", fir_mm_done.get(grp - 2, 0))
                for r, eng in enumerate(("V", "S", "V", "S")):
                    if eng == "V":
                        E("V", lambda h, r=r, rw=rw, ct=ct, gp2=gp2: h.tensor_copy(
                            vdup[32 * r:32 * r + 32, gp2, :], vfir[rw:rw + 32, ct, 1 + r:1093 + r]),
                          deps=[tdep, ("S", p.count["S"])])
                    else:
                        E("S", lambda h, r=r, rw=rw, ct=ct, gp2=gp2: h.activation(
                            vdup[32 * r:32 * r + 32, gp2, :], vfir[rw:rw + 32, ct, 1 + r:1093 + r],
                            AF.Identity), deps=[tdep])
                fir_stage_cnt[grp] = (p.count["G"], p.count["V"], p.count["S"])

            def mm_fir(grp):
                lim = (grp // 16) * 16 + 16
                while fir_staged[0] < lim and fir_staged[0] <= grp + 1:
                    stage_fir(fir_staged[0])
                    fir_staged[0] += 1
                gp2 = grp % 2
                sl3 = grp % 3
                ct = (grp % 16) // 4
                rw = 32 * ((grp % 16) % 4)
                sg, sv, ss = fir_stage_cnt[grp]
                stoff = 0
                for nm, K in FIRS:
                    bi = fir_nm_ctr[0] % 2
                    fir_nm_ctr[0] += 1
                    bank = (pw1, pbig)[bi]
                    bv, bs = fir_bank_free[bi]
                    NG_ = (K + 2) // 4 + 1
                    for G in range(NG_):
                        off = 4 * G + (64 - K)
                        E("T", lambda h, gp2=gp2, bank=bank, so=stoff + G, off=off,
                          s0=(G == 0), s1=(G == NG_ - 1): h.matmul(
                            bank[:, 0:256], STfb[:, sl3, so, :],
                            vdup[:, gp2, off:off + 1024:4],
                            start=s0, stop=s1, skip_group_check=True),
                          deps=([("V", max(sv, bv, fir_guard[0])),
                                 ("S", max(ss, bs, fir_guard[1])), ("G", sg)] if G == 0 else []))
                    stoff += NG_
                    for dd in range(4):
                        if dd % 2 == 0:
                            E("V", lambda h, nm=nm, bank=bank, dd=dd, rw=rw, ct=ct: h.tensor_copy(
                                fsb[nm][rw:rw + 32, ct, dd:1024:4], bank[32 * dd:32 * dd + 32, 0:256]),
                              deps=["T", ("Y", fsb_spill_y[0])])
                        else:
                            E("S", lambda h, nm=nm, bank=bank, dd=dd, rw=rw, ct=ct: h.activation(
                                fsb[nm][rw:rw + 32, ct, dd:1024:4], bank[32 * dd:32 * dd + 32, 0:256],
                                AF.Identity), deps=["T", ("Y", fsb_spill_y[0])])
                    fir_bank_free[bi] = (p.count["V"], p.count["S"])
                fir_mm_done[grp] = p.count["T"]

            for t in range(NT):
                d = t % 2
                sl = slice(TILE * t, TILE * (t + 1))
                g_x = E("G", lambda h, sl=sl: h.dma_start(out=xbf[:, 0, :, :], in_=xr[:, :, sl]),
                        deps=[("S", p.count["S"]), ("T", p.count["T"]), ("V", p.count["V"])], inc=16)
                E("T", lambda h: h.matmul(psml[:, :], onesrb[:], zrowb[:], start=True, stop=False),
                  deps=["V", "S"])
                for dt in range(8):
                    if dt % 2 == 0:
                        E("S", lambda h, dt=dt: h.activation(sqx[:, 0, :], xbf[:, 0, dt, :], AF.Square),
                          deps=[("G", g_x), ("T", p.count["T"])])
                    else:
                        E("V", lambda h, dt=dt: h.tensor_mul(sqx[:, 1, :], xbf[:, 0, dt, :],
                                                             xbf[:, 0, dt, :]),
                          deps=[("G", g_x), ("T", p.count["T"])])
                    E("T", lambda h, dt=dt: h.matmul(psml[0:1, 0:512], onescb[:], sqx[:, dt % 2, :],
                                                     start=False, stop=False, skip_group_check=True),
                      deps=["S" if dt % 2 == 0 else "V"])
                    E("T", lambda h, dt=dt: h.matmul(psml[32:33, 0:512], onescb[:], xbf[:, 0, dt, :],
                                                          start=False, stop=False, skip_group_check=True))
                E("V", lambda h: h.tensor_scalar(lnrow[0:1, 0, :], psml[32:33, 0:512], 1.0 / 1024, None, ALU.mult),
                  deps=["T"])
                E("V", lambda h: h.tensor_scalar(lnrow[0:1, 1, :], psml[0:1, 0:512], 1.0 / 1024, None, ALU.mult))
                E("S", lambda h: h.activation(lnrow[0:1, 2, :], lnrow[0:1, 0, :], AF.Square), deps=["V"])
                E("V", lambda h: h.tensor_sub(lnrow[0:1, 1, :], lnrow[0:1, 1, :], lnrow[0:1, 2, :]), deps=["S"])
                E("V", lambda h: h.tensor_scalar(lnrow[0:1, 1, :], lnrow[0:1, 1, :], 1e-5, None, ALU.add))
                E("S", lambda h: h.activation(lnrow[0:1, 1, :], lnrow[0:1, 1, :], AF.Sqrt), deps=["V"])
                E("V", lambda h: h.reciprocal(lnrow[0:1, 1, :], lnrow[0:1, 1, :]), deps=["S"])
                E("V", lambda h: h.tensor_mul(lnrow[0:1, 2, :], lnrow[0:1, 0, :], lnrow[0:1, 1, :]))
                E("V", lambda h: h.tensor_scalar(lnrow[0:1, 2, :], lnrow[0:1, 2, :], -1.0, None, ALU.mult))
                E("V", lambda h: h.tensor_copy(lnb16[0:1, 0, :], lnrow[0:1, 1, :]), deps=["T"])
                E("V", lambda h: h.tensor_copy(lnb16[0:1, 1, :], lnrow[0:1, 2, :]))
                E("T", lambda h: h.matmul(pw2[:, :], onesrb[:], lnb16[0:1, 0, :], start=True, stop=True),
                  deps=["V"])
                E("T", lambda h: h.matmul(pw3[:, :], onesrb[:], lnb16[0:1, 1, :], start=True, stop=True))
                E("T", lambda h: h.matmul(pbig[:, 0:32], onesrb[:], zrowb[0:1, 0:32],
                                          start=True, stop=False), deps=["V", "S"])
                bcast_t = p.count["T"]
                tmpf_war = {0: bcast_t, 1: bcast_t}
                for dt in range(8):
                    sl2 = dt % 2
                    E("V", lambda h, dt=dt, sl2=sl2: h.tensor_mul(tmpf[:, sl2, :], xbf[:, 0, dt, :], pw2[:, :]),
                      deps=([("T", bcast_t), ("Y", p.count["Y"])] if dt == 0
                            else [("T", tmpf_war[sl2])]))
                    E("V", lambda h, sl2=sl2: h.tensor_add(tmpf[:, sl2, :], tmpf[:, sl2, :], pw3[:, :]))
                    vdone = p.count["V"]
                    E("S", lambda h, dt=dt, sl2=sl2: h.activation(xn_sb[:, dt, :], tmpf[:, sl2, :],
                                                                  AF.Identity), deps=[("V", vdone)])
                    for blk in range(4):
                        E("T", lambda h, sl2=sl2, dt=dt, blk=blk: h.matmul(
                            pbig[:, 8 * blk:8 * blk + 8], tmpf[:, sl2, 128 * blk:128 * blk + 128],
                            rwTf[:, dt, :], start=False, stop=False,
                            skip_group_check=True), deps=([("V", vdone)] if blk == 0 else []))
                    tmpf_war[sl2] = p.count["T"]
                E("V", lambda h: h.tensor_copy(rlstg[:, :, :].rearrange("p a c -> p (a c)"), pbig[:, 0:32]),
                  deps=["T"])
                E("Y", lambda h, t=t: h.dma_start(
                    out=ds["rl_sp"].rearrange("(t b p) c -> p t b c", p=128, b=4)[:, t, :, :],
                    in_=rlstg[:, :, :]), deps=["V"], inc=16)
                xn_s = p.count["S"]
                E("Y", lambda h, sl=sl: h.dma_start(
                    out=ds["xn_sp"].rearrange("(a p) l -> p a l", p=128)[:, :, sl], in_=xn_sb[:, :, :]),
                  deps=[("S", xn_s)], inc=16)
                for hh in range(2):
                    for dt in range(8):
                        E("T", lambda h, hh=hh, dt=dt: h.matmul(
                            psml[0:1, 0:512], Wbb[:, dt, hh:hh + 1], xn_sb[:, dt, :],
                            start=(dt == 0), stop=(dt == 7), skip_group_check=True),
                          deps=(["V", ("S", xn_s)] if dt == 0 else []))
                    E("S", lambda h: h.activation(bT[0:1, 0, :], psml[0:1, 0:512], AF.Tanh, scale=0.5),
                      deps=["T"])
                    E("V", lambda h: h.tensor_scalar(bT[0:1, 0, :], bT[0:1, 0, :], 0.5, 0.5,
                                                     ALU.mult, ALU.add), deps=["S"])
                    E("V", lambda h, hh=hh: h.tensor_copy(bb16[0:1, hh, :], bT[0:1, 0, :]), deps=["T"])
                    for blk in range(4):
                        E("T", lambda h, hh=hh, blk=blk: h.transpose(
                            ptr[:, 0:1], bb16[0:1, hh, 128 * blk:128 * blk + 128], idb[0:1, 0:1]),
                          deps=["V"])
                        E("V", lambda h, hh=hh, blk=blk: h.tensor_copy(bcol[:, hh, blk:blk + 1], ptr[:, 0:1]),
                          deps=["T", "S"])
                l2n_pending = []

                def l2n_flush():
                    while l2n_pending:
                        dsth_, rawb_, hh, rowb, mmt = l2n_pending.pop(0)
                        E("S", lambda h, rowb=rowb: h.activation(
                            lnrow[0:1, 0, :], psml[rowb:rowb + 1, 0:512], AF.Sqrt, bias=biasr[0:1, 0:1]),
                          deps=[("T", mmt), ("V", p.count["V"])])
                        E("V", lambda h: h.reciprocal(lnrow[0:1, 0, :], lnrow[0:1, 0, :]), deps=["S"])
                        E("V", lambda h: h.tensor_copy(rsqb[0:1, 0, :], lnrow[0:1, 0, :]), deps=["T"])
                        E("T", lambda h: h.matmul(
                            pw2[:, :], onesrb[:], rsqb[0:1, 0, :], start=True, stop=True),
                          deps=["V", "S"])
                        tt = p.count["T"]
                        for ci in range(2):
                            ct = 2 * hh + ci
                            E("V", lambda h, dsth_=dsth_, rawb_=rawb_, ct=ct: h.tensor_mul(
                                dsth_[:, ct, :], rawb_[:, ct, :], pw2[:, :]), deps=[("T", tt)])
                for ai, (Wsb, ring, dsth) in enumerate(((Wqb, rq, qh_sb), (Wkb, rk, kh_sb), (Wvb, rv, None))):
                    rawb = qkraw if ai == 0 else qkraw2
                    for ct in range(4):
                        ring_g = 0
                        if t > 0:
                            ring_g = E("G", lambda h, r=ring, ct=ct: h.tensor_copy(
                                r[:, ct, 0:8], r[:, ct, 512:520]), deps=["T", "S"])
                        for dt in range(8):
                            E("T", lambda h, Wsb=Wsb, ct=ct, dt=dt: h.matmul(
                                pw1[:, :], Wsb[:, dt, 128 * ct:128 * ct + 128], xn_sb[:, dt, :],
                                start=(dt == 0), stop=(dt == 7)),
                              deps=(["V", "S"] if dt == 0 else []))
                        E("S", lambda h, r=ring, ct=ct: h.activation(r[:, ct, 8:520], pw1[:, :], AF.Identity),
                          deps=["T", ("G", ring_g)])
                        for jt in range(4):
                            E("T", lambda h, ai=ai, jt=jt, ct=ct, r=ring: h.matmul(
                                pw2[:, :], cstb[:, ai, jt, ct, :], r[:, ct, 5 + jt:5 + jt + 512],
                                start=(jt == 0), stop=(jt == 3)),
                              deps=(["V", "S"] if jt == 0 else []))
                        if dsth is None:
                            E("S", lambda h, ct=ct, t=t: h.activation(
                                vfir[:, ct, 64 + 512 * (t % 2):64 + 512 * (t % 2) + 512],
                                pw2[:, :], AF.Silu), deps=["T", "V", "G", ("Y", p.count["Y"])])
                        else:
                            E("S", lambda h, rawb=rawb, ct=ct: h.activation(rawb[:, ct, :], pw2[:, :],
                                                                            AF.Silu), deps=["T", "V"])
                    if dsth is None:
                        l2n_flush()
                        continue
                    l2n_flush()
                    for hh in range(2):
                        rowb = (0, 32, 64, 32)[2 * ai + hh]
                        for ci in range(2):
                            ct = 2 * hh + ci
                            E("S", lambda h, rawb=rawb, ct=ct, sl2=(2 * hh + ci) % 2: h.activation(
                                sqx[:, sl2, :], rawb[:, ct, :], AF.Square),
                              deps=[("T", p.count["T"]), ("V", p.count["V"])])
                            E("T", lambda h, rowb=rowb, sl2=(2 * hh + ci) % 2, s0=(ci == 0): h.matmul(
                                psml[rowb:rowb + 1, 0:512], onescb[:], sqx[:, sl2, :],
                                start=s0, stop=False, skip_group_check=True), deps=["S"])
                        l2n_pending.append((dsth, rawb, hh, rowb, p.count["T"]))
                if t % 2 == 1:
                    base16 = (t // 2) * 16
                    fir_pending.extend(range(base16, base16 + 16))
                for g in range(4):
                    ga = t * 4 + g
                    gsl = slice(128 * g, 128 * (g + 1))
                    pmm = [pbig, pw1]
                    pap = [pw2, psml]
                    ptrs = [0]
                    ptr_free = [("V", 0)]

                    def ptr_slot():
                        s = ptrs[0] % 4
                        ptrs[0] += 1
                        return s, ptr[:, 128 * s:128 * s + 128]
                    tc = [p.count["T"], p.count["T"]]
                    vc = [p.count["V"], p.count["V"]]
                    voff = 64 + 512 * (t % 2) + 128 * g
                    for hh in range(2):
                        tc[hh] = E("T", lambda h, hh=hh, gsl=gsl: h.matmul(
                            pmm[hh][:, 0:128], onesrb[:], bb16[0:1, hh, gsl], start=True, stop=True),
                            deps=["V", "S"])
                    for hh in range(2):
                        for ci, ct in enumerate((2 * hh, 2 * hh + 1)):
                            E("V", lambda h, hh=hh, ci=ci, ct=ct, gsl=gsl: h.tensor_mul(
                                kbg[hh][:, ci, :], kh_sb[:, ct, gsl], pmm[hh][:, 0:128]),
                              deps=[("T", tc[hh])])
                        vc[hh] = p.count["V"]
                    for hh in range(2):
                        for ci, ct in enumerate((2 * hh, 2 * hh + 1)):
                            E("T", lambda h, hh=hh, ci=ci, ct=ct, gsl=gsl: h.matmul(
                                pmm[hh][:, 0:128], kbg[hh][:, ci, :], kh_sb[:, ct, gsl],
                                start=(ci == 0), stop=(ci == 1), skip_group_check=True),
                              deps=[("V", vc[hh])])
                        for ci, ct in enumerate((2 * hh, 2 * hh + 1)):
                            E("T", lambda h, hh=hh, ci=ci, ct=ct, gsl=gsl: h.matmul(
                                pmm[hh][:, 128:256], kh_sb[:, ct, gsl], kbg[hh][:, ci, :],
                                start=(ci == 0), stop=(ci == 1), skip_group_check=True))
                        for ci, ct in enumerate((2 * hh, 2 * hh + 1)):
                            E("T", lambda h, hh=hh, ci=ci, ct=ct, gsl=gsl: h.matmul(
                                pmm[hh][:, 256:384], kh_sb[:, ct, gsl], qh_sb[:, ct, gsl],
                                start=(ci == 0), stop=(ci == 1), skip_group_check=True))
                        tc[hh] = p.count["T"]
                    for hh in range(2):
                        E("V", lambda h, hh=hh: h.tensor_mul(At[hh][:, :], pmm[hh][:, 0:128], mlowb[:, :]),
                          deps=[("T", tc[hh])])
                        E("V", lambda h, hh=hh: h.tensor_mul(AtT[hh][:, :], pmm[hh][:, 128:256], mupb[:, :]))
                        for r in range(4):
                            rs = slice(32 * r, 32 * r + 32)
                            E("V", lambda h, hh=hh, rs=rs: h.tensor_mul(
                                locT[hh][rs, :], pmm[hh][rs, 256 + rs.start:256 + rs.start + 32],
                                mincb[rs, rs]))
                        vc[hh] = p.count["V"]
                    def apg(hh, x):
                        if isinstance(x, tuple):
                            return x[0][hh][:, x[1], :]
                        return x[hh][:, :]
                    xb_units = []
                    for hh in range(2):
                        for ci in range(2):
                            xb_units.append(("v", hh, ci))
                            xb_units.append(("k", hh, ci))

                    def emit_xb(unit, voff=voff, g=g, gsl=gsl):
                        kind, hh, ci = unit
                        ct = 2 * hh + ci
                        s, pv = ptr_slot()
                        if kind == "v":
                            E("T", lambda h, ct=ct, pv=pv, voff=voff: h.transpose(
                                pv, vfir[:, ct, voff:voff + 128], idb[:, :]),
                              deps=[ptr_free[0], ("S", p.count["S"])])
                            tt = p.count["T"]
                            E("S", lambda h, hh=hh, ci=ci, pv=pv, g=g: h.activation(
                                X0[hh][:, 128 * ci:128 * ci + 128], pv, AF.Identity,
                                scale=bcol[:, hh, g:g + 1]), deps=[("T", tt)])
                            ptr_free[0] = ("S", p.count["S"])
                        else:
                            E("T", lambda h, ct=ct, pv=pv, gsl=gsl: h.transpose(
                                pv, kh_sb[:, ct, gsl], idb[:, :]),
                              deps=[ptr_free[0]])
                            tt = p.count["T"]
                            E("V", lambda h, hh=hh, ci=ci, pv=pv: h.tensor_copy(
                                kts[hh][:, 128 * ci:128 * ci + 128], pv), deps=[("T", tt)])
                            E("V", lambda h, hh=hh, ci=ci, pv=pv, g=g: h.tensor_scalar(
                                X0[hh][:, 256 + 128 * ci:256 + 128 * ci + 128], pv, bcol[:, hh, g:g + 1],
                                None, ALU.mult))
                            ptr_free[0] = ("V", p.count["V"])
                    xb_i = 0
                    sc_hh = [p.count["S"], p.count["S"]]
                    sqlv = (((AtT, At, (sqA, 0)), (At, AtT, (sqAT, 0))),
                            (((sqAT, 0), (sqA, 0), (sqA, 1)), ((sqA, 0), (sqAT, 0), (sqAT, 1))),
                            (((sqAT, 1), (sqA, 1), (sqA, 2)), ((sqA, 1), (sqAT, 1), (sqAT, 2))),
                            (((sqA, 2), (sqAT, 2), (sqAT, 3)),))
                    for level in sqlv:
                        for _ in range(2):
                            if xb_i < len(xb_units):
                                emit_xb(xb_units[xb_i])
                                xb_i += 1
                        for hh in range(2):
                            for li, (lhsT, rhs, dstx) in enumerate(level):
                                E("T", lambda h, hh=hh, a=lhsT, b=rhs, li=li: h.matmul(
                                    pmm[hh][:, 128 * li:128 * li + 128], apg(hh, a), apg(hh, b),
                                    start=True, stop=True, skip_group_check=True),
                                  deps=([("V", vc[hh]), ("S", sc_hh[hh])] if li == 0 else []))
                            tc[hh] = p.count["T"]
                        for hh in range(2):
                            for li, (lhsT, rhs, dstx) in enumerate(level):
                                if li % 2 == 0:
                                    E("S", lambda h, hh=hh, dstx=dstx, li=li: h.activation(
                                        apg(hh, dstx), pmm[hh][:, 128 * li:128 * li + 128], AF.Identity),
                                      deps=[("T", tc[hh])])
                                else:
                                    E("V", lambda h, hh=hh, dstx=dstx, li=li: h.tensor_copy(
                                        apg(hh, dstx), pmm[hh][:, 128 * li:128 * li + 128]),
                                      deps=[("T", tc[hh])])
                            vc[hh] = p.count["V"]
                            sc_hh[hh] = p.count["S"]
                    while xb_i < len(xb_units):
                        emit_xb(xb_units[xb_i])
                        xb_i += 1
                    xb_done = p.count["V"]
                    fir_guard[0] = p.count["V"]
                    fir_guard[1] = p.count["S"]
                    xb_done_s = p.count["S"]
                    fir_guard[0] = p.count["V"]
                    fir_guard[1] = p.count["S"]
                    xcur = [X0[0], X0[1]]
                    xnxt = [X1[0], X1[1]]
                    for lvl in ((sqAT, 3), (sqAT, 2), (sqAT, 1), (sqAT, 0), AtT):
                        for hh in range(2):
                            tc[hh] = E("T", lambda h, hh=hh, lvl=lvl, xc=xcur[hh]: h.matmul(
                                pap[hh][:, :], apg(hh, lvl), xc[:, :], start=True, stop=True,
                                skip_group_check=True), deps=[("V", vc[hh]), ("V", xb_done),
                                                              ("S", sc_hh[hh]),
                                                              ("S", xb_done_s)])
                        for hh in range(2):
                            E("V", lambda h, hh=hh, xc=xcur[hh], xn_=xnxt[hh]: h.tensor_add(
                                xn_[:, :], xc[:, :], pap[hh][:, :]), deps=[("T", tc[hh])])
                            vc[hh] = p.count["V"]
                        xcur, xnxt = xnxt, xcur
                    Xf = xcur
                    for hh in range(2):
                        for ci in range(2):
                            s, pv = ptr_slot()
                            E("T", lambda h, hh=hh, ci=ci, pv=pv, Xf=Xf: h.transpose(
                                pv, Xf[hh][:, 256 + 128 * ci:256 + 128 * ci + 128], idb[:, :]),
                              deps=[ptr_free[0], ("V", vc[hh])])
                            tt = p.count["T"]
                            E("V", lambda h, hh=hh, ci=ci, pv=pv: h.tensor_copy(wT[hh][:, ci, :], pv),
                              deps=[("T", tt)])
                            ptr_free[0] = ("V", p.count["V"])
                        vc[hh] = p.count["V"]
                    ureg = [pw3[0:32, 0:256], pw2[0:32, 0:256]]
                    oreg = [pw3[32:64, 0:256], pw2[32:64, 0:256]]
                    vall = p.count["V"]
                    for r in range(4):
                        gc = ga * 4 + r
                        rsl = slice(32 * r, 32 * r + 32)
                        tp96 = (32 * r, 0)
                        sbf_c = [None, None]
                        if gc > 0:
                            E("V", lambda h: h.tensor_copy(Sbf[:, 0, :], Sp[0][:, :]),
                              deps=[("T", tc[0]), ("V", vall)])
                            sbf_c[0] = ("V", p.count["V"])
                            E("S", lambda h: h.activation(Sbf[:, 1, :], Sp[1][:, :], AF.Identity),
                              deps=[("T", tc[1])])
                            sbf_c[1] = ("S", p.count["S"])
                            ut = [0, 0]
                            for hh in range(2):
                                for ci in range(2):
                                    E("T", lambda h, hh=hh, ci=ci, rsl=rsl: h.matmul(
                                        ureg[hh], wT[hh][:, ci, rsl], Sbf[:, hh, 256 * ci:256 * ci + 256],
                                        start=(ci == 0), stop=(ci == 1), skip_group_check=True),
                                      deps=[sbf_c[hh]])
                                ut[hh] = p.count["T"]
                            for hh in range(2):
                                E("V", lambda h, hh=hh, rsl=rsl, Xf=Xf: h.tensor_sub(
                                    u_sb[hh][rsl, :], Xf[hh][rsl, 0:256], ureg[hh]),
                                  deps=[("T", ut[hh])])
                                vc[hh] = p.count["V"]
                            ot = [0, 0]
                            for hh in range(2):
                                for ci, ct in enumerate((2 * hh, 2 * hh + 1)):
                                    E("T", lambda h, hh=hh, ct=ct, ci=ci, g=g, r=r: h.matmul(
                                        oreg[hh], qh_sb[:, ct, 128 * g + 32 * r:128 * g + 32 * r + 32],
                                        Sbf[:, hh, 256 * ci:256 * ci + 256], start=(ci == 0), stop=False,
                                        skip_group_check=True), deps=[("V", vc[hh])])
                                E("T", lambda h, hh=hh, rsl=rsl, r=r: h.matmul(
                                    oreg[hh], locT[hh][rsl, :], u_sb[hh][rsl, :],
                                    start=False, stop=True, skip_group_check=True,
                                    tile_position=(32 * r, 32)))
                                ot[hh] = p.count["T"]
                        else:
                            for hh in range(2):
                                E("V", lambda h, hh=hh, rsl=rsl, Xf=Xf: h.tensor_copy(
                                    u_sb[hh][rsl, :], Xf[hh][rsl, 0:256]), deps=[("T", tc[hh])])
                                vc[hh] = p.count["V"]
                            ot = [0, 0]
                            for hh in range(2):
                                E("T", lambda h, hh=hh, rsl=rsl, r=r: h.matmul(
                                    oreg[hh], locT[hh][rsl, :], u_sb[hh][rsl, :],
                                    start=True, stop=True, skip_group_check=True,
                                    tile_position=(32 * r, 32)), deps=[("V", vc[hh]), ("V", vall)])
                                ot[hh] = p.count["T"]
                        for hh in range(2):
                            E("S", lambda h, hh=hh, rsl=rsl: h.activation(osg[hh][rsl, :], oreg[hh],
                                                                          AF.Identity),
                              deps=[("T", ot[hh])])
                        os_s = p.count["S"]
                        for hh in range(2):
                            for ci in range(2):
                                E("T", lambda h, hh=hh, ci=ci, rsl=rsl, tp96=tp96: h.matmul(
                                    Sp[hh][:, 256 * ci:256 * ci + 256],
                                    kts[hh][rsl, 128 * ci:128 * ci + 128],
                                    u_sb[hh][rsl, :], start=False, stop=True,
                                    skip_group_check=True, tile_position=tp96),
                                  deps=[("V", vc[hh]), ("S", os_s)])
                            tc[hh] = p.count["T"]
                        if fir_pending:
                            mm_fir(fir_pending.pop(0))
                    for hh in range(2):
                        for ci in range(2):
                            s, pv = ptr_slot()
                            E("T", lambda h, hh=hh, ci=ci, pv=pv: h.transpose(
                                pv, osg[hh][:, 128 * ci:128 * ci + 128], idb[:, :]),
                              deps=[ptr_free[0], ("V", vc[hh]), ("S", p.count["S"])])
                            tt = p.count["T"]
                            E("V", lambda h, hh=hh, ci=ci, t=t, g=g, pv=pv: h.tensor_copy(
                                od_win[:, 2 * hh + ci, 512 * (t % 2) + 128 * g:512 * (t % 2) + 128 * g + 128],
                                pv), deps=[("T", tt), ("Y", odspill_y[0])])
                            ptr_free[0] = ("V", p.count["V"])
                if t % 2 == 1:
                    w = t // 2
                    wsl = slice(WIN * w, WIN * (w + 1))
                    for grp in fir_pending:
                        mm_fir(grp)
                    fir_pending.clear()
                    for sub in range(2):
                        ssl = slice(512 * sub, 512 * sub + 512)
                        vof = 64 + 512 * sub
                        E("T", lambda h: h.matmul(psml[:, 0:160], onesrb[:], zrowb[0:1, 0:160],
                                                  start=True, stop=False), deps=["V", "S"])

                        def pathf(pi, hh, ci):
                            if pi < 3:
                                return fsb[("fs", "fm", "fl")[pi]][:, 2 * hh + ci, ssl]
                            if pi == 3:
                                return od_win[:, 2 * hh + ci, ssl]
                            return vfir[:, 2 * hh + ci, vof:vof + 512]
                        combos = [(hh, pi, kd) for hh in range(2) for pi in range(P5)
                                  for kd in range(3)]
                        LEAD = 3
                        stk_free = {i: 0 for i in range(5)}
                        kscr_free = {i: 0 for i in range(3)}
                        add_v = {}
                        kp_of = {}
                        kp_ctr = [0]

                        def preproc(idx, pathf=pathf):
                            hh, pi, kd = combos[idx]
                            sslot = idx % 5
                            if kd == 0:
                                a, b = pathf(pi, hh, 0), pathf(pi, hh, 1)
                                E("G", lambda h, a=a, b=b, sslot=sslot: h.tensor_add(
                                    stk[:, sslot, :], a, b),
                                  deps=[("S", p.count["S"]), ("V", p.count["V"]),
                                        ("T", stk_free[sslot])])
                                add_v[idx] = ("G", p.count["G"])
                                return
                            if True:
                                kp = kp_ctr[0] % 3
                                kp_ctr[0] += 1
                                kp_of[idx] = kp
                                af = AF.Square if kd == 1 else AF.Abs
                                for ci in range(2):
                                    E("S", lambda h, pi=pi, hh=hh, ci=ci, kp=kp, af=af, pathf=pathf:
                                      h.activation(kscr[:, 2 * kp + ci, :], pathf(pi, hh, ci), af),
                                      deps=[("T", kscr_free[kp]), ("V", p.count["V"])])
                                a, b = kscr[:, 2 * kp, :], kscr[:, 2 * kp + 1, :]
                                sdep = ("S", p.count["S"])
                            E("V", lambda h, a=a, b=b, sslot=sslot: h.tensor_add(
                                stk[:, sslot, :], a, b), deps=[sdep, ("T", stk_free[sslot])])
                            add_v[idx] = ("V", p.count["V"])

                        def emit_mm(idx):
                            hh, pi, kd = combos[idx]
                            sslot = idx % 5
                            combo = pi * 3 + kd
                            for blk in range(4):
                                E("T", lambda h, hh=hh, combo=combo, sslot=sslot, blk=blk: h.matmul(
                                    psml[:, 40 * blk:40 * blk + 40],
                                    stk[:, sslot, 128 * blk:128 * blk + 128],
                                    selstatb[:, hh, combo, :], start=False, stop=False,
                                    skip_group_check=True),
                                  deps=([add_v[idx]] if blk == 0 else []))
                            stk_free[sslot] = p.count["T"]
                            if idx in kp_of:
                                kscr_free[kp_of[idx]] = p.count["T"]
                        for idx in range(30):
                            preproc(idx)
                            if idx >= LEAD:
                                emit_mm(idx - LEAD)
                        for idx in range(30 - LEAD, 30):
                            emit_mm(idx)
                        for blk in range(4):
                            E("V", lambda h, blk=blk: h.tensor_scalar(
                                psml[:, 40 * blk:40 * blk + 40].rearrange("p (a b) -> p a b", b=4)[:, :, 0:3],
                                psml[:, 40 * blk:40 * blk + 40].rearrange("p (a b) -> p a b", b=4)[:, :, 0:3],
                                1.0 / 256, None, ALU.mult), deps=["T"])
                            E("S", lambda h, blk=blk: h.activation(
                                psml[:, 40 * blk:40 * blk + 40].rearrange("p (a b) -> p a b", b=4)[:, :, 3:4],
                                psml[:, 40 * blk:40 * blk + 40].rearrange("p (a b) -> p a b", b=4)[:, :, 3:4],
                                AF.Sqrt), deps=["V"])
                            E("V", lambda h, blk=blk: h.tensor_copy(stag40[:, :], psml[:, 40 * blk:40 * blk + 40]),
                              deps=["S", "T"])
                            E("T", lambda h: h.transpose(ptr[0:40, 0:128], stag40[:, :], idb[:, :]), deps=["V"])
                            E("V", lambda h, blk=blk: h.tensor_copy(statstg[:, 128 * blk:128 * blk + 128],
                                                                    ptr[0:40, 0:128]), deps=["T", ("Y", p.count["Y"])])
                        E("Y", lambda h, w=w, sub=sub: h.dma_start(
                            out=ds["stats_in"][:, WIN * w + 512 * sub:WIN * w + 512 * sub + 512],
                            in_=statstg[:, :]), deps=["V"], inc=16)
                    E("Y", lambda h, wsl=wsl: h.dma_start(
                        out=ds["od_sp"].rearrange("(a p) l -> p a l", p=128)[:, :, wsl],
                        in_=od_win[:, :, :]), deps=["V"], inc=16)
                    odspill_y[0] = p.count["Y"]
                    E("Y", lambda h, wsl=wsl: h.dma_start(
                        out=ds["v_sp"].rearrange("(a p) l -> p a l", p=128)[:, :, wsl],
                        in_=vfir[:, :, 64:1088]), deps=["S", "V"], inc=16)
                    for nm in ("fs", "fm", "fl"):
                        E("Y", lambda h, nm=nm, wsl=wsl: h.dma_start(
                            out=ds[nm + "_sp"].rearrange("(a p) l -> p a l", p=128)[:, :, wsl],
                            in_=fsb[nm][:, :, :]), deps=["V", "S"], inc=16)
                    fsb_spill_y[0] = p.count["Y"]
                    spill_y = p.count["Y"]
                    for ct in range(4):
                        E("V", lambda h, ct=ct: h.tensor_copy(vfir[:, ct, 0:64], vfir[:, ct, 1024:1088]),
                          deps=[("Y", spill_y), ("T", p.count["T"])])
            E("G", lambda h: h.collective_compute(
                "AllGather", ALU.bypass, replica_groups=PAIRS,
                ins=[ds["stats_in"]], outs=[ds["stats_out"]]),
              deps=[("Y", p.count["Y"])], inc=0, cinc=1)
            cc_stats = p.ccount
            p.barrier()

        with contextlib.ExitStack() as es2:
            def sb2(n, s, d=BF16):
                return es2.enter_context(nc.sbuf_tensor(n, s, d))
            up_b = sb2("up_b", [128, 2, 8, 2048])
            dn_b = sb2("dn_b", [128, 2, 8, 1024])
            xn_ld = sb2("xn_ld", [128, 2, 8, 512])
            rlld = sb2("rlld", [128, 2, 4, 8], F32)
            selrow8b = sb2("selrow8b", [8, 4, 128])
            hid = sb2("hid", [128, 8, 512])
            htmp = sb2("htmp", [128, 512])
            gateT = sb2("gateT", [8, L])
            gtb16 = sb2("gtb16", [128, 8])
            rb8 = sb2("rb8", [128, 6, 8], F32)
            cols8 = sb2("cols8", [128, 8], F32)
            mstage = sb2("mstage", [128, 2, 8, 512], F32)
            tmpf_moe = sb2("tmpf_moe", [128, 512])
            gbc = sb2("gbc", [128, 512])
            E("G", lambda h: h.dma_start(out=selrow8b[:], in_=hb["selrow8e"].rearrange("e k m -> k e m")), inc=16)
            E("G", lambda h: h.dma_start(out=up_b[:, 0, :, :], in_=hb["up"][0].rearrange("(a p) c -> p a c", p=128)), inc=16)
            E("G", lambda h: h.dma_start(out=dn_b[:, 0, :, :], in_=hb["down"][0].rearrange("(a p) c -> p a c", p=128)), inc=16)
            moe_accum_g = {}
            xnr = ds["xn_sp"].rearrange("(a p) l -> p a l", p=128)
            for e in range(4):
                if e < 3:
                    E("G", lambda h, e=e: h.dma_start(
                        out=up_b[:, (e + 1) % 2, :, :],
                        in_=hb["up"][e + 1].rearrange("(a p) c -> p a c", p=128)),
                      deps=[("T", p.count["T"])], inc=16)
                    E("G", lambda h, e=e: h.dma_start(
                        out=dn_b[:, (e + 1) % 2, :, :],
                        in_=hb["down"][e + 1].rearrange("(a p) c -> p a c", p=128)), inc=16)
                eb = e % 2
                w_g = p.count["G"]
                for t in range(NT):
                    d = t % 2
                    sl = slice(TILE * t, TILE * (t + 1))
                    it = e * NT + t
                    if it == 0:
                        xn_y = {}
                        xn_war = {0: 0, 1: 0}

                        def load_xn(it2):
                            tt_ = it2 % NT
                            d_ = tt_ % 2
                            sl_ = slice(TILE * tt_, TILE * (tt_ + 1))
                            E("Y", lambda h, d_=d_, sl_=sl_: h.dma_start(
                                out=xn_ld[:, d_, :, :], in_=xnr[:, :, sl_]),
                              deps=[("T", xn_war[d_])], inc=16)
                            xn_y[d_] = p.count["Y"]
                        load_xn(0)
                    if it + 1 < 4 * NT:
                        load_xn(it + 1)
                    y_xn = xn_y[d]
                    if e == 0:
                        E("Y", lambda h, d=d, t=t: h.dma_start(
                            out=rlld[:, d, :, :],
                            in_=ds["rl_sp"].rearrange("(t b p) c -> p t b c", p=128, b=4)[:, t, :, :]),
                          deps=[("V", p.count["V"])], inc=16)
                        y_rl = p.count["Y"]
                        for blk in range(4):
                            ps8 = rlld[:, d, blk, :]
                            E("V", lambda h, ps8=ps8: h.reduce_max(cols8[:, 0:1], ps8, axis=AX),
                              deps=["T", ("Y", y_rl)])
                            E("V", lambda h, ps8=ps8: h.tensor_scalar(rb8[:, 0, :], ps8, cols8[:, 0:1],
                                                                      None, ALU.subtract))
                            E("S", lambda h: h.activation(rb8[:, 0, :], rb8[:, 0, :], AF.Exp), deps=["V"])
                            E("V", lambda h: h.reduce_sum(cols8[:, 1:2], rb8[:, 0, :], axis=AX), deps=["S"])
                            E("V", lambda h: h.reciprocal(cols8[:, 1:2], cols8[:, 1:2]))
                            E("V", lambda h: h.tensor_scalar(rb8[:, 1, :], rb8[:, 0, :], cols8[:, 1:2],
                                                             None, ALU.mult))
                            E("V", lambda h: h.reduce_max(cols8[:, 2:3], rb8[:, 1, :], axis=AX))
                            E("V", lambda h: h.tensor_scalar(rb8[:, 2, :], rb8[:, 1, :], cols8[:, 2:3],
                                                             None, ALU.is_ge))
                            E("V", lambda h: h.tensor_mul(rb8[:, 3, :], rb8[:, 1, :], rb8[:, 2, :]))
                            E("V", lambda h: h.tensor_sub(rb8[:, 3, :], rb8[:, 1, :], rb8[:, 3, :]))
                            E("V", lambda h: h.reduce_max(cols8[:, 3:4], rb8[:, 3, :], axis=AX))
                            E("V", lambda h: h.tensor_scalar(rb8[:, 4, :], rb8[:, 3, :], cols8[:, 3:4],
                                                             None, ALU.is_ge))
                            E("V", lambda h: h.tensor_add(cols8[:, 4:5], cols8[:, 2:3], cols8[:, 3:4]))
                            E("V", lambda h: h.reciprocal(cols8[:, 4:5], cols8[:, 4:5]))
                            E("V", lambda h: h.tensor_mul(cols8[:, 5:6], cols8[:, 2:3], cols8[:, 4:5]))
                            E("V", lambda h: h.tensor_mul(cols8[:, 6:7], cols8[:, 3:4], cols8[:, 4:5]))
                            E("V", lambda h: h.tensor_scalar(rb8[:, 5, :], rb8[:, 4, :], cols8[:, 6:7],
                                                             None, ALU.mult))
                            E("V", lambda h: h.scalar_tensor_tensor(gtb16[:, :], rb8[:, 2, :], cols8[:, 5:6],
                                                                    rb8[:, 5, :], ALU.mult, ALU.add), deps=["T"])
                            E("T", lambda h: h.transpose(ptr[0:8, 0:128], gtb16[:, :], idb[:, :]), deps=["V"])
                            E("V", lambda h, t=t, blk=blk: h.tensor_copy(
                                gateT[0:8, TILE * t + 128 * blk:TILE * t + 128 * blk + 128], ptr[0:8, 0:128]),
                              deps=["T"])
                    E("T", lambda h, e=e, t=t: h.matmul(
                        pw3[:, :], selrow8b[:, e, :], gateT[0:8, TILE * t:TILE * t + 512],
                        start=True, stop=True), deps=["V"])
                    E("V", lambda h: h.tensor_copy(gbc[:, :], pw3[:, :]), deps=["T"])
                    gbc_v = p.count["V"]
                    pair_vdone = {}
                    pair_sdone = {}
                    for ci in range(8):
                        bA, bB = (pw1, pw2) if ci % 2 == 0 else (pw3, psml)
                        dep0 = [("V", pair_vdone.get(ci - 2, gbc_v)),
                                ("S", pair_sdone.get(ci - 2, 0)), ("G", w_g), ("Y", y_xn)]
                        for dt in range(8):
                            E("T", lambda h, eb=eb, d=d, ci=ci, dt=dt, bA=bA: h.matmul(
                                bA[:, :], up_b[:, eb, dt, 128 * ci:128 * ci + 128], xn_ld[:, d, dt, :],
                                start=(dt == 0), stop=(dt == 7)),
                              deps=(dep0 if dt == 0 else []))
                        for dt in range(8):
                            E("T", lambda h, eb=eb, d=d, ci=ci, dt=dt, bB=bB: h.matmul(
                                bB[:, :], up_b[:, eb, dt, 1024 + 128 * ci:1024 + 128 * ci + 128],
                                xn_ld[:, d, dt, :], start=(dt == 0), stop=(dt == 7)))
                        tdone = p.count["T"]
                        E("S", lambda h, bA=bA: h.activation(htmp[:, :], bA[:, :], AF.Silu),
                          deps=[("T", tdone)])
                        pair_sdone[ci] = p.count["S"]
                        E("V", lambda h, bB=bB: h.tensor_mul(tmpf_moe[:, :], htmp[:, :], bB[:, :]),
                          deps=["S"])
                        E("V", lambda h, ci=ci: h.tensor_mul(hid[:, ci, :], tmpf_moe[:, :], gbc[:, :]))
                        pair_vdone[ci] = p.count["V"]
                    oc_vdone = {}
                    for oc in range(8):
                        bC = pw1 if oc % 2 == 0 else pw2
                        dep0 = [("V", oc_vdone.get(oc - 2, p.count["V"]))]
                        for ht in range(8):
                            E("T", lambda h, eb=eb, oc=oc, ht=ht, bC=bC: h.matmul(
                                bC[:, :], dn_b[:, eb, ht, 128 * oc:128 * oc + 128], hid[:, ht, :],
                                start=(ht == 0), stop=(ht == 7)),
                              deps=(dep0 if ht == 0 else []))
                        tdone = p.count["T"]
                        E("V", lambda h, d=d, oc=oc, bC=bC: h.tensor_copy(mstage[:, d, oc, :], bC[:, :]),
                          deps=[("T", tdone), ("G", moe_accum_g.get((e, t - 2), 0))])
                        oc_vdone[oc] = p.count["V"]
                    mv = p.count["V"]
                    E("G", lambda h, e=e, t=t, d=d: h.dma_start(
                        out=ds["moe_sp"][t // 4].rearrange("(a p) l -> p a l", p=128)[
                            :, :, 512 * (t % 4):512 * (t % 4) + 512],
                        in_=mstage[:, d, :, :],
                        accum_op=(ALU.bypass if e == 0 else ALU.add)),
                      deps=[("V", mv), ("G", moe_accum_g.get((e - 1, t), 0))], inc=16)
                    moe_accum_g[(e, t)] = p.count["G"]
                    xn_war[d] = p.count["T"]
                    if e == 3 and t % 4 == 3:
                        lq = t // 4
                        E("G", lambda h, lq=lq: h.collective_compute(
                            "ReduceScatter", ALU.add, replica_groups=PAIRS,
                            ins=[ds["moe_sp"][lq]], outs=[ds["moe_rs"][lq]]),
                          inc=0, cinc=1)
            if DEBUG:
                dbgg = nc.dram_tensor("dbg_gate", [8, L], F32, kind="ExternalOutput").ap()
                E("G", lambda h, dbgg=dbgg: h.dma_start(out=dbgg[:], in_=gateT[:, :]),
                  deps=["V", "T", ("G", p.count["G"])], inc=16)
                for lq in range(4):
                    dbgm = nc.dram_tensor("dbg_msp%d" % lq, [1024, 2048], F32, kind="ExternalOutput").ap()
                    E("Y", lambda h, dbgm=dbgm, lq=lq: h.dma_start(out=dbgm[:], in_=ds["moe_sp"][lq]),
                      deps=[("G", p.count["G"])], inc=16)
            cc_moe = p.ccount
            p.barrier()

        with contextlib.ExitStack() as es3:
            def sb3(n, s, d=BF16):
                return es3.enter_context(nc.sbuf_tensor(n, s, d))
            gW1ab = sb3("gW1ab", [128, 8, 1024])
            gb1b = sb3("gb1b", [128, 8], F32)
            gW2b = sb3("gW2b", [128, 8, 20])
            statsAG = sb3("statsAG", [80, L])
            xn_ld2 = sb3("xn_ld2", [128, 2, 8, 512])
            G1 = sb3("G1", [128, 8, 512])
            lgstg = sb3("lgstg", [128, 4, 20], F32)
            E("G", lambda h: h.dma_start(out=gW1ab[:], in_=hb["gW1a"].rearrange("(a p) c -> p a c", p=128)), inc=16)
            E("Y", lambda h: h.dma_start(out=gb1b[:], in_=hb["gb1c"][:]), inc=16)
            E("G", lambda h: h.dma_start(out=gW2b[:], in_=hb["gW2"].rearrange("(a p) c -> p a c", p=128)), inc=16)
            E("G", lambda h: h.dma_start(out=statsAG[:], in_=ds["stats_out"][:]),
              deps=[("C", cc_stats)], inc=16)
            w2a_g = p.count["G"]
            w2a_y = p.count["Y"]
            for t in range(NT):
                d = t % 2
                sl = slice(TILE * t, TILE * (t + 1))
                E("Y", lambda h, d=d, sl=sl: h.dma_start(out=xn_ld2[:, d, :, :], in_=xnr[:, :, sl]),
                  deps=[("T", p.count["T"])], inc=16)
                y_xn = p.count["Y"]
                for ct in range(8):
                    for dt in range(8):
                        E("T", lambda h, d=d, ct=ct, dt=dt: h.matmul(
                            pw1[:, :], gW1ab[:, dt, 128 * ct:128 * ct + 128], xn_ld2[:, d, dt, :],
                            start=(dt == 0), stop=False, skip_group_check=True),
                          deps=(["S", ("G", w2a_g), ("Y", y_xn)] if dt == 0 else []))
                    E("T", lambda h, ct=ct, sl=sl: h.matmul(
                        pw1[:, :], Rsb[:, 128 * ct:128 * ct + 128], statsAG[:, sl],
                        start=False, stop=True, skip_group_check=True),
                      deps=[("G", Rsb_g)])
                    E("S", lambda h, ct=ct: h.activation(G1[:, ct, :], pw1[:, :], AF.Gelu,
                                                         bias=gb1b[:, ct:ct + 1]), deps=["T"])
                for blk in range(4):
                    for ct in range(8):
                        E("T", lambda h, ct=ct, blk=blk: h.matmul(
                            psml[:, 20 * blk:20 * blk + 20], G1[:, ct, 128 * blk:128 * blk + 128],
                            gW2b[:, ct, :], start=(ct == 0), stop=(ct == 7), skip_group_check=True),
                          deps=(["S", "V"] if ct == 0 else []))
                    E("V", lambda h, blk=blk: h.tensor_copy(lgstg[:, blk, :], psml[:, 20 * blk:20 * blk + 20]),
                      deps=["T", ("Y", p.count["Y"])])
                E("Y", lambda h, t=t: h.dma_start(
                    out=ds["lg_in"].rearrange("(t b p) c -> p t b c", p=128, b=4)[:, t, :, :],
                    in_=lgstg[:, :, :]), deps=["V"], inc=16)
            E("G", lambda h: h.collective_compute(
                "AllReduce", ALU.add, replica_groups=PAIRS,
                ins=[ds["lg_in"]], outs=[ds["lg_out"]]),
              deps=[("Y", p.count["Y"])], inc=0, cinc=1)
            cc_lg = p.ccount
            p.barrier()

        with contextlib.ExitStack() as es4:
            def sb4(n, s, d=BF16):
                return es4.enter_context(nc.sbuf_tensor(n, s, d))
            selmixb = sb4("selmixb", [20, 10, 128])
            lgld = sb4("lgld", [128, 2, 4, 20], F32)
            prw = sb4("prw", [128, 4, 20], F32)
            prmax = sb4("prmax", [128, 4, 4], F32)
            prb16 = sb4("prb16", [128, 20])
            probsT = sb4("probsT", [20, 512])
            pth = {}
            for nm in ("v", "od", "fs", "fm", "fl"):
                pth[nm] = sb4("p_" + nm, [128, 2, 4, 512])
            oacc = sb4("oacc", [128, 2, 512], F32)
            otmp = sb4("otmp", [128, 512], F32)
            sqo = sb4("sqo", [128, 2, 512])
            orow = sb4("orow", [1, 2, 512], F32)
            orb16 = sb4("orb16", [1, 512])
            obuf = sb4("obuf", [128, 4, 512])
            nwcb2 = sb4("nwcb2", [128, 2], F32)
            E("G", lambda h: h.dma_start(out=selmixb[:], in_=hb["selmix"].rearrange("s k m -> k s m")), inc=16)
            E("Y", lambda h: h.dma_start(out=nwcb2[:], in_=hb["nwc"][:]), inc=16)
            sel_g = p.count["G"]
            nw_y = p.count["Y"]
            for t in range(NT):
                d = t % 2
                sl = slice(TILE * t, TILE * (t + 1))
                E("Y", lambda h, d=d, t=t: h.dma_start(
                    out=lgld[:, d, :, :],
                    in_=ds["lg_out"].rearrange("(t b p) c -> p t b c", p=128, b=4)[:, t, :, :]),
                  deps=[("C", cc_lg), ("V", p.count["V"]), ("S", p.count["S"])], inc=16)
                for nm in ("v", "od", "fs", "fm", "fl"):
                    E("Y", lambda h, nm=nm, d=d, sl=sl: h.dma_start(
                        out=pth[nm][:, d, :, :],
                        in_=ds[(nm if nm in ("fs", "fm", "fl") else nm) + "_sp"].rearrange(
                            "(a p) l -> p a l", p=128)[:, :, sl]),
                      deps=[("T", p.count["T"]), ("V", p.count["V"])], inc=16)
                y_ld = p.count["Y"]
                for blk in range(4):
                    E("V", lambda h, d=d, blk=blk: h.tensor_add(prw[:, blk, :], lgld[:, d, blk, :], gb2bc[:, :]),
                      deps=[("Y", y_ld)])
                    E("V", lambda h, blk=blk: h.tensor_mul(prw[:, blk, :], prw[:, blk, :], itaubc[:, :]))
                    E("V", lambda h, blk=blk: h.reduce_max(
                        prmax[:, blk, :].rearrange("p (a b) -> p a b", b=1),
                        prw[:, blk, :].rearrange("p (a b) -> p a b", b=5), axis=AX))
                for blk in range(4):
                    for hh4 in range(4):
                        E("V", lambda h, blk=blk, hh4=hh4: h.tensor_scalar(
                            prw[:, blk, 5 * hh4:5 * hh4 + 5], prw[:, blk, 5 * hh4:5 * hh4 + 5],
                            prmax[:, blk, hh4:hh4 + 1], None, ALU.subtract))
                E("S", lambda h: h.activation(prw[:, :, :], prw[:, :, :], AF.Exp), deps=["V"])
                for blk in range(4):
                    E("V", lambda h, blk=blk: h.reduce_sum(
                        prmax[:, blk, :].rearrange("p (a b) -> p a b", b=1),
                        prw[:, blk, :].rearrange("p (a b) -> p a b", b=5), axis=AX), deps=["S"])
                    E("V", lambda h, blk=blk: h.reciprocal(prmax[:, blk, :], prmax[:, blk, :]))
                    for hh4 in range(4):
                        E("V", lambda h, blk=blk, hh4=hh4: h.tensor_scalar(
                            prw[:, blk, 5 * hh4:5 * hh4 + 5], prw[:, blk, 5 * hh4:5 * hh4 + 5],
                            prmax[:, blk, hh4:hh4 + 1], None, ALU.mult))
                    E("V", lambda h, blk=blk: h.tensor_mul(prw[:, blk, :], prw[:, blk, :], facbc[:, :]))
                    E("V", lambda h, blk=blk: h.tensor_add(prw[:, blk, :], prw[:, blk, :], epsbc[:, :]))
                    E("V", lambda h, blk=blk: h.tensor_copy(prb16[:, :], prw[:, blk, :]), deps=["T"])
                    E("T", lambda h: h.transpose(ptr[0:20, 0:128], prb16[:, :], idb[:, :]), deps=["V"])
                    E("V", lambda h, blk=blk: h.tensor_copy(probsT[:, 128 * blk:128 * blk + 128], ptr[0:20, 0:128]),
                      deps=["T"])
                for hh in range(2):
                    for pi, nm in enumerate(("fs", "fm", "fl", "od", "v")):
                        E("T", lambda h, hh=hh, pi=pi: h.matmul(
                            pw2[:, :], selmixb[:, 5 * hh + pi, :],
                            probsT[:, :], start=True, stop=True), deps=["V"])
                        for ci in range(2):
                            ct = 2 * hh + ci
                            if pi == 0:
                                E("V", lambda h, nm=nm, d=d, ct=ct, ci=ci: h.tensor_mul(
                                    oacc[:, ci, :], pth[nm][:, d, ct, :], pw2[:, :]), deps=["T"])
                            else:
                                E("V", lambda h, nm=nm, d=d, ct=ct: h.tensor_mul(
                                    otmp[:, :], pth[nm][:, d, ct, :], pw2[:, :]), deps=["T"])
                                E("V", lambda h, ci=ci: h.tensor_add(oacc[:, ci, :], oacc[:, ci, :], otmp[:, :]))
                    for ci in range(2):
                        E("S", lambda h, ci=ci: h.activation(sqo[:, ci, :], oacc[:, ci, :], AF.Square),
                          deps=["V"])
                        E("T", lambda h, ci=ci: h.matmul(psml[0:1, 0:512], onescb[:], sqo[:, ci, :],
                                                         start=(ci == 0), stop=(ci == 1),
                                                         skip_group_check=True), deps=["S"])
                    E("V", lambda h: h.tensor_scalar(orow[0:1, 0, :], psml[0:1, 0:512], 1.0 / 256, 1e-5,
                                                     ALU.mult, ALU.add), deps=["T"])
                    E("S", lambda h: h.activation(orow[0:1, 0, :], orow[0:1, 0, :], AF.Sqrt), deps=["V"])
                    E("V", lambda h: h.reciprocal(orow[0:1, 0, :], orow[0:1, 0, :]), deps=["S"])
                    E("V", lambda h: h.tensor_copy(orb16[0:1, :], orow[0:1, 0, :]), deps=["T"])
                    E("T", lambda h: h.matmul(pw2[:, :], onesrb[:], orb16[0:1, :], start=True, stop=True),
                      deps=["V"])
                    for ci in range(2):
                        ct = 2 * hh + ci
                        E("V", lambda h, ci=ci: h.tensor_mul(otmp[:, :], oacc[:, ci, :], pw2[:, :]),
                          deps=["T"])
                        E("V", lambda h, ct=ct, ci=ci: h.tensor_scalar(
                            obuf[:, ct, :], otmp[:, :], nwcb2[:, ci:ci + 1], None, ALU.mult),
                          deps=[("Y", p.count["Y"])])
                half = t // 8
                if DEBUG:
                    E("Y", lambda h, t=t: h.dma_start(
                        out=ds["probs_dbg"].rearrange("(t b p) c -> p t b c", p=128, b=4)[:, t, :, :],
                        in_=prw[:, :, :]), deps=["V"], inc=16)
                E("Y", lambda h, t=t, half=half: h.dma_start(
                    out=ds["o_in%d" % half].rearrange("(a p) l -> p a l", p=128)[
                        :, :, TILE * (t % 8):TILE * (t % 8) + 512],
                    in_=obuf[:, :, :]), deps=["V"], inc=16)
                if t == 7:
                    E("G", lambda h: h.collective_compute(
                        "AllGather", ALU.bypass, replica_groups=PAIRS,
                        ins=[ds["o_in0"]], outs=[ds["o_out0"]]),
                      deps=[("Y", p.count["Y"])], inc=0, cinc=1)
            E("G", lambda h: h.collective_compute(
                "AllGather", ALU.bypass, replica_groups=PAIRS,
                ins=[ds["o_in1"]], outs=[ds["o_out1"]]),
              deps=[("Y", p.count["Y"])], inc=0, cinc=1)
            cc_o = p.ccount
            p.barrier()

        with contextlib.ExitStack() as es5:
            def sb5(n, s, d=BF16):
                return es5.enter_context(nc.sbuf_tensor(n, s, d))
            Wob = sb5("Wob", [128, 8, 512])
            o_ld = sb5("o_ld", [128, 2, 8, 512])
            grs = sb5("grs", [128, 2, 4, 512], F32)
            gt = sb5("gt", [128, 4, 512])
            xrt = sb5("xrt", [128, 2, 4, 512], F32)
            sgt = sb5("sgt", [128, 512], F32)
            outsb = sb5("outsb", [128, 4, 512], F32)
            E("G", lambda h: h.dma_start(out=Wob[:], in_=hb["Wo"].rearrange("(a p) c -> p a c", p=128)), inc=16)
            wo_g = p.count["G"]
            xrr = hb["xres"].rearrange("(a p) l -> p a l", p=128)
            outr = out_h.rearrange("(a p) l -> p a l", p=128)
            for t in range(NT):
                d = t % 2
                half = t // 8
                oo = ds["o_out%d" % half].rearrange("(a p) l -> p a l", p=128)
                E("Y", lambda h, d=d, t=t, oo=oo, half=half: h.dma_start(
                    out=o_ld[:, d, :, :], in_=oo[:, :, TILE * (t % 8):TILE * (t % 8) + 512]),
                  deps=[("C", cc_o - (1 if half == 0 else 0)), ("T", p.count["T"])], inc=16)
                E("Y", lambda h, d=d, t=t: h.dma_start(
                    out=grs[:, d, :, :],
                    in_=ds["moe_rs"][t // 4].rearrange("(a p) l -> p a l", p=128)[
                        :, :, 512 * (t % 4):512 * (t % 4) + 512]),
                  deps=[("C", cc_moe), ("S", p.count["S"])], inc=16)
                E("Y", lambda h, d=d, sl=slice(TILE * t, TILE * t + 512): h.dma_start(
                    out=xrt[:, d, :, :], in_=xrr[:, :, sl]),
                  deps=[("V", p.count["V"])], inc=16)
                y_ld3 = p.count["Y"]
                for a4 in range(4):
                    E("S", lambda h, d=d, a4=a4: h.activation(gt[:, a4, :], grs[:, d, a4, :], AF.Silu),
                      deps=[("Y", y_ld3), ("V", p.count["V"])])
                for oc in range(4):
                    for kt in range(8):
                        E("T", lambda h, d=d, oc=oc, kt=kt: h.matmul(
                            pw1[:, :], Wob[:, kt, 128 * oc:128 * oc + 128], o_ld[:, d, kt, :],
                            start=(kt == 0), stop=(kt == 7)),
                          deps=(["V", ("G", wo_g), ("Y", y_ld3)] if kt == 0 else []))
                    E("V", lambda h, oc=oc: h.tensor_mul(sgt[:, :], gt[:, oc, :], pw1[:, :]),
                      deps=["T", "S"])
                    E("V", lambda h, d=d, oc=oc: h.tensor_add(outsb[:, oc, :], xrt[:, d, oc, :], sgt[:, :]))
                E("Y", lambda h, t=t: h.dma_start(
                    out=outr[:, :, TILE * t:TILE * t + 512], in_=outsb[:, :, :]),
                  deps=["V"], inc=16)
            if DEBUG:
                for buf, nm2 in ((gb2bc, "gb2bc"), (itaubc, "itaubc"), (facbc, "facbc"), (epsbc, "epsbc")):
                    dbg2 = nc.dram_tensor("dbg_" + nm2, [128, 20], F32, kind="ExternalOutput").ap()
                    E("Y", lambda h, dbg2=dbg2, buf=buf: h.dma_start(out=dbg2[:], in_=buf[:]),
                      deps=[("Y", p.count["Y"])], inc=16)
                dbgr = nc.dram_tensor("dbg_row20", [1, 4, 20], F32, kind="ExternalOutput").ap()
                E("Y", lambda h, dbgr=dbgr: h.dma_start(out=dbgr[:], in_=row20[:, :, :]), inc=16)
                dbgs = nc.dram_tensor("dbg_rowsml", [1, 32], F32, kind="ExternalOutput").ap()
                E("Y", lambda h, dbgs=dbgs: h.dma_start(out=dbgs[:], in_=rowsml[:, :]), inc=16)
                for nm, shp in (("xn_sp", [1024, L]), ("v_sp", [512, L]), ("od_sp", [512, L]),
                                ("fs_sp", [512, L]), ("fm_sp", [512, L]), ("fl_sp", [512, L]),
                                ("stats_out", [80, L]), ("o_out0", [1024, L // 2]),
                                ("o_out1", [1024, L // 2])):
                    dbg = nc.dram_tensor("dbg_" + nm, shp, F32, kind="ExternalOutput").ap()
                    E("G", lambda h, dbg=dbg, nm=nm: h.dma_start(out=dbg[:], in_=ds[nm][:]),
                      deps=[("Y", p.count["Y"]), ("C", p.ccount)], inc=16)
                dbg = nc.dram_tensor("dbg_lg", [L, 20], F32, kind="ExternalOutput").ap()
                E("G", lambda h, dbg=dbg: h.dma_start(out=dbg[:], in_=ds["lg_out"][:]), inc=16)
                dbg = nc.dram_tensor("dbg_probs", [L, 20], F32, kind="ExternalOutput").ap()
                E("G", lambda h, dbg=dbg: h.dma_start(out=dbg[:], in_=ds["probs_dbg"][:]), inc=16)
                dbg = nc.dram_tensor("dbg_rl", [L, 8], F32, kind="ExternalOutput").ap()
                E("G", lambda h, dbg=dbg: h.dma_start(out=dbg[:], in_=ds["rl_sp"][:]), inc=16)
                for lq in range(4):
                    dbg = nc.dram_tensor("dbg_moe%d" % lq, [512, 2048], F32, kind="ExternalOutput").ap()
                    E("G", lambda h, dbg=dbg, lq=lq: h.dma_start(out=dbg[:], in_=ds["moe_rs"][lq]), inc=16)
            E("G", lambda h: h.nop(), deps=[("Y", p.count["Y"]), ("G", p.count["G"])])

        p.finalize(block)
    return nc



def kernel(**inputs):
    maps = host_prep(inputs)
    nc = build_nc()
    res = run_bass_kernel_spmd(nc, maps, core_ids=list(range(8)))
    out = np.zeros((B, L, D), np.float32)
    for c in range(8):
        b, j = c // 2, c % 2
        out[b, :, 512 * j:512 * (j + 1)] = res.results[c]["out"].T
    return out


# version 25 (final)
# speedup vs baseline: 1.0051x; 1.0030x over previous
import contextlib
import numpy as np
import ml_dtypes
import concourse.bass as bass
import concourse.mybir as mybir
from concourse.bass_utils import run_bass_kernel_spmd

F32 = mybir.dt.float32
BF16 = mybir.dt.bfloat16
AF = mybir.ActivationFunctionType
ALU = mybir.AluOpType
AX = mybir.AxisListType.X
DEBUG = False

B, L, D, H = 4, 8192, 1024, 4
DV = 256
P5 = 5
TILE = 512
NT = L // TILE
WIN = 1024
NW = L // WIN
PAIRS = [[0, 1], [2, 3], [4, 5], [6, 7]]
QUADS = [[0, 2, 4, 6], [1, 3, 5, 7]]
FIRS = (("fs", 3), ("fm", 15), ("fl", 64))
ENGINES = ("T", "V", "S", "G", "Y")



def stats_row(j, p, h_loc, kind):
    return j * 40 + (p * 2 + h_loc) * 4 + kind


def build_selR(b, j):
    sel = np.zeros((40, 128, 80), np.float32)
    for t in range(40):
        gg = b * 20 + t // 2
        p = gg // 16
        r = gg % 16
        h_glob, kind = r // 4, r % 4
        sel[t, :, stats_row(h_glob // 2, p, h_glob % 2, kind)] = 1.0
    return sel


def build_fir_ST(w_slice, K):
    NG_ = (K + 2) // 4 + 1
    ST = np.zeros((NG_, 128, 128), np.float32)
    for G in range(NG_):
        for r in range(4):
            for d in range(4):
                jt = 4 * G + r - d
                if 0 <= jt <= K - 1:
                    ST[G, 32 * r:32 * r + 32, 32 * d:32 * d + 32][
                        np.arange(32), np.arange(32)] = w_slice[:, jt]
    return ST


def build_selstat2():
    s = np.zeros((30, 128, 96), np.float32)
    for r in range(30):
        s[r, :, 32 * (r // 10) + r % 10] = 1.0
    return s


def build_selrow8e(j):
    s = np.zeros((4, 8, 128), np.float32)
    for e in range(4):
        s[e, 4 * j + e, :] = 1.0
    return s


def build_selmix(j):
    sm = np.zeros((10, 20, 128), np.float32)
    for hh in range(2):
        for pi in range(5):
            sm[hh * 5 + pi, (2 * j + hh) * 5 + pi, :] = 1.0
    return sm


def build_selstat():
    combos = [(p, kd) for p in range(P5) for kd in range(3)]
    sel = np.zeros((2, 15, 128, 40), np.float32)
    for h in range(2):
        for ci, (p, kd) in enumerate(combos):
            base = (p * 2 + h) * 4
            if kd == 0:
                sel[h, ci, :, base + 0] = 1.0
            elif kd == 1:
                sel[h, ci, :, base + 1] = 1.0
                sel[h, ci, :, base + 3] = 1.0
            else:
                sel[h, ci, :, base + 2] = 1.0
    return sel


def host_prep(inp):
    x = np.asarray(inp["L0_attn_hidden_states"], np.float32)
    gW1 = np.asarray(inp["L0_attn_gW1"], np.float32)
    masks = {}
    mlow = np.zeros((128, 128), np.float32)
    mup = np.zeros((128, 128), np.float32)
    minc = np.zeros((128, 128), np.float32)
    for r in range(4):
        s = slice(32 * r, 32 * r + 32)
        mlow[s, s] = -np.tril(np.ones((32, 32), np.float32), -1)
        mup[s, s] = -np.triu(np.ones((32, 32), np.float32), 1)
        minc[s, s] = np.triu(np.ones((32, 32), np.float32), 0)
    ident = np.eye(128, dtype=np.float32)
    onesc = np.ones((128, 1), np.float32)
    onesr = np.ones((1, 128), np.float32)
    ones5 = np.ones((1, 5), np.float32)
    zrow = np.zeros((1, 512), np.float32)

    maps = []
    for c in range(8):
        b, j = c // 2, c % 2
        hsl = slice(512 * j, 512 * (j + 1))
        csl = slice(1024 * j, 1024 * (j + 1))
        d = {
            "x": np.ascontiguousarray(x[b].T),
            "xres": np.ascontiguousarray(x[b].T[hsl]),
            "Wq": np.ascontiguousarray(inp["L0_attn_Wq"][:, hsl]),
            "Wk": np.ascontiguousarray(inp["L0_attn_Wk"][:, hsl]),
            "Wv": np.ascontiguousarray(inp["L0_attn_Wv"][:, hsl]),
            "Wb2": np.ascontiguousarray(inp["L0_attn_Wb"][:, 2 * j:2 * j + 2]),
            "gW1a": np.ascontiguousarray(gW1[0:1024, csl]),
            "gW1r": np.ascontiguousarray(gW1[1024 + 5120 * b:1024 + 5120 * (b + 1), csl]),
            "gb1c": np.ascontiguousarray(np.asarray(inp["L0_attn_gb1"], np.float32)[csl].reshape(8, 128).T),
            "gW2": np.ascontiguousarray(np.asarray(inp["L0_attn_gW2"], np.float32)[csl]),
            "gb2r": np.asarray(inp["L0_attn_gb2"], np.float32).reshape(1, 20),
            "ltau": np.asarray(inp["L0_attn_log_tau"], np.float32).reshape(1, 4),
            "epsr": np.asarray(inp["L0_attn_eps_logit"], np.float32).reshape(1, 20),
            "nwc": np.ascontiguousarray(np.asarray(inp["L0_attn_norm_w"], np.float32).reshape(2, 128).T),
            "Wo": np.ascontiguousarray(inp["L0_attn_Wo"][:, hsl]),
            "rwT": np.ascontiguousarray(np.asarray(inp["L0_ffn_router_w"], np.float32).T),
            "up": np.ascontiguousarray(inp["L0_ffn_up_proj"][4 * j:4 * j + 4]),
            "down": np.ascontiguousarray(inp["L0_ffn_down_proj"][4 * j:4 * j + 4]),
            "selR": build_selR(b, j),
            "selmix": build_selmix(j),
            "selstat": build_selstat(),
            "selrow8e": build_selrow8e(j),
            "ident": ident, "onesc": onesc, "onesr": onesr,
            "biasrow": np.array([[1e-6, 1e-5]], np.float32),
            "ones5": ones5, "zrow": zrow,
            "mlow": mlow, "mup": mup, "minc": minc,
        }
        cst = np.zeros((3, 4, 4, 128, 128), np.float32)
        for ai, nm in enumerate(("conv_q", "conv_k", "conv_v")):
            cw = np.asarray(inp["L0_attn_" + nm], np.float32)[hsl]
            for jt in range(4):
                for ct in range(4):
                    cst[ai, jt, ct][np.arange(128), np.arange(128)] = cw[128 * ct:128 * ct + 128, jt]
        d["convST"] = cst
        stf = np.zeros((16, 24, 128, 128), np.float32)
        for g in range(16):
            off = 0
            for nm, K in (("fir_s", 3), ("fir_m", 15), ("fir_l", 64)):
                w = np.asarray(inp["L0_attn_" + nm], np.float32)[hsl][32 * g:32 * g + 32]
                st = build_fir_ST(w, K)
                stf[g, off:off + st.shape[0]] = st
                off += st.shape[0]
        d["STf"] = stf
        maps.append(d)
    return maps



class Prog:
    def __init__(self, nc, es):
        self.nc = nc
        self.sems = {e: es.enter_context(nc.semaphore("sem" + e)) for e in ENGINES}
        self.sems["D"] = es.enter_context(nc.semaphore("semD"))
        self.csem = es.enter_context(nc.semaphore("semC"))
        self.count = {e: 0 for e in ENGINES}
        self.count["D"] = 0
        self.ccount = 0
        self.last_wait = {}
        self.prog = {e: [] for e in ENGINES}

    def op(self, eng, fn, deps=(), inc=1, cinc=0, dinc=0):
        if eng in ("V", "S") and not dinc:
            inc = max(inc, 1)
            if self.count[eng] > 0:
                deps = list(deps) + [(eng, self.count[eng])]
        waits = []
        for d in deps:
            if isinstance(d, tuple):
                de, dv = d
            else:
                de, dv = d, (self.ccount if d == "C" else self.count[d])
            if dv <= 0:
                continue
            key = (eng, de)
            if self.last_wait.get(key, -1) >= dv:
                continue
            self.last_wait[key] = dv
            waits.append((self.csem if de == "C" else self.sems[de], dv))
        if dinc:
            self.count["D"] += dinc
            sem = self.sems["D"]
            niv = dinc
        else:
            self.count[eng] += inc
            if cinc:
                self.ccount += cinc
            sem = self.csem if cinc else self.sems[eng]
            niv = cinc if cinc else inc

        def emit(h, waits=waits, fn=fn, sem=sem, niv=niv):
            for s, v in waits:
                h.wait_ge(s, v)
            ins = fn(h)
            if niv:
                ins.then_inc(sem, niv)
        self.prog[eng].append(emit)
        return self.count["D"] if dinc else (self.ccount if cinc else self.count[eng])

    def barrier(self):
        snap = dict(self.count)
        for e in ENGINES:
            deps = [(d, snap[d]) for d in ENGINES if d != e and snap[d] > 0]
            self.op(e, lambda h: h.nop(), deps=deps)

    def finalize(self, block):
        for ename, attr in (("G", "gpsimd"), ("S", "scalar"), ("T", "tensor"),
                            ("V", "vector"), ("Y", "sync")):
            plist = self.prog[ename]

            def runner(h, plist=plist):
                for f in plist:
                    f(h)
            getattr(block, attr)(runner)



def build_nc():
    nc = bass.Bass("TRN2", target_bir_lowering=False)
    hb = {}
    for name, shape, dt in (
        ("x", [1024, L], F32), ("xres", [512, L], F32),
        ("Wq", [1024, 512], F32), ("Wk", [1024, 512], F32), ("Wv", [1024, 512], F32),
        ("Wb2", [1024, 2], F32), ("gW1a", [1024, 1024], F32),
        ("gW1r", [5120, 1024], F32), ("gb1c", [128, 8], F32),
        ("gW2", [1024, 20], F32), ("gb2r", [1, 20], F32),
        ("ltau", [1, 4], F32), ("epsr", [1, 20], F32), ("nwc", [128, 2], F32),
        ("Wo", [1024, 512], F32), ("rwT", [1024, 8], F32),
        ("up", [4, 1024, 2048], F32), ("down", [4, 1024, 1024], F32),
        ("selR", [40, 128, 80], F32),
        ("selstat", [2, 15, 128, 40], F32),
        ("selrow8e", [4, 8, 128], F32),
        ("selmix", [10, 20, 128], F32),
        ("ident", [128, 128], F32), ("onesc", [128, 1], F32), ("biasrow", [1, 2], F32),
        ("onesr", [1, 128], F32), ("ones5", [1, 5], F32), ("zrow", [1, 512], F32),
        ("mlow", [128, 128], F32), ("mup", [128, 128], F32), ("minc", [128, 128], F32),
        ("convST", [3, 4, 4, 128, 128], F32), ("STf", [16, 24, 128, 128], F32),
    ):
        hb[name] = nc.dram_tensor(name, shape, dt, kind="ExternalInput").ap()
    out_h = nc.dram_tensor("out", [512, L], F32, kind="ExternalOutput").ap()
    ds = {}
    for name, shape, dt in (
        ("xn_sp", [1024, L], BF16), ("v_sp", [512, L], BF16),
        ("od_sp", [512, L], BF16),
        ("fs_sp", [512, L], BF16), ("fm_sp", [512, L], BF16), ("fl_sp", [512, L], BF16),
        ("stats_in", [40, L], BF16), ("stats_out", [80, L], BF16),
        ("R_in", [80, 1024], F32), ("R_out", [80, 1024], F32),
        ("lg_in", [L, 20], F32), ("lg_out", [L, 20], F32),
        ("o_in0", [512, L // 2], BF16), ("o_in1", [512, L // 2], BF16),
        ("o_out0", [1024, L // 2], BF16), ("o_out1", [1024, L // 2], BF16),
        ("moe_sp", [4, 1024, 2048], F32), ("moe_rs", [4, 512, 2048], F32),
        ("probs_dbg", [L, 20], F32), ("rl_sp", [L, 8], F32),
    ):
        ds[name] = nc.dram_tensor(name, shape, dt, kind="Internal").ap()

    es = contextlib.ExitStack()
    with es:
        def sb(n, s, d=BF16):
            return es.enter_context(nc.sbuf_tensor(n, s, d))

        def psum(n, s, d=F32):
            return es.enter_context(nc.psum_tensor(n, s, d))

        p = Prog(nc, es)
        E = p.op

        idb = sb("idb", [128, 128])
        onescb = sb("onescb", [128, 1])
        onesrb = sb("onesrb", [1, 128])
        zrowb = sb("zrowb", [1, 512])
        biasr = sb("biasr", [1, 2], F32)
        mlowb = sb("mlowb", [128, 128])
        mupb = sb("mupb", [128, 128])
        mincb = sb("mincb", [128, 128])
        Rsb = sb("Rsb", [80, 1024])
        gb2bc = sb("gb2bc", [128, 20], F32)
        itaubc = sb("itaubc", [128, 20], F32)
        facbc = sb("facbc", [128, 20], F32)
        epsbc = sb("epsbc", [128, 20], F32)
        row20 = sb("row20", [1, 4, 20], F32)
        rowsml = sb("rowsml", [1, 32], F32)
        rowb16 = sb("rowb16", [1, 4, 20])
        ones5v = sb("ones5v", [1, 5], F32)

        pbig = psum("pbig", [128, 512])
        ptr = psum("ptr", [128, 512], BF16)
        pw1 = psum("pw1", [128, 512])
        pw2 = psum("pw2", [128, 512])
        pw3 = psum("pw3", [128, 512])
        psml = psum("psml", [128, 512])

        block = es.enter_context(nc.Block())

        E("Y", lambda h: h.dma_start(out=biasr[:], in_=hb["biasrow"][:]), inc=16)
        for buf, src in ((idb, hb["ident"]), (onescb, hb["onesc"]), (onesrb, hb["onesr"]),
                         (zrowb, hb["zrow"]), (mlowb, hb["mlow"]), (mupb, hb["mup"]),
                         (mincb, hb["minc"])):
            E("G", lambda h, b=buf, s=src: h.dma_start(out=b[:], in_=s[:]), inc=16)
        E("Y", lambda h: h.dma_start(out=row20[0:1, 0, :], in_=hb["gb2r"][:]), inc=16)
        E("Y", lambda h: h.dma_start(out=rowsml[0:1, 0:4], in_=hb["ltau"][:]), inc=16)
        E("Y", lambda h: h.dma_start(out=rowsml[0:1, 4:24], in_=hb["epsr"][:]), inc=16)
        E("Y", lambda h: h.dma_start(out=ones5v[:], in_=hb["ones5"][:]), inc=16)
        consts_g = p.count["G"]
        consts_y = p.count["Y"]

        E("S", lambda h: h.activation(rowsml[0:1, 24:28], rowsml[0:1, 0:4], AF.Abs),
          deps=[("Y", consts_y)])
        E("S", lambda h: h.activation(rowsml[0:1, 24:28], rowsml[0:1, 24:28], AF.Exp, scale=-1.0))
        E("S", lambda h: h.activation(rowsml[0:1, 24:28], rowsml[0:1, 24:28], AF.Ln, bias=1.0))
        E("S", lambda h: h.activation(rowsml[0:1, 28:32], rowsml[0:1, 0:4], AF.Relu))
        E("V", lambda h: h.tensor_add(rowsml[0:1, 24:28], rowsml[0:1, 24:28], rowsml[0:1, 28:32]), deps=["S"])
        E("V", lambda h: h.tensor_scalar(rowsml[0:1, 24:28], rowsml[0:1, 24:28], 0.5, None, ALU.add))
        E("V", lambda h: h.reciprocal(rowsml[0:1, 24:28], rowsml[0:1, 24:28]))
        E("S", lambda h: h.activation(row20[0:1, 3, :], rowsml[0:1, 4:24], AF.Sigmoid,), deps=["V"])
        E("S", lambda h: h.activation(row20[0:1, 3, :], row20[0:1, 3, :], AF.Identity, scale=0.2))
        E("V", lambda h: h.reduce_sum(rowsml[0:1, 28:32], row20[0:1, 3, :].rearrange("o (a b) -> o a b", b=5),
                                      axis=AX), deps=["S"])
        E("V", lambda h: h.tensor_scalar(rowsml[0:1, 28:32], rowsml[0:1, 28:32], -1.0, 1.0, ALU.mult, ALU.add))
        for hh in range(4):
            E("V", lambda h, hh=hh: h.tensor_scalar(
                row20[0:1, 1, 5 * hh:5 * hh + 5], ones5v[0:1, :], rowsml[0:1, 24 + hh:25 + hh],
                None, ALU.mult))
            E("V", lambda h, hh=hh: h.tensor_scalar(
                row20[0:1, 2, 5 * hh:5 * hh + 5], ones5v[0:1, :], rowsml[0:1, 28 + hh:29 + hh],
                None, ALU.mult))
        E("V", lambda h: h.tensor_copy(rowb16[:], row20[:]))
        rows_ready = p.count["V"]
        for i, dst in enumerate((gb2bc, itaubc, facbc, epsbc)):
            E("T", lambda h, i=i: h.matmul(psml[:, 0:20], onesrb[:], rowb16[0:1, i, :],
                                           start=True, stop=True),
              deps=[("V", rows_ready), ("G", consts_g), "V"])
            E("V", lambda h, dst=dst: h.tensor_copy(dst[:], psml[:, 0:20]), deps=["T"])

        selRb = sb("selRb", [128, 2, 80])
        g1rb = sb("g1rb", [128, 2, 1024])
        for t in range(40):
            d = t % 2
            E("G", lambda h, t=t, d=d: h.dma_start(out=selRb[:, d, :], in_=hb["selR"][t]),
              deps=[("T", p.count["T"])], inc=16)
            E("G", lambda h, t=t, d=d: h.dma_start(out=g1rb[:, d, :], in_=hb["gW1r"][128 * t:128 * t + 128, :]),
              inc=16)
            for half in range(2):
                E("T", lambda h, t=t, d=d, half=half: h.matmul(
                    (pw1 if half == 0 else pw2)[0:80, 0:512], selRb[:, d, :],
                    g1rb[:, d, 512 * half:512 * half + 512],
                    start=(t == 0), stop=(t == 39), skip_group_check=True), deps=["G"])
        Rq = sb("Rq", [80, 1024], F32)
        E("V", lambda h: h.tensor_copy(Rq[:, 0:512], pw1[0:80, :]), deps=["T"])
        E("V", lambda h: h.tensor_copy(Rq[:, 512:1024], pw2[0:80, :]))
        E("Y", lambda h: h.dma_start(out=ds["R_in"][:], in_=Rq[:]), deps=["V"], inc=16)
        E("G", lambda h: h.collective_compute(
            "AllReduce", ALU.add, replica_groups=QUADS,
            ins=[ds["R_in"]], outs=[ds["R_out"]]),
          deps=[("Y", p.count["Y"])], inc=0, cinc=1)
        rc_R = p.ccount
        E("G", lambda h: h.dma_start(out=Rsb[:], in_=ds["R_out"][:]), deps=[("C", rc_R)], inc=16)
        Rsb_g = p.count["G"]

        with contextlib.ExitStack() as es1:
            def sb1(n, s, d=BF16):
                return es1.enter_context(nc.sbuf_tensor(n, s, d))
            Sp = [es1.enter_context(nc.psum_tensor("Sp%d" % i, [128, 512], F32)) for i in range(2)]
            Wqb = sb1("Wqb", [128, 8, 512])
            Wkb = sb1("Wkb", [128, 8, 512])
            Wvb = sb1("Wvb", [128, 8, 512])
            Wbb = sb1("Wbb", [128, 8, 2])
            cstb = sb1("cstb", [128, 3, 4, 4, 128])
            rwTf = sb1("rwTf", [128, 8, 8], F32)
            rlstg = sb1("rlstg", [128, 4, 8], F32)
            E("G", lambda h: h.dma_start(out=Wqb[:], in_=hb["Wq"].rearrange("(a p) c -> p a c", p=128)), inc=16)
            E("G", lambda h: h.dma_start(out=Wkb[:], in_=hb["Wk"].rearrange("(a p) c -> p a c", p=128)), inc=16)
            E("G", lambda h: h.dma_start(out=Wvb[:], in_=hb["Wv"].rearrange("(a p) c -> p a c", p=128)), inc=16)
            E("G", lambda h: h.dma_start(out=Wbb[:], in_=hb["Wb2"].rearrange("(a p) c -> p a c", p=128)), inc=16)
            E("G", lambda h: h.dma_start(out=cstb[:], in_=hb["convST"].rearrange("a t c p m -> p a t c m")), inc=16)
            selstatb = sb1("selstatb", [128, 2, 15, 40])
            E("G", lambda h: h.dma_start(out=selstatb[:], in_=hb["selstat"].rearrange("a s p c -> p a s c")), inc=16)
            consts_g = p.count["G"]
            E("Y", lambda h: h.dma_start(out=rwTf[:], in_=hb["rwT"].rearrange("(a p) c -> p a c", p=128)), inc=16)
            xbf = sb1("xbf", [128, 1, 8, 512])
            sqx = sb1("sqx", [128, 2, 512])
            lnrow = sb1("lnrow", [1, 3, 512], F32)
            lnb16 = sb1("lnb16", [1, 2, 512])
            xn_sb = sb1("xn_sb", [128, 8, 512])
            tmpf = sb1("tmpf", [128, 2, 512], F32)
            rq = sb1("rq", [128, 4, 520])
            rk = sb1("rk", [128, 4, 520])
            rv = sb1("rv", [128, 4, 520])
            qkraw = sb1("qkraw", [128, 4, 512])
            qkraw2 = sb1("qkraw2", [128, 4, 512])
            qh_sb = sb1("qh_sb", [128, 4, 512])
            kh_sb = sb1("kh_sb", [128, 4, 512])
            bT = sb1("bT", [1, 1, 512], F32)
            bb16 = sb1("bb16", [1, 2, 512])
            bcol = sb1("bcol", [128, 2, 4], F32)
            rsqb = sb1("rsqb", [1, 1, 512])
            kbg = [sb1("kbg%d" % i, [128, 2, 128]) for i in range(2)]
            At = [sb1("At%d" % i, [128, 128]) for i in range(2)]
            AtT = [sb1("AtT%d" % i, [128, 128]) for i in range(2)]
            locT = [sb1("locT%d" % i, [128, 32]) for i in range(2)]
            sqA = [sb1("sqA%d" % i, [128, 3, 128]) for i in range(2)]
            sqAT = [sb1("sqAT%d" % i, [128, 4, 128]) for i in range(2)]
            X0 = [sb1("X0%d" % i, [128, 512]) for i in range(2)]
            X1 = [sb1("X1%d" % i, [128, 512]) for i in range(2)]
            wT = [sb1("wT%d" % i, [128, 2, 128]) for i in range(2)]
            kts = [sb1("kts%d" % i, [128, 256]) for i in range(2)]
            Sbf = sb1("Sbf", [128, 2, 512])
            u_sb = [sb1("u_sb%d" % i, [128, 256]) for i in range(2)]
            osg = [sb1("osg%d" % i, [128, 256]) for i in range(2)]
            vfir = sb1("vfir", [128, 4, 1096])
            vdup = sb1("vdup", [128, 2, 1092])
            STfb = sb1("STfb", [128, 3, 24, 128])
            fsb = {"fs": sb1("fssb", [128, 4, 1024]), "fm": sb1("fmsb", [128, 4, 1024]),
                   "fl": sb1("flsb", [128, 4, 1024])}
            od_win = sb1("od_win", [128, 4, 1024])
            kscr = sb1("kscr", [128, 6, 512])
            stk = sb1("stk", [128, 5, 512])
            stag40 = sb1("stag40", [128, 40])
            statstg = sb1("statstg", [40, 512])

            E("T", lambda h: h.matmul(Sp[0][:, :], onesrb[:], zrowb[:], start=True, stop=False),
              deps=[("G", consts_g)])
            E("T", lambda h: h.matmul(Sp[1][:, :], onesrb[:], zrowb[:], start=True, stop=False))
            for ring in (rq, rk, rv):
                E("V", lambda h, r=ring: h.memset(r[:, :, 0:8], 0.0))
            E("V", lambda h: h.memset(vfir[:, :, 0:64], 0.0))
            E("V", lambda h: h.memset(vfir[:, :, 1088:1096], 0.0))

            xr = hb["x"].rearrange("(a p) l -> p a l", p=128)
            fir_pending = []
            odspill_y = [0]
            stats_spill_y = [0]
            fsb_spill_y = [0]
            fir_guard = [0, 0]
            fir_guard = [0, 0]
            fir_staged = [0]
            fir_mm_done = {}
            fir_stage_cnt = {}
            fir_bank_free = {0: (0, 0), 1: (0, 0)}
            fir_nm_ctr = [0]

            def stage_fir(grp):
                gl = grp % 16
                gp2 = grp % 2
                sl3 = grp % 3
                ct = gl // 4
                rw = 32 * (gl % 4)
                E("G", lambda h, gl=gl, sl3=sl3: h.dma_start(
                    out=STfb[:, sl3, :, :], in_=hb["STf"][gl].rearrange("G p m -> p G m")),
                  deps=[("T", fir_mm_done.get(grp - 3, 0))], inc=16)
                tdep = ("T", fir_mm_done.get(grp - 2, 0))
                for r, eng in enumerate(("V", "S", "V", "S")):
                    if eng == "V":
                        E("V", lambda h, r=r, rw=rw, ct=ct, gp2=gp2: h.tensor_copy(
                            vdup[32 * r:32 * r + 32, gp2, :], vfir[rw:rw + 32, ct, 1 + r:1093 + r]),
                          deps=[tdep, ("S", p.count["S"])])
                    else:
                        E("S", lambda h, r=r, rw=rw, ct=ct, gp2=gp2: h.activation(
                            vdup[32 * r:32 * r + 32, gp2, :], vfir[rw:rw + 32, ct, 1 + r:1093 + r],
                            AF.Identity), deps=[tdep])
                fir_stage_cnt[grp] = (p.count["G"], p.count["V"], p.count["S"])

            def mm_fir(grp):
                lim = (grp // 16) * 16 + 16
                while fir_staged[0] < lim and fir_staged[0] <= grp + 1:
                    stage_fir(fir_staged[0])
                    fir_staged[0] += 1
                gp2 = grp % 2
                sl3 = grp % 3
                ct = (grp % 16) // 4
                rw = 32 * ((grp % 16) % 4)
                sg, sv, ss = fir_stage_cnt[grp]
                stoff = 0
                for nm, K in FIRS:
                    bi = fir_nm_ctr[0] % 2
                    fir_nm_ctr[0] += 1
                    bank = (pw1, pbig)[bi]
                    bv, bs = fir_bank_free[bi]
                    NG_ = (K + 2) // 4 + 1
                    for G in range(NG_):
                        off = 4 * G + (64 - K)
                        E("T", lambda h, gp2=gp2, bank=bank, so=stoff + G, off=off,
                          s0=(G == 0), s1=(G == NG_ - 1): h.matmul(
                            bank[:, 0:256], STfb[:, sl3, so, :],
                            vdup[:, gp2, off:off + 1024:4],
                            start=s0, stop=s1, skip_group_check=True),
                          deps=([("V", max(sv, bv, fir_guard[0])),
                                 ("S", max(ss, bs, fir_guard[1])), ("G", sg)] if G == 0 else []))
                    stoff += NG_
                    for dd in range(4):
                        if dd % 2 == 0:
                            E("V", lambda h, nm=nm, bank=bank, dd=dd, rw=rw, ct=ct: h.tensor_copy(
                                fsb[nm][rw:rw + 32, ct, dd:1024:4], bank[32 * dd:32 * dd + 32, 0:256]),
                              deps=["T", ("Y", fsb_spill_y[0])])
                        else:
                            E("S", lambda h, nm=nm, bank=bank, dd=dd, rw=rw, ct=ct: h.activation(
                                fsb[nm][rw:rw + 32, ct, dd:1024:4], bank[32 * dd:32 * dd + 32, 0:256],
                                AF.Identity), deps=["T", ("Y", fsb_spill_y[0])])
                    fir_bank_free[bi] = (p.count["V"], p.count["S"])
                fir_mm_done[grp] = p.count["T"]

            for t in range(NT):
                d = t % 2
                sl = slice(TILE * t, TILE * (t + 1))
                g_x = E("G", lambda h, sl=sl: h.dma_start(out=xbf[:, 0, :, :], in_=xr[:, :, sl]),
                        deps=[("S", p.count["S"]), ("T", p.count["T"]), ("V", p.count["V"])], inc=16)
                E("T", lambda h: h.matmul(psml[:, :], onesrb[:], zrowb[:], start=True, stop=False),
                  deps=["V", "S"])
                for dt in range(8):
                    if dt % 2 == 0:
                        E("S", lambda h, dt=dt: h.activation(sqx[:, 0, :], xbf[:, 0, dt, :], AF.Square),
                          deps=[("G", g_x), ("T", p.count["T"])])
                    else:
                        E("V", lambda h, dt=dt: h.tensor_mul(sqx[:, 1, :], xbf[:, 0, dt, :],
                                                             xbf[:, 0, dt, :]),
                          deps=[("G", g_x), ("T", p.count["T"])])
                    E("T", lambda h, dt=dt: h.matmul(psml[0:1, 0:512], onescb[:], sqx[:, dt % 2, :],
                                                     start=False, stop=False, skip_group_check=True),
                      deps=["S" if dt % 2 == 0 else "V"])
                    E("T", lambda h, dt=dt: h.matmul(psml[32:33, 0:512], onescb[:], xbf[:, 0, dt, :],
                                                          start=False, stop=False, skip_group_check=True))
                E("V", lambda h: h.tensor_scalar(lnrow[0:1, 0, :], psml[32:33, 0:512], 1.0 / 1024, None, ALU.mult),
                  deps=["T"])
                E("V", lambda h: h.tensor_scalar(lnrow[0:1, 1, :], psml[0:1, 0:512], 1.0 / 1024, None, ALU.mult))
                E("S", lambda h: h.activation(lnrow[0:1, 2, :], lnrow[0:1, 0, :], AF.Square), deps=["V"])
                E("V", lambda h: h.tensor_sub(lnrow[0:1, 1, :], lnrow[0:1, 1, :], lnrow[0:1, 2, :]), deps=["S"])
                E("V", lambda h: h.tensor_scalar(lnrow[0:1, 1, :], lnrow[0:1, 1, :], 1e-5, None, ALU.add))
                E("S", lambda h: h.activation(lnrow[0:1, 1, :], lnrow[0:1, 1, :], AF.Sqrt), deps=["V"])
                E("V", lambda h: h.reciprocal(lnrow[0:1, 1, :], lnrow[0:1, 1, :]), deps=["S"])
                E("V", lambda h: h.tensor_mul(lnrow[0:1, 2, :], lnrow[0:1, 0, :], lnrow[0:1, 1, :]))
                E("V", lambda h: h.tensor_scalar(lnrow[0:1, 2, :], lnrow[0:1, 2, :], -1.0, None, ALU.mult))
                E("V", lambda h: h.tensor_copy(lnb16[0:1, 0, :], lnrow[0:1, 1, :]), deps=["T"])
                E("V", lambda h: h.tensor_copy(lnb16[0:1, 1, :], lnrow[0:1, 2, :]))
                E("T", lambda h: h.matmul(pw2[:, :], onesrb[:], lnb16[0:1, 0, :], start=True, stop=True),
                  deps=["V"])
                E("T", lambda h: h.matmul(pw3[:, :], onesrb[:], lnb16[0:1, 1, :], start=True, stop=True))
                E("T", lambda h: h.matmul(pbig[:, 0:32], onesrb[:], zrowb[0:1, 0:32],
                                          start=True, stop=False), deps=["V", "S"])
                bcast_t = p.count["T"]
                tmpf_war = {0: bcast_t, 1: bcast_t}
                for dt in range(8):
                    sl2 = dt % 2
                    E("V", lambda h, dt=dt, sl2=sl2: h.tensor_mul(tmpf[:, sl2, :], xbf[:, 0, dt, :], pw2[:, :]),
                      deps=([("T", bcast_t), ("Y", p.count["Y"])] if dt == 0
                            else [("T", tmpf_war[sl2])]))
                    E("V", lambda h, sl2=sl2: h.tensor_add(tmpf[:, sl2, :], tmpf[:, sl2, :], pw3[:, :]))
                    vdone = p.count["V"]
                    E("S", lambda h, dt=dt, sl2=sl2: h.activation(xn_sb[:, dt, :], tmpf[:, sl2, :],
                                                                  AF.Identity), deps=[("V", vdone)])
                    for blk in range(4):
                        E("T", lambda h, sl2=sl2, dt=dt, blk=blk: h.matmul(
                            pbig[:, 8 * blk:8 * blk + 8], tmpf[:, sl2, 128 * blk:128 * blk + 128],
                            rwTf[:, dt, :], start=False, stop=False,
                            skip_group_check=True), deps=([("V", vdone)] if blk == 0 else []))
                    tmpf_war[sl2] = p.count["T"]
                E("V", lambda h: h.tensor_copy(rlstg[:, :, :].rearrange("p a c -> p (a c)"), pbig[:, 0:32]),
                  deps=["T"])
                E("Y", lambda h, t=t: h.dma_start(
                    out=ds["rl_sp"].rearrange("(t b p) c -> p t b c", p=128, b=4)[:, t, :, :],
                    in_=rlstg[:, :, :]), deps=["V"], inc=16)
                xn_s = p.count["S"]
                E("Y", lambda h, sl=sl: h.dma_start(
                    out=ds["xn_sp"].rearrange("(a p) l -> p a l", p=128)[:, :, sl], in_=xn_sb[:, :, :]),
                  deps=[("S", xn_s)], inc=16)
                for hh in range(2):
                    for dt in range(8):
                        E("T", lambda h, hh=hh, dt=dt: h.matmul(
                            psml[0:1, 0:512], Wbb[:, dt, hh:hh + 1], xn_sb[:, dt, :],
                            start=(dt == 0), stop=(dt == 7), skip_group_check=True),
                          deps=(["V", ("S", xn_s)] if dt == 0 else []))
                    E("S", lambda h: h.activation(bT[0:1, 0, :], psml[0:1, 0:512], AF.Tanh, scale=0.5),
                      deps=["T"])
                    E("V", lambda h: h.tensor_scalar(bT[0:1, 0, :], bT[0:1, 0, :], 0.5, 0.5,
                                                     ALU.mult, ALU.add), deps=["S"])
                    E("V", lambda h, hh=hh: h.tensor_copy(bb16[0:1, hh, :], bT[0:1, 0, :]), deps=["T"])
                    for blk in range(4):
                        E("T", lambda h, hh=hh, blk=blk: h.transpose(
                            ptr[:, 0:1], bb16[0:1, hh, 128 * blk:128 * blk + 128], idb[0:1, 0:1]),
                          deps=["V"])
                        E("V", lambda h, hh=hh, blk=blk: h.tensor_copy(bcol[:, hh, blk:blk + 1], ptr[:, 0:1]),
                          deps=["T", "S"])
                l2n_pending = []

                def l2n_flush():
                    while l2n_pending:
                        dsth_, rawb_, hh, rowb, mmt = l2n_pending.pop(0)
                        E("S", lambda h, rowb=rowb: h.activation(
                            lnrow[0:1, 0, :], psml[rowb:rowb + 1, 0:512], AF.Sqrt, bias=biasr[0:1, 0:1]),
                          deps=[("T", mmt), ("V", p.count["V"])])
                        E("V", lambda h: h.reciprocal(lnrow[0:1, 0, :], lnrow[0:1, 0, :]), deps=["S"])
                        E("V", lambda h: h.tensor_copy(rsqb[0:1, 0, :], lnrow[0:1, 0, :]), deps=["T"])
                        E("T", lambda h: h.matmul(
                            pw2[:, :], onesrb[:], rsqb[0:1, 0, :], start=True, stop=True),
                          deps=["V", "S"])
                        tt = p.count["T"]
                        for ci in range(2):
                            ct = 2 * hh + ci
                            E("V", lambda h, dsth_=dsth_, rawb_=rawb_, ct=ct: h.tensor_mul(
                                dsth_[:, ct, :], rawb_[:, ct, :], pw2[:, :]), deps=[("T", tt)])
                for ai, (Wsb, ring, dsth) in enumerate(((Wqb, rq, qh_sb), (Wkb, rk, kh_sb), (Wvb, rv, None))):
                    rawb = qkraw if ai == 0 else qkraw2
                    for ct in range(4):
                        ring_g = 0
                        if t > 0:
                            ring_g = E("G", lambda h, r=ring, ct=ct: h.tensor_copy(
                                r[:, ct, 0:8], r[:, ct, 512:520]), deps=["T", "S"])
                        for dt in range(8):
                            E("T", lambda h, Wsb=Wsb, ct=ct, dt=dt: h.matmul(
                                pw1[:, :], Wsb[:, dt, 128 * ct:128 * ct + 128], xn_sb[:, dt, :],
                                start=(dt == 0), stop=(dt == 7)),
                              deps=(["V", "S"] if dt == 0 else []))
                        E("S", lambda h, r=ring, ct=ct: h.activation(r[:, ct, 8:520], pw1[:, :], AF.Identity),
                          deps=["T", ("G", ring_g)])
                        for jt in range(4):
                            E("T", lambda h, ai=ai, jt=jt, ct=ct, r=ring: h.matmul(
                                pw2[:, :], cstb[:, ai, jt, ct, :], r[:, ct, 5 + jt:5 + jt + 512],
                                start=(jt == 0), stop=(jt == 3)),
                              deps=(["V", "S"] if jt == 0 else []))
                        if dsth is None:
                            E("S", lambda h, ct=ct, t=t: h.activation(
                                vfir[:, ct, 64 + 512 * (t % 2):64 + 512 * (t % 2) + 512],
                                pw2[:, :], AF.Silu), deps=["T", "V", "G", ("Y", p.count["Y"])])
                        else:
                            E("S", lambda h, rawb=rawb, ct=ct: h.activation(rawb[:, ct, :], pw2[:, :],
                                                                            AF.Silu), deps=["T", "V"])
                    if dsth is None:
                        l2n_flush()
                        continue
                    l2n_flush()
                    for hh in range(2):
                        rowb = (0, 32, 64, 32)[2 * ai + hh]
                        for ci in range(2):
                            ct = 2 * hh + ci
                            E("S", lambda h, rawb=rawb, ct=ct, sl2=(2 * hh + ci) % 2: h.activation(
                                sqx[:, sl2, :], rawb[:, ct, :], AF.Square),
                              deps=[("T", p.count["T"]), ("V", p.count["V"])])
                            E("T", lambda h, rowb=rowb, sl2=(2 * hh + ci) % 2, s0=(ci == 0): h.matmul(
                                psml[rowb:rowb + 1, 0:512], onescb[:], sqx[:, sl2, :],
                                start=s0, stop=False, skip_group_check=True), deps=["S"])
                        l2n_pending.append((dsth, rawb, hh, rowb, p.count["T"]))
                if t % 2 == 1:
                    base16 = (t // 2) * 16
                    fir_pending.extend(range(base16, base16 + 16))
                for g in range(4):
                    ga = t * 4 + g
                    gsl = slice(128 * g, 128 * (g + 1))
                    pmm = [pbig, pw1]
                    pap = [pw2, psml]
                    ptrs = [0]
                    ptr_free = [("V", 0)]

                    def ptr_slot():
                        s = ptrs[0] % 4
                        ptrs[0] += 1
                        return s, ptr[:, 128 * s:128 * s + 128]
                    tc = [p.count["T"], p.count["T"]]
                    vc = [p.count["V"], p.count["V"]]
                    voff = 64 + 512 * (t % 2) + 128 * g
                    for hh in range(2):
                        tc[hh] = E("T", lambda h, hh=hh, gsl=gsl: h.matmul(
                            pmm[hh][:, 0:128], onesrb[:], bb16[0:1, hh, gsl], start=True, stop=True),
                            deps=["V", "S"])
                    for hh in range(2):
                        for ci, ct in enumerate((2 * hh, 2 * hh + 1)):
                            E("V", lambda h, hh=hh, ci=ci, ct=ct, gsl=gsl: h.tensor_mul(
                                kbg[hh][:, ci, :], kh_sb[:, ct, gsl], pmm[hh][:, 0:128]),
                              deps=[("T", tc[hh])])
                        vc[hh] = p.count["V"]
                    for hh in range(2):
                        for ci, ct in enumerate((2 * hh, 2 * hh + 1)):
                            E("T", lambda h, hh=hh, ci=ci, ct=ct, gsl=gsl: h.matmul(
                                pmm[hh][:, 0:128], kbg[hh][:, ci, :], kh_sb[:, ct, gsl],
                                start=(ci == 0), stop=(ci == 1), skip_group_check=True),
                              deps=[("V", vc[hh])])
                        for ci, ct in enumerate((2 * hh, 2 * hh + 1)):
                            E("T", lambda h, hh=hh, ci=ci, ct=ct, gsl=gsl: h.matmul(
                                pmm[hh][:, 128:256], kh_sb[:, ct, gsl], kbg[hh][:, ci, :],
                                start=(ci == 0), stop=(ci == 1), skip_group_check=True))
                        for ci, ct in enumerate((2 * hh, 2 * hh + 1)):
                            E("T", lambda h, hh=hh, ci=ci, ct=ct, gsl=gsl: h.matmul(
                                pmm[hh][:, 256:384], kh_sb[:, ct, gsl], qh_sb[:, ct, gsl],
                                start=(ci == 0), stop=(ci == 1), skip_group_check=True))
                        tc[hh] = p.count["T"]
                    for hh in range(2):
                        E("V", lambda h, hh=hh: h.tensor_mul(At[hh][:, :], pmm[hh][:, 0:128], mlowb[:, :]),
                          deps=[("T", tc[hh])])
                        E("V", lambda h, hh=hh: h.tensor_mul(AtT[hh][:, :], pmm[hh][:, 128:256], mupb[:, :]))
                        for r in range(4):
                            rs = slice(32 * r, 32 * r + 32)
                            E("V", lambda h, hh=hh, rs=rs: h.tensor_mul(
                                locT[hh][rs, :], pmm[hh][rs, 256 + rs.start:256 + rs.start + 32],
                                mincb[rs, rs]))
                        vc[hh] = p.count["V"]
                    def apg(hh, x):
                        if isinstance(x, tuple):
                            return x[0][hh][:, x[1], :]
                        return x[hh][:, :]
                    xb_units = []
                    for hh in range(2):
                        for ci in range(2):
                            xb_units.append(("v", hh, ci))
                            xb_units.append(("k", hh, ci))

                    def emit_xb(unit, voff=voff, g=g, gsl=gsl):
                        kind, hh, ci = unit
                        ct = 2 * hh + ci
                        s, pv = ptr_slot()
                        if kind == "v":
                            E("T", lambda h, ct=ct, pv=pv, voff=voff: h.transpose(
                                pv, vfir[:, ct, voff:voff + 128], idb[:, :]),
                              deps=[ptr_free[0], ("S", p.count["S"])])
                            tt = p.count["T"]
                            E("S", lambda h, hh=hh, ci=ci, pv=pv, g=g: h.activation(
                                X0[hh][:, 128 * ci:128 * ci + 128], pv, AF.Identity,
                                scale=bcol[:, hh, g:g + 1]), deps=[("T", tt)])
                            ptr_free[0] = ("S", p.count["S"])
                        else:
                            E("T", lambda h, ct=ct, pv=pv, gsl=gsl: h.transpose(
                                pv, kh_sb[:, ct, gsl], idb[:, :]),
                              deps=[ptr_free[0]])
                            tt = p.count["T"]
                            E("V", lambda h, hh=hh, ci=ci, pv=pv: h.tensor_copy(
                                kts[hh][:, 128 * ci:128 * ci + 128], pv), deps=[("T", tt)])
                            E("V", lambda h, hh=hh, ci=ci, pv=pv, g=g: h.tensor_scalar(
                                X0[hh][:, 256 + 128 * ci:256 + 128 * ci + 128], pv, bcol[:, hh, g:g + 1],
                                None, ALU.mult))
                            ptr_free[0] = ("V", p.count["V"])
                    xb_i = 0
                    sc_hh = [p.count["S"], p.count["S"]]
                    sqlv = (((AtT, At, (sqA, 0)), (At, AtT, (sqAT, 0))),
                            (((sqAT, 0), (sqA, 0), (sqA, 1)), ((sqA, 0), (sqAT, 0), (sqAT, 1))),
                            (((sqAT, 1), (sqA, 1), (sqA, 2)), ((sqA, 1), (sqAT, 1), (sqAT, 2))),
                            (((sqA, 2), (sqAT, 2), (sqAT, 3)),))
                    for level in sqlv:
                        for _ in range(2):
                            if xb_i < len(xb_units):
                                emit_xb(xb_units[xb_i])
                                xb_i += 1
                        for hh in range(2):
                            for li, (lhsT, rhs, dstx) in enumerate(level):
                                E("T", lambda h, hh=hh, a=lhsT, b=rhs, li=li: h.matmul(
                                    pmm[hh][:, 128 * li:128 * li + 128], apg(hh, a), apg(hh, b),
                                    start=True, stop=True, skip_group_check=True),
                                  deps=([("V", vc[hh]), ("S", sc_hh[hh])] if li == 0 else []))
                            tc[hh] = p.count["T"]
                        for hh in range(2):
                            for li, (lhsT, rhs, dstx) in enumerate(level):
                                if li % 2 == 0:
                                    E("S", lambda h, hh=hh, dstx=dstx, li=li: h.activation(
                                        apg(hh, dstx), pmm[hh][:, 128 * li:128 * li + 128], AF.Identity),
                                      deps=[("T", tc[hh])])
                                else:
                                    E("V", lambda h, hh=hh, dstx=dstx, li=li: h.tensor_copy(
                                        apg(hh, dstx), pmm[hh][:, 128 * li:128 * li + 128]),
                                      deps=[("T", tc[hh])])
                            vc[hh] = p.count["V"]
                            sc_hh[hh] = p.count["S"]
                    while xb_i < len(xb_units):
                        emit_xb(xb_units[xb_i])
                        xb_i += 1
                    xb_done = p.count["V"]
                    fir_guard[0] = p.count["V"]
                    fir_guard[1] = p.count["S"]
                    xb_done_s = p.count["S"]
                    fir_guard[0] = p.count["V"]
                    fir_guard[1] = p.count["S"]
                    xcur = [X0[0], X0[1]]
                    xnxt = [X1[0], X1[1]]
                    for lvl in ((sqAT, 3), (sqAT, 2), (sqAT, 1), (sqAT, 0), AtT):
                        for hh in range(2):
                            tc[hh] = E("T", lambda h, hh=hh, lvl=lvl, xc=xcur[hh]: h.matmul(
                                pap[hh][:, :], apg(hh, lvl), xc[:, :], start=True, stop=True,
                                skip_group_check=True), deps=[("V", vc[hh]), ("V", xb_done),
                                                              ("S", sc_hh[hh]),
                                                              ("S", xb_done_s)])
                        for hh in range(2):
                            E("V", lambda h, hh=hh, xc=xcur[hh], xn_=xnxt[hh]: h.tensor_add(
                                xn_[:, :], xc[:, :], pap[hh][:, :]), deps=[("T", tc[hh])])
                            vc[hh] = p.count["V"]
                        xcur, xnxt = xnxt, xcur
                    Xf = xcur
                    for hh in range(2):
                        for ci in range(2):
                            s, pv = ptr_slot()
                            E("T", lambda h, hh=hh, ci=ci, pv=pv, Xf=Xf: h.transpose(
                                pv, Xf[hh][:, 256 + 128 * ci:256 + 128 * ci + 128], idb[:, :]),
                              deps=[ptr_free[0], ("V", vc[hh])])
                            tt = p.count["T"]
                            E("V", lambda h, hh=hh, ci=ci, pv=pv: h.tensor_copy(wT[hh][:, ci, :], pv),
                              deps=[("T", tt)])
                            ptr_free[0] = ("V", p.count["V"])
                        vc[hh] = p.count["V"]
                    ureg = [pw3[0:32, 0:256], pw2[0:32, 0:256]]
                    oreg = [pw3[32:64, 0:256], pw2[32:64, 0:256]]
                    vall = p.count["V"]
                    for r in range(4):
                        gc = ga * 4 + r
                        rsl = slice(32 * r, 32 * r + 32)
                        tp96 = (32 * r, 0)
                        sbf_c = [None, None]
                        if gc > 0:
                            E("V", lambda h: h.tensor_copy(Sbf[:, 0, :], Sp[0][:, :]),
                              deps=[("T", tc[0]), ("V", vall)])
                            sbf_c[0] = ("V", p.count["V"])
                            E("S", lambda h: h.activation(Sbf[:, 1, :], Sp[1][:, :], AF.Identity),
                              deps=[("T", tc[1])])
                            sbf_c[1] = ("S", p.count["S"])
                            ut = [0, 0]
                            for hh in range(2):
                                for ci in range(2):
                                    E("T", lambda h, hh=hh, ci=ci, rsl=rsl: h.matmul(
                                        ureg[hh], wT[hh][:, ci, rsl], Sbf[:, hh, 256 * ci:256 * ci + 256],
                                        start=(ci == 0), stop=(ci == 1), skip_group_check=True),
                                      deps=[sbf_c[hh]])
                                ut[hh] = p.count["T"]
                            for hh in range(2):
                                E("V", lambda h, hh=hh, rsl=rsl, Xf=Xf: h.tensor_sub(
                                    u_sb[hh][rsl, :], Xf[hh][rsl, 0:256], ureg[hh]),
                                  deps=[("T", ut[hh])])
                                vc[hh] = p.count["V"]
                            ot = [0, 0]
                            for hh in range(2):
                                for ci, ct in enumerate((2 * hh, 2 * hh + 1)):
                                    E("T", lambda h, hh=hh, ct=ct, ci=ci, g=g, r=r: h.matmul(
                                        oreg[hh], qh_sb[:, ct, 128 * g + 32 * r:128 * g + 32 * r + 32],
                                        Sbf[:, hh, 256 * ci:256 * ci + 256], start=(ci == 0), stop=False,
                                        skip_group_check=True), deps=[("V", vc[hh])])
                                E("T", lambda h, hh=hh, rsl=rsl, r=r: h.matmul(
                                    oreg[hh], locT[hh][rsl, :], u_sb[hh][rsl, :],
                                    start=False, stop=True, skip_group_check=True,
                                    tile_position=(32 * r, 32)))
                                ot[hh] = p.count["T"]
                        else:
                            for hh in range(2):
                                E("V", lambda h, hh=hh, rsl=rsl, Xf=Xf: h.tensor_copy(
                                    u_sb[hh][rsl, :], Xf[hh][rsl, 0:256]), deps=[("T", tc[hh])])
                                vc[hh] = p.count["V"]
                            ot = [0, 0]
                            for hh in range(2):
                                E("T", lambda h, hh=hh, rsl=rsl, r=r: h.matmul(
                                    oreg[hh], locT[hh][rsl, :], u_sb[hh][rsl, :],
                                    start=True, stop=True, skip_group_check=True,
                                    tile_position=(32 * r, 32)), deps=[("V", vc[hh]), ("V", vall)])
                                ot[hh] = p.count["T"]
                        for hh in range(2):
                            E("S", lambda h, hh=hh, rsl=rsl: h.activation(osg[hh][rsl, :], oreg[hh],
                                                                          AF.Identity),
                              deps=[("T", ot[hh])])
                        os_s = p.count["S"]
                        for hh in range(2):
                            for ci in range(2):
                                E("T", lambda h, hh=hh, ci=ci, rsl=rsl, tp96=tp96: h.matmul(
                                    Sp[hh][:, 256 * ci:256 * ci + 256],
                                    kts[hh][rsl, 128 * ci:128 * ci + 128],
                                    u_sb[hh][rsl, :], start=False, stop=True,
                                    skip_group_check=True, tile_position=tp96),
                                  deps=[("V", vc[hh]), ("S", os_s)])
                            tc[hh] = p.count["T"]
                        if fir_pending:
                            mm_fir(fir_pending.pop(0))
                    for hh in range(2):
                        for ci in range(2):
                            s, pv = ptr_slot()
                            E("T", lambda h, hh=hh, ci=ci, pv=pv: h.transpose(
                                pv, osg[hh][:, 128 * ci:128 * ci + 128], idb[:, :]),
                              deps=[ptr_free[0], ("V", vc[hh]), ("S", p.count["S"])])
                            tt = p.count["T"]
                            E("V", lambda h, hh=hh, ci=ci, t=t, g=g, pv=pv: h.tensor_copy(
                                od_win[:, 2 * hh + ci, 512 * (t % 2) + 128 * g:512 * (t % 2) + 128 * g + 128],
                                pv), deps=[("T", tt), ("Y", odspill_y[0])])
                            ptr_free[0] = ("V", p.count["V"])
                if t % 2 == 1:
                    w = t // 2
                    wsl = slice(WIN * w, WIN * (w + 1))
                    for grp in fir_pending:
                        mm_fir(grp)
                    fir_pending.clear()
                    for sub in range(2):
                        ssl = slice(512 * sub, 512 * sub + 512)
                        vof = 64 + 512 * sub
                        E("T", lambda h: h.matmul(psml[:, 0:160], onesrb[:], zrowb[0:1, 0:160],
                                                  start=True, stop=False), deps=["V", "S"])

                        def pathf(pi, hh, ci):
                            if pi < 3:
                                return fsb[("fs", "fm", "fl")[pi]][:, 2 * hh + ci, ssl]
                            if pi == 3:
                                return od_win[:, 2 * hh + ci, ssl]
                            return vfir[:, 2 * hh + ci, vof:vof + 512]
                        combos = [(hh, pi, kd) for hh in range(2) for pi in range(P5)
                                  for kd in range(3)]
                        LEAD = 3
                        stk_free = {i: 0 for i in range(5)}
                        kscr_free = {i: 0 for i in range(3)}
                        add_v = {}
                        kp_of = {}
                        kp_ctr = [0]

                        def preproc(idx, pathf=pathf):
                            hh, pi, kd = combos[idx]
                            sslot = idx % 5
                            if kd == 0:
                                a, b = pathf(pi, hh, 0), pathf(pi, hh, 1)
                                E("G", lambda h, a=a, b=b, sslot=sslot: h.tensor_add(
                                    stk[:, sslot, :], a, b),
                                  deps=[("S", p.count["S"]), ("V", p.count["V"]),
                                        ("T", stk_free[sslot])])
                                add_v[idx] = ("G", p.count["G"])
                                return
                            if True:
                                kp = kp_ctr[0] % 3
                                kp_ctr[0] += 1
                                kp_of[idx] = kp
                                af = AF.Square if kd == 1 else AF.Abs
                                for ci in range(2):
                                    E("S", lambda h, pi=pi, hh=hh, ci=ci, kp=kp, af=af, pathf=pathf:
                                      h.activation(kscr[:, 2 * kp + ci, :], pathf(pi, hh, ci), af),
                                      deps=[("T", kscr_free[kp]), ("V", p.count["V"])])
                                a, b = kscr[:, 2 * kp, :], kscr[:, 2 * kp + 1, :]
                                sdep = ("S", p.count["S"])
                            if kd == 2:
                                E("G", lambda h, a=a, b=b, sslot=sslot: h.tensor_add(
                                    stk[:, sslot, :], a, b), deps=[sdep, ("T", stk_free[sslot])])
                                add_v[idx] = ("G", p.count["G"])
                            else:
                                E("V", lambda h, a=a, b=b, sslot=sslot: h.tensor_add(
                                    stk[:, sslot, :], a, b), deps=[sdep, ("T", stk_free[sslot])])
                                add_v[idx] = ("V", p.count["V"])

                        def emit_mm(idx):
                            hh, pi, kd = combos[idx]
                            sslot = idx % 5
                            combo = pi * 3 + kd
                            for blk in range(4):
                                E("T", lambda h, hh=hh, combo=combo, sslot=sslot, blk=blk: h.matmul(
                                    psml[:, 40 * blk:40 * blk + 40],
                                    stk[:, sslot, 128 * blk:128 * blk + 128],
                                    selstatb[:, hh, combo, :], start=False, stop=False,
                                    skip_group_check=True),
                                  deps=([add_v[idx]] if blk == 0 else []))
                            stk_free[sslot] = p.count["T"]
                            if idx in kp_of:
                                kscr_free[kp_of[idx]] = p.count["T"]
                        for idx in range(30):
                            preproc(idx)
                            if idx >= LEAD:
                                emit_mm(idx - LEAD)
                        for idx in range(30 - LEAD, 30):
                            emit_mm(idx)
                        for blk in range(4):
                            E("V", lambda h, blk=blk: h.tensor_scalar(
                                psml[:, 40 * blk:40 * blk + 40].rearrange("p (a b) -> p a b", b=4)[:, :, 0:3],
                                psml[:, 40 * blk:40 * blk + 40].rearrange("p (a b) -> p a b", b=4)[:, :, 0:3],
                                1.0 / 256, None, ALU.mult), deps=["T"])
                            E("S", lambda h, blk=blk: h.activation(
                                psml[:, 40 * blk:40 * blk + 40].rearrange("p (a b) -> p a b", b=4)[:, :, 3:4],
                                psml[:, 40 * blk:40 * blk + 40].rearrange("p (a b) -> p a b", b=4)[:, :, 3:4],
                                AF.Sqrt), deps=["V"])
                            E("V", lambda h, blk=blk: h.tensor_copy(stag40[:, :], psml[:, 40 * blk:40 * blk + 40]),
                              deps=["S", "T"])
                            E("T", lambda h: h.transpose(ptr[0:40, 0:128], stag40[:, :], idb[:, :]), deps=["V"])
                            E("V", lambda h, blk=blk: h.tensor_copy(statstg[:, 128 * blk:128 * blk + 128],
                                                                    ptr[0:40, 0:128]), deps=["T", ("Y", p.count["Y"])])
                        E("Y", lambda h, w=w, sub=sub: h.dma_start(
                            out=ds["stats_in"][:, WIN * w + 512 * sub:WIN * w + 512 * sub + 512],
                            in_=statstg[:, :]), deps=["V"], inc=16)
                    E("Y", lambda h, wsl=wsl: h.dma_start(
                        out=ds["od_sp"].rearrange("(a p) l -> p a l", p=128)[:, :, wsl],
                        in_=od_win[:, :, :]), deps=["V"], inc=16)
                    odspill_y[0] = p.count["Y"]
                    E("Y", lambda h, wsl=wsl: h.dma_start(
                        out=ds["v_sp"].rearrange("(a p) l -> p a l", p=128)[:, :, wsl],
                        in_=vfir[:, :, 64:1088]), deps=["S", "V"], inc=16)
                    for nm in ("fs", "fm", "fl"):
                        E("Y", lambda h, nm=nm, wsl=wsl: h.dma_start(
                            out=ds[nm + "_sp"].rearrange("(a p) l -> p a l", p=128)[:, :, wsl],
                            in_=fsb[nm][:, :, :]), deps=["V", "S"], inc=16)
                    fsb_spill_y[0] = p.count["Y"]
                    spill_y = p.count["Y"]
                    for ct in range(4):
                        E("V", lambda h, ct=ct: h.tensor_copy(vfir[:, ct, 0:64], vfir[:, ct, 1024:1088]),
                          deps=[("Y", spill_y), ("T", p.count["T"])])
            E("G", lambda h: h.collective_compute(
                "AllGather", ALU.bypass, replica_groups=PAIRS,
                ins=[ds["stats_in"]], outs=[ds["stats_out"]]),
              deps=[("Y", p.count["Y"])], inc=0, cinc=1)
            cc_stats = p.ccount
            p.barrier()

        with contextlib.ExitStack() as es2:
            def sb2(n, s, d=BF16):
                return es2.enter_context(nc.sbuf_tensor(n, s, d))
            up_b = sb2("up_b", [128, 2, 8, 2048])
            dn_b = sb2("dn_b", [128, 2, 8, 1024])
            xn_ld = sb2("xn_ld", [128, 2, 8, 512])
            rlld = sb2("rlld", [128, 2, 4, 8], F32)
            selrow8b = sb2("selrow8b", [8, 4, 128])
            hid = sb2("hid", [128, 8, 512])
            htmp = sb2("htmp", [128, 512])
            gateT = sb2("gateT", [8, L])
            gtb16 = sb2("gtb16", [128, 8])
            rb8 = sb2("rb8", [128, 6, 8], F32)
            cols8 = sb2("cols8", [128, 8], F32)
            mstage = sb2("mstage", [128, 2, 8, 512], F32)
            tmpf_moe = sb2("tmpf_moe", [128, 512])
            gbc = sb2("gbc", [128, 512])
            E("G", lambda h: h.dma_start(out=selrow8b[:], in_=hb["selrow8e"].rearrange("e k m -> k e m")), inc=16)
            E("G", lambda h: h.dma_start(out=up_b[:, 0, :, :], in_=hb["up"][0].rearrange("(a p) c -> p a c", p=128)), inc=16)
            E("G", lambda h: h.dma_start(out=dn_b[:, 0, :, :], in_=hb["down"][0].rearrange("(a p) c -> p a c", p=128)), inc=16)
            moe_accum_g = {}
            xnr = ds["xn_sp"].rearrange("(a p) l -> p a l", p=128)
            for e in range(4):
                if e < 3:
                    E("G", lambda h, e=e: h.dma_start(
                        out=up_b[:, (e + 1) % 2, :, :],
                        in_=hb["up"][e + 1].rearrange("(a p) c -> p a c", p=128)),
                      deps=[("T", p.count["T"])], inc=16)
                    E("G", lambda h, e=e: h.dma_start(
                        out=dn_b[:, (e + 1) % 2, :, :],
                        in_=hb["down"][e + 1].rearrange("(a p) c -> p a c", p=128)), inc=16)
                eb = e % 2
                w_g = p.count["G"]
                for t in range(NT):
                    d = t % 2
                    sl = slice(TILE * t, TILE * (t + 1))
                    it = e * NT + t
                    if it == 0:
                        xn_y = {}
                        xn_war = {0: 0, 1: 0}

                        def load_xn(it2):
                            tt_ = it2 % NT
                            d_ = tt_ % 2
                            sl_ = slice(TILE * tt_, TILE * (tt_ + 1))
                            E("Y", lambda h, d_=d_, sl_=sl_: h.dma_start(
                                out=xn_ld[:, d_, :, :], in_=xnr[:, :, sl_]),
                              deps=[("T", xn_war[d_])], inc=16)
                            xn_y[d_] = p.count["Y"]
                        load_xn(0)
                    if it + 1 < 4 * NT:
                        load_xn(it + 1)
                    y_xn = xn_y[d]
                    if e == 0:
                        E("Y", lambda h, d=d, t=t: h.dma_start(
                            out=rlld[:, d, :, :],
                            in_=ds["rl_sp"].rearrange("(t b p) c -> p t b c", p=128, b=4)[:, t, :, :]),
                          deps=[("V", p.count["V"])], inc=16)
                        y_rl = p.count["Y"]
                        for blk in range(4):
                            ps8 = rlld[:, d, blk, :]
                            E("V", lambda h, ps8=ps8: h.reduce_max(cols8[:, 0:1], ps8, axis=AX),
                              deps=["T", ("Y", y_rl)])
                            E("V", lambda h, ps8=ps8: h.tensor_scalar(rb8[:, 0, :], ps8, cols8[:, 0:1],
                                                                      None, ALU.subtract))
                            E("S", lambda h: h.activation(rb8[:, 0, :], rb8[:, 0, :], AF.Exp), deps=["V"])
                            E("V", lambda h: h.reduce_sum(cols8[:, 1:2], rb8[:, 0, :], axis=AX), deps=["S"])
                            E("V", lambda h: h.reciprocal(cols8[:, 1:2], cols8[:, 1:2]))
                            E("V", lambda h: h.tensor_scalar(rb8[:, 1, :], rb8[:, 0, :], cols8[:, 1:2],
                                                             None, ALU.mult))
                            E("V", lambda h: h.reduce_max(cols8[:, 2:3], rb8[:, 1, :], axis=AX))
                            E("V", lambda h: h.tensor_scalar(rb8[:, 2, :], rb8[:, 1, :], cols8[:, 2:3],
                                                             None, ALU.is_ge))
                            E("V", lambda h: h.tensor_mul(rb8[:, 3, :], rb8[:, 1, :], rb8[:, 2, :]))
                            E("V", lambda h: h.tensor_sub(rb8[:, 3, :], rb8[:, 1, :], rb8[:, 3, :]))
                            E("V", lambda h: h.reduce_max(cols8[:, 3:4], rb8[:, 3, :], axis=AX))
                            E("V", lambda h: h.tensor_scalar(rb8[:, 4, :], rb8[:, 3, :], cols8[:, 3:4],
                                                             None, ALU.is_ge))
                            E("V", lambda h: h.tensor_add(cols8[:, 4:5], cols8[:, 2:3], cols8[:, 3:4]))
                            E("V", lambda h: h.reciprocal(cols8[:, 4:5], cols8[:, 4:5]))
                            E("V", lambda h: h.tensor_mul(cols8[:, 5:6], cols8[:, 2:3], cols8[:, 4:5]))
                            E("V", lambda h: h.tensor_mul(cols8[:, 6:7], cols8[:, 3:4], cols8[:, 4:5]))
                            E("V", lambda h: h.tensor_scalar(rb8[:, 5, :], rb8[:, 4, :], cols8[:, 6:7],
                                                             None, ALU.mult))
                            E("V", lambda h: h.scalar_tensor_tensor(gtb16[:, :], rb8[:, 2, :], cols8[:, 5:6],
                                                                    rb8[:, 5, :], ALU.mult, ALU.add), deps=["T"])
                            E("T", lambda h: h.transpose(ptr[0:8, 0:128], gtb16[:, :], idb[:, :]), deps=["V"])
                            E("V", lambda h, t=t, blk=blk: h.tensor_copy(
                                gateT[0:8, TILE * t + 128 * blk:TILE * t + 128 * blk + 128], ptr[0:8, 0:128]),
                              deps=["T"])
                    E("T", lambda h, e=e, t=t: h.matmul(
                        pw3[:, :], selrow8b[:, e, :], gateT[0:8, TILE * t:TILE * t + 512],
                        start=True, stop=True), deps=["V"])
                    E("V", lambda h: h.tensor_copy(gbc[:, :], pw3[:, :]), deps=["T"])
                    gbc_v = p.count["V"]
                    pair_vdone = {}
                    pair_sdone = {}
                    for ci in range(8):
                        bA, bB = (pw1, pw2) if ci % 2 == 0 else (pw3, psml)
                        dep0 = [("V", pair_vdone.get(ci - 2, gbc_v)),
                                ("S", pair_sdone.get(ci - 2, 0)), ("G", w_g), ("Y", y_xn)]
                        for dt in range(8):
                            E("T", lambda h, eb=eb, d=d, ci=ci, dt=dt, bA=bA: h.matmul(
                                bA[:, :], up_b[:, eb, dt, 128 * ci:128 * ci + 128], xn_ld[:, d, dt, :],
                                start=(dt == 0), stop=(dt == 7)),
                              deps=(dep0 if dt == 0 else []))
                        for dt in range(8):
                            E("T", lambda h, eb=eb, d=d, ci=ci, dt=dt, bB=bB: h.matmul(
                                bB[:, :], up_b[:, eb, dt, 1024 + 128 * ci:1024 + 128 * ci + 128],
                                xn_ld[:, d, dt, :], start=(dt == 0), stop=(dt == 7)))
                        tdone = p.count["T"]
                        E("S", lambda h, bA=bA: h.activation(htmp[:, :], bA[:, :], AF.Silu),
                          deps=[("T", tdone)])
                        pair_sdone[ci] = p.count["S"]
                        E("V", lambda h, bB=bB: h.tensor_mul(tmpf_moe[:, :], htmp[:, :], bB[:, :]),
                          deps=["S"])
                        E("V", lambda h, ci=ci: h.tensor_mul(hid[:, ci, :], tmpf_moe[:, :], gbc[:, :]))
                        pair_vdone[ci] = p.count["V"]
                    oc_vdone = {}
                    for oc in range(8):
                        bC = pw1 if oc % 2 == 0 else pw2
                        dep0 = [("V", oc_vdone.get(oc - 2, p.count["V"]))]
                        for ht in range(8):
                            E("T", lambda h, eb=eb, oc=oc, ht=ht, bC=bC: h.matmul(
                                bC[:, :], dn_b[:, eb, ht, 128 * oc:128 * oc + 128], hid[:, ht, :],
                                start=(ht == 0), stop=(ht == 7)),
                              deps=(dep0 if ht == 0 else []))
                        tdone = p.count["T"]
                        E("V", lambda h, d=d, oc=oc, bC=bC: h.tensor_copy(mstage[:, d, oc, :], bC[:, :]),
                          deps=[("T", tdone), ("G", moe_accum_g.get((e, t - 2), 0))])
                        oc_vdone[oc] = p.count["V"]
                    mv = p.count["V"]
                    E("G", lambda h, e=e, t=t, d=d: h.dma_start(
                        out=ds["moe_sp"][t // 4].rearrange("(a p) l -> p a l", p=128)[
                            :, :, 512 * (t % 4):512 * (t % 4) + 512],
                        in_=mstage[:, d, :, :],
                        accum_op=(ALU.bypass if e == 0 else ALU.add)),
                      deps=[("V", mv), ("G", moe_accum_g.get((e - 1, t), 0))], inc=16)
                    moe_accum_g[(e, t)] = p.count["G"]
                    xn_war[d] = p.count["T"]
                    if e == 3 and t % 4 == 3:
                        lq = t // 4
                        E("G", lambda h, lq=lq: h.collective_compute(
                            "ReduceScatter", ALU.add, replica_groups=PAIRS,
                            ins=[ds["moe_sp"][lq]], outs=[ds["moe_rs"][lq]]),
                          inc=0, cinc=1)
            if DEBUG:
                dbgg = nc.dram_tensor("dbg_gate", [8, L], F32, kind="ExternalOutput").ap()
                E("G", lambda h, dbgg=dbgg: h.dma_start(out=dbgg[:], in_=gateT[:, :]),
                  deps=["V", "T", ("G", p.count["G"])], inc=16)
                for lq in range(4):
                    dbgm = nc.dram_tensor("dbg_msp%d" % lq, [1024, 2048], F32, kind="ExternalOutput").ap()
                    E("Y", lambda h, dbgm=dbgm, lq=lq: h.dma_start(out=dbgm[:], in_=ds["moe_sp"][lq]),
                      deps=[("G", p.count["G"])], inc=16)
            cc_moe = p.ccount
            p.barrier()

        with contextlib.ExitStack() as es3:
            def sb3(n, s, d=BF16):
                return es3.enter_context(nc.sbuf_tensor(n, s, d))
            gW1ab = sb3("gW1ab", [128, 8, 1024])
            gb1b = sb3("gb1b", [128, 8], F32)
            gW2b = sb3("gW2b", [128, 8, 20])
            statsAG = sb3("statsAG", [80, L])
            xn_ld2 = sb3("xn_ld2", [128, 2, 8, 512])
            G1 = sb3("G1", [128, 8, 512])
            lgstg = sb3("lgstg", [128, 4, 20], F32)
            E("G", lambda h: h.dma_start(out=gW1ab[:], in_=hb["gW1a"].rearrange("(a p) c -> p a c", p=128)), inc=16)
            E("Y", lambda h: h.dma_start(out=gb1b[:], in_=hb["gb1c"][:]), inc=16)
            E("G", lambda h: h.dma_start(out=gW2b[:], in_=hb["gW2"].rearrange("(a p) c -> p a c", p=128)), inc=16)
            E("G", lambda h: h.dma_start(out=statsAG[:], in_=ds["stats_out"][:]),
              deps=[("C", cc_stats)], inc=16)
            w2a_g = p.count["G"]
            w2a_y = p.count["Y"]
            for t in range(NT):
                d = t % 2
                sl = slice(TILE * t, TILE * (t + 1))
                E("Y", lambda h, d=d, sl=sl: h.dma_start(out=xn_ld2[:, d, :, :], in_=xnr[:, :, sl]),
                  deps=[("T", p.count["T"])], inc=16)
                y_xn = p.count["Y"]
                for ct in range(8):
                    for dt in range(8):
                        E("T", lambda h, d=d, ct=ct, dt=dt: h.matmul(
                            pw1[:, :], gW1ab[:, dt, 128 * ct:128 * ct + 128], xn_ld2[:, d, dt, :],
                            start=(dt == 0), stop=False, skip_group_check=True),
                          deps=(["S", ("G", w2a_g), ("Y", y_xn)] if dt == 0 else []))
                    E("T", lambda h, ct=ct, sl=sl: h.matmul(
                        pw1[:, :], Rsb[:, 128 * ct:128 * ct + 128], statsAG[:, sl],
                        start=False, stop=True, skip_group_check=True),
                      deps=[("G", Rsb_g)])
                    E("S", lambda h, ct=ct: h.activation(G1[:, ct, :], pw1[:, :], AF.Gelu,
                                                         bias=gb1b[:, ct:ct + 1]), deps=["T"])
                for blk in range(4):
                    for ct in range(8):
                        E("T", lambda h, ct=ct, blk=blk: h.matmul(
                            psml[:, 20 * blk:20 * blk + 20], G1[:, ct, 128 * blk:128 * blk + 128],
                            gW2b[:, ct, :], start=(ct == 0), stop=(ct == 7), skip_group_check=True),
                          deps=(["S", "V"] if ct == 0 else []))
                    E("V", lambda h, blk=blk: h.tensor_copy(lgstg[:, blk, :], psml[:, 20 * blk:20 * blk + 20]),
                      deps=["T", ("Y", p.count["Y"])])
                E("Y", lambda h, t=t: h.dma_start(
                    out=ds["lg_in"].rearrange("(t b p) c -> p t b c", p=128, b=4)[:, t, :, :],
                    in_=lgstg[:, :, :]), deps=["V"], inc=16)
            E("G", lambda h: h.collective_compute(
                "AllReduce", ALU.add, replica_groups=PAIRS,
                ins=[ds["lg_in"]], outs=[ds["lg_out"]]),
              deps=[("Y", p.count["Y"])], inc=0, cinc=1)
            cc_lg = p.ccount
            p.barrier()

        with contextlib.ExitStack() as es4:
            def sb4(n, s, d=BF16):
                return es4.enter_context(nc.sbuf_tensor(n, s, d))
            selmixb = sb4("selmixb", [20, 10, 128])
            lgld = sb4("lgld", [128, 2, 4, 20], F32)
            prw = sb4("prw", [128, 4, 20], F32)
            prmax = sb4("prmax", [128, 4, 4], F32)
            prb16 = sb4("prb16", [128, 20])
            probsT = sb4("probsT", [20, 512])
            pth = {}
            for nm in ("v", "od", "fs", "fm", "fl"):
                pth[nm] = sb4("p_" + nm, [128, 2, 4, 512])
            oacc = sb4("oacc", [128, 2, 512], F32)
            otmp = sb4("otmp", [128, 512], F32)
            sqo = sb4("sqo", [128, 2, 512])
            orow = sb4("orow", [1, 2, 512], F32)
            orb16 = sb4("orb16", [1, 512])
            obuf = sb4("obuf", [128, 4, 512])
            nwcb2 = sb4("nwcb2", [128, 2], F32)
            E("G", lambda h: h.dma_start(out=selmixb[:], in_=hb["selmix"].rearrange("s k m -> k s m")), inc=16)
            E("Y", lambda h: h.dma_start(out=nwcb2[:], in_=hb["nwc"][:]), inc=16)
            sel_g = p.count["G"]
            nw_y = p.count["Y"]
            for t in range(NT):
                d = t % 2
                sl = slice(TILE * t, TILE * (t + 1))
                E("Y", lambda h, d=d, t=t: h.dma_start(
                    out=lgld[:, d, :, :],
                    in_=ds["lg_out"].rearrange("(t b p) c -> p t b c", p=128, b=4)[:, t, :, :]),
                  deps=[("C", cc_lg), ("V", p.count["V"]), ("S", p.count["S"])], inc=16)
                for nm in ("v", "od", "fs", "fm", "fl"):
                    E("Y", lambda h, nm=nm, d=d, sl=sl: h.dma_start(
                        out=pth[nm][:, d, :, :],
                        in_=ds[(nm if nm in ("fs", "fm", "fl") else nm) + "_sp"].rearrange(
                            "(a p) l -> p a l", p=128)[:, :, sl]),
                      deps=[("T", p.count["T"]), ("V", p.count["V"])], inc=16)
                y_ld = p.count["Y"]
                for blk in range(4):
                    E("V", lambda h, d=d, blk=blk: h.tensor_add(prw[:, blk, :], lgld[:, d, blk, :], gb2bc[:, :]),
                      deps=[("Y", y_ld)])
                    E("V", lambda h, blk=blk: h.tensor_mul(prw[:, blk, :], prw[:, blk, :], itaubc[:, :]))
                    E("V", lambda h, blk=blk: h.reduce_max(
                        prmax[:, blk, :].rearrange("p (a b) -> p a b", b=1),
                        prw[:, blk, :].rearrange("p (a b) -> p a b", b=5), axis=AX))
                for blk in range(4):
                    for hh4 in range(4):
                        E("V", lambda h, blk=blk, hh4=hh4: h.tensor_scalar(
                            prw[:, blk, 5 * hh4:5 * hh4 + 5], prw[:, blk, 5 * hh4:5 * hh4 + 5],
                            prmax[:, blk, hh4:hh4 + 1], None, ALU.subtract))
                E("S", lambda h: h.activation(prw[:, :, :], prw[:, :, :], AF.Exp), deps=["V"])
                for blk in range(4):
                    E("V", lambda h, blk=blk: h.reduce_sum(
                        prmax[:, blk, :].rearrange("p (a b) -> p a b", b=1),
                        prw[:, blk, :].rearrange("p (a b) -> p a b", b=5), axis=AX), deps=["S"])
                    E("V", lambda h, blk=blk: h.reciprocal(prmax[:, blk, :], prmax[:, blk, :]))
                    for hh4 in range(4):
                        E("V", lambda h, blk=blk, hh4=hh4: h.tensor_scalar(
                            prw[:, blk, 5 * hh4:5 * hh4 + 5], prw[:, blk, 5 * hh4:5 * hh4 + 5],
                            prmax[:, blk, hh4:hh4 + 1], None, ALU.mult))
                    E("V", lambda h, blk=blk: h.tensor_mul(prw[:, blk, :], prw[:, blk, :], facbc[:, :]))
                    E("V", lambda h, blk=blk: h.tensor_add(prw[:, blk, :], prw[:, blk, :], epsbc[:, :]))
                    E("V", lambda h, blk=blk: h.tensor_copy(prb16[:, :], prw[:, blk, :]), deps=["T"])
                    E("T", lambda h: h.transpose(ptr[0:20, 0:128], prb16[:, :], idb[:, :]), deps=["V"])
                    E("V", lambda h, blk=blk: h.tensor_copy(probsT[:, 128 * blk:128 * blk + 128], ptr[0:20, 0:128]),
                      deps=["T"])
                for hh in range(2):
                    for pi, nm in enumerate(("fs", "fm", "fl", "od", "v")):
                        E("T", lambda h, hh=hh, pi=pi: h.matmul(
                            pw2[:, :], selmixb[:, 5 * hh + pi, :],
                            probsT[:, :], start=True, stop=True), deps=["V"])
                        for ci in range(2):
                            ct = 2 * hh + ci
                            if pi == 0:
                                E("V", lambda h, nm=nm, d=d, ct=ct, ci=ci: h.tensor_mul(
                                    oacc[:, ci, :], pth[nm][:, d, ct, :], pw2[:, :]), deps=["T"])
                            else:
                                E("V", lambda h, nm=nm, d=d, ct=ct: h.tensor_mul(
                                    otmp[:, :], pth[nm][:, d, ct, :], pw2[:, :]), deps=["T"])
                                E("V", lambda h, ci=ci: h.tensor_add(oacc[:, ci, :], oacc[:, ci, :], otmp[:, :]))
                    for ci in range(2):
                        E("S", lambda h, ci=ci: h.activation(sqo[:, ci, :], oacc[:, ci, :], AF.Square),
                          deps=["V"])
                        E("T", lambda h, ci=ci: h.matmul(psml[0:1, 0:512], onescb[:], sqo[:, ci, :],
                                                         start=(ci == 0), stop=(ci == 1),
                                                         skip_group_check=True), deps=["S"])
                    E("V", lambda h: h.tensor_scalar(orow[0:1, 0, :], psml[0:1, 0:512], 1.0 / 256, 1e-5,
                                                     ALU.mult, ALU.add), deps=["T"])
                    E("S", lambda h: h.activation(orow[0:1, 0, :], orow[0:1, 0, :], AF.Sqrt), deps=["V"])
                    E("V", lambda h: h.reciprocal(orow[0:1, 0, :], orow[0:1, 0, :]), deps=["S"])
                    E("V", lambda h: h.tensor_copy(orb16[0:1, :], orow[0:1, 0, :]), deps=["T"])
                    E("T", lambda h: h.matmul(pw2[:, :], onesrb[:], orb16[0:1, :], start=True, stop=True),
                      deps=["V"])
                    for ci in range(2):
                        ct = 2 * hh + ci
                        E("V", lambda h, ci=ci: h.tensor_mul(otmp[:, :], oacc[:, ci, :], pw2[:, :]),
                          deps=["T"])
                        E("V", lambda h, ct=ct, ci=ci: h.tensor_scalar(
                            obuf[:, ct, :], otmp[:, :], nwcb2[:, ci:ci + 1], None, ALU.mult),
                          deps=[("Y", p.count["Y"])])
                half = t // 8
                if DEBUG:
                    E("Y", lambda h, t=t: h.dma_start(
                        out=ds["probs_dbg"].rearrange("(t b p) c -> p t b c", p=128, b=4)[:, t, :, :],
                        in_=prw[:, :, :]), deps=["V"], inc=16)
                E("Y", lambda h, t=t, half=half: h.dma_start(
                    out=ds["o_in%d" % half].rearrange("(a p) l -> p a l", p=128)[
                        :, :, TILE * (t % 8):TILE * (t % 8) + 512],
                    in_=obuf[:, :, :]), deps=["V"], inc=16)
                if t == 7:
                    E("G", lambda h: h.collective_compute(
                        "AllGather", ALU.bypass, replica_groups=PAIRS,
                        ins=[ds["o_in0"]], outs=[ds["o_out0"]]),
                      deps=[("Y", p.count["Y"])], inc=0, cinc=1)
            E("G", lambda h: h.collective_compute(
                "AllGather", ALU.bypass, replica_groups=PAIRS,
                ins=[ds["o_in1"]], outs=[ds["o_out1"]]),
              deps=[("Y", p.count["Y"])], inc=0, cinc=1)
            cc_o = p.ccount
            p.barrier()

        with contextlib.ExitStack() as es5:
            def sb5(n, s, d=BF16):
                return es5.enter_context(nc.sbuf_tensor(n, s, d))
            Wob = sb5("Wob", [128, 8, 512])
            o_ld = sb5("o_ld", [128, 2, 8, 512])
            grs = sb5("grs", [128, 2, 4, 512], F32)
            gt = sb5("gt", [128, 4, 512])
            xrt = sb5("xrt", [128, 2, 4, 512], F32)
            sgt = sb5("sgt", [128, 512], F32)
            outsb = sb5("outsb", [128, 4, 512], F32)
            E("G", lambda h: h.dma_start(out=Wob[:], in_=hb["Wo"].rearrange("(a p) c -> p a c", p=128)), inc=16)
            wo_g = p.count["G"]
            xrr = hb["xres"].rearrange("(a p) l -> p a l", p=128)
            outr = out_h.rearrange("(a p) l -> p a l", p=128)
            for t in range(NT):
                d = t % 2
                half = t // 8
                oo = ds["o_out%d" % half].rearrange("(a p) l -> p a l", p=128)
                E("Y", lambda h, d=d, t=t, oo=oo, half=half: h.dma_start(
                    out=o_ld[:, d, :, :], in_=oo[:, :, TILE * (t % 8):TILE * (t % 8) + 512]),
                  deps=[("C", cc_o - (1 if half == 0 else 0)), ("T", p.count["T"])], inc=16)
                E("Y", lambda h, d=d, t=t: h.dma_start(
                    out=grs[:, d, :, :],
                    in_=ds["moe_rs"][t // 4].rearrange("(a p) l -> p a l", p=128)[
                        :, :, 512 * (t % 4):512 * (t % 4) + 512]),
                  deps=[("C", cc_moe), ("S", p.count["S"])], inc=16)
                E("Y", lambda h, d=d, sl=slice(TILE * t, TILE * t + 512): h.dma_start(
                    out=xrt[:, d, :, :], in_=xrr[:, :, sl]),
                  deps=[("V", p.count["V"])], inc=16)
                y_ld3 = p.count["Y"]
                for a4 in range(4):
                    E("S", lambda h, d=d, a4=a4: h.activation(gt[:, a4, :], grs[:, d, a4, :], AF.Silu),
                      deps=[("Y", y_ld3), ("V", p.count["V"])])
                for oc in range(4):
                    for kt in range(8):
                        E("T", lambda h, d=d, oc=oc, kt=kt: h.matmul(
                            pw1[:, :], Wob[:, kt, 128 * oc:128 * oc + 128], o_ld[:, d, kt, :],
                            start=(kt == 0), stop=(kt == 7)),
                          deps=(["V", ("G", wo_g), ("Y", y_ld3)] if kt == 0 else []))
                    E("V", lambda h, oc=oc: h.tensor_mul(sgt[:, :], gt[:, oc, :], pw1[:, :]),
                      deps=["T", "S"])
                    E("V", lambda h, d=d, oc=oc: h.tensor_add(outsb[:, oc, :], xrt[:, d, oc, :], sgt[:, :]))
                E("Y", lambda h, t=t: h.dma_start(
                    out=outr[:, :, TILE * t:TILE * t + 512], in_=outsb[:, :, :]),
                  deps=["V"], inc=16)
            if DEBUG:
                for buf, nm2 in ((gb2bc, "gb2bc"), (itaubc, "itaubc"), (facbc, "facbc"), (epsbc, "epsbc")):
                    dbg2 = nc.dram_tensor("dbg_" + nm2, [128, 20], F32, kind="ExternalOutput").ap()
                    E("Y", lambda h, dbg2=dbg2, buf=buf: h.dma_start(out=dbg2[:], in_=buf[:]),
                      deps=[("Y", p.count["Y"])], inc=16)
                dbgr = nc.dram_tensor("dbg_row20", [1, 4, 20], F32, kind="ExternalOutput").ap()
                E("Y", lambda h, dbgr=dbgr: h.dma_start(out=dbgr[:], in_=row20[:, :, :]), inc=16)
                dbgs = nc.dram_tensor("dbg_rowsml", [1, 32], F32, kind="ExternalOutput").ap()
                E("Y", lambda h, dbgs=dbgs: h.dma_start(out=dbgs[:], in_=rowsml[:, :]), inc=16)
                for nm, shp in (("xn_sp", [1024, L]), ("v_sp", [512, L]), ("od_sp", [512, L]),
                                ("fs_sp", [512, L]), ("fm_sp", [512, L]), ("fl_sp", [512, L]),
                                ("stats_out", [80, L]), ("o_out0", [1024, L // 2]),
                                ("o_out1", [1024, L // 2])):
                    dbg = nc.dram_tensor("dbg_" + nm, shp, F32, kind="ExternalOutput").ap()
                    E("G", lambda h, dbg=dbg, nm=nm: h.dma_start(out=dbg[:], in_=ds[nm][:]),
                      deps=[("Y", p.count["Y"]), ("C", p.ccount)], inc=16)
                dbg = nc.dram_tensor("dbg_lg", [L, 20], F32, kind="ExternalOutput").ap()
                E("G", lambda h, dbg=dbg: h.dma_start(out=dbg[:], in_=ds["lg_out"][:]), inc=16)
                dbg = nc.dram_tensor("dbg_probs", [L, 20], F32, kind="ExternalOutput").ap()
                E("G", lambda h, dbg=dbg: h.dma_start(out=dbg[:], in_=ds["probs_dbg"][:]), inc=16)
                dbg = nc.dram_tensor("dbg_rl", [L, 8], F32, kind="ExternalOutput").ap()
                E("G", lambda h, dbg=dbg: h.dma_start(out=dbg[:], in_=ds["rl_sp"][:]), inc=16)
                for lq in range(4):
                    dbg = nc.dram_tensor("dbg_moe%d" % lq, [512, 2048], F32, kind="ExternalOutput").ap()
                    E("G", lambda h, dbg=dbg, lq=lq: h.dma_start(out=dbg[:], in_=ds["moe_rs"][lq]), inc=16)
            E("G", lambda h: h.nop(), deps=[("Y", p.count["Y"]), ("G", p.count["G"])])

        p.finalize(block)
    return nc



def kernel(**inputs):
    maps = host_prep(inputs)
    nc = build_nc()
    res = run_bass_kernel_spmd(nc, maps, core_ids=list(range(8)))
    out = np.zeros((B, L, D), np.float32)
    for c in range(8):
        b, j = c // 2, c % 2
        out[b, :, 512 * j:512 * (j + 1)] = res.results[c]["out"].T
    return out
